# version 04
# baseline (speedup 1.0000x reference)
import os as _os
import numpy as np
import ml_dtypes

_DBG_NL = int(_os.environ.get("KDBG_NL", "2"))
_DBG_FFN = int(_os.environ.get("KDBG_FFN", "1"))

import concourse.bass as bass
import concourse.mybir as mybir
from concourse.bass_utils import run_bass_kernel_spmd

F32 = mybir.dt.float32
BF16 = mybir.dt.bfloat16
Act = mybir.ActivationFunctionType
Alu = mybir.AluOpType
AX = mybir.AxisListType

D = 1024
DIN = 2048
DST = 64
NH = 32
HD = 64
CONVDIM = DIN + 2 * DST
DPROJ = 2 * DIN + 2 * DST + NH
HID = 4096
BK = 32
EPS = 1e-5
LN2 = float(np.log(2.0))
N_CORES = 8
B_FULL, L_FULL = 16, 2048
CH = 128

KD = D // 128
NDC = (DPROJ + 127) // 128
KDIN = DIN // 128
NOC = D // 128
NXT = (CONVDIM + 127) // 128
NHT = DIN // 128



class Step:
    __slots__ = ("fn", "waits", "inc")

    def __init__(self, fn, waits, inc):
        self.fn, self.waits, self.inc = fn, waits, inc


class Buf:
    __slots__ = ("writer", "readers", "name")

    def __init__(self, name=""):
        self.writer = None
        self.readers = []
        self.name = name


class Prog:
    ENGS = ("pe", "dve", "act", "pool", "sp")

    def __init__(self, nc):
        self.nc = nc
        self.steps = {e: [] for e in self.ENGS}
        self.count = {e: 0 for e in self.ENGS}
        self.sems = {}
        self._guards = []
        self.bufsem = {}
        self.nbufsem = 0
        self.waited = {e: {} for e in self.ENGS}

    def sbuf(self, name, shape, dtype):
        g = self.nc.sbuf_tensor(name, list(shape), dtype)
        t = g.__enter__()
        self._guards.append(g)
        return t

    def psum(self, name, shape, dtype=F32):
        g = self.nc.psum_tensor(name, list(shape), dtype)
        t = g.__enter__()
        self._guards.append(g)
        return t

    def emit(self, eng, fn, reads=(), writes=(), dma=0):
        waits = {}
        wd = self.waited[eng]

        def add(dep):
            if dep is None:
                return
            de, dv = dep
            if wd.get(de, 0) >= dv:
                return
            if waits.get(de, 0) < dv:
                waits[de] = dv

        for b in reads:
            add(b.writer)
        for b in writes:
            add(b.writer)
            for r in b.readers:
                add(r)
        for de, dv in waits.items():
            wd[de] = dv
        if dma:
            assert len(writes) == 1, "dma step must write exactly one Buf"
            bkey = id(writes[0])
            sname = self.bufsem.get(bkey)
            if sname is None:
                sname = "D%d" % self.nbufsem
                self.nbufsem += 1
                self.bufsem[bkey] = sname
                self.count[sname] = 0
                self.steps.setdefault(sname, None)
            inc = 16 * dma
            self.count[sname] += inc
            tag = (sname, self.count[sname])
            for b in reads:
                b.readers.append(tag)
            writes[0].writer = tag
            writes[0].readers = []
            self.steps[eng].append(Step(fn, waits, (sname, inc)))
            return tag
        self.count[eng] += 1
        tag = (eng, self.count[eng])
        for b in reads:
            b.readers.append(tag)
        for b in writes:
            b.writer = tag
            b.readers = []
        self.steps[eng].append(Step(fn, waits, 1))
        return tag

    def barrier(self):
        snap = dict(self.count)
        for e in self.ENGS:
            waits = {d: v for d, v in snap.items() if v > 0}
            self.count[e] += 1
            self.steps[e].append(Step(lambda en: en.nop(), waits, 1))
            self.waited[e] = dict(waits)

    def build(self):
        nc = self.nc
        for e in list(self.count):
            g = nc.semaphore("S" + e)
            self.sems[e] = g.__enter__()
            self._guards.append(g)
        blk_g = nc.Block()
        blk = blk_g.__enter__()

        def runner(name):
            def run(e):
                cnt = 0
                for st in self.steps[name]:
                    if not st.waits:
                        e.wait_ge(self.sems[name], cnt)
                    for de, dv in sorted(st.waits.items()):
                        e.wait_ge(self.sems[de], dv)
                    if not isinstance(st.inc, tuple):
                        cnt += st.inc
                    last = st.fn(e)
                    if isinstance(st.inc, tuple):
                        sname, inc = st.inc
                        insts = last if isinstance(last, (list, tuple)) else [last]
                        assert len(insts) * 16 == inc
                        for i in insts:
                            i.then_inc(self.sems[sname], 16)
                    else:
                        if isinstance(last, (list, tuple)):
                            last = last[-1]
                        last.then_inc(self.sems[name], 1)
            return run

        blk.tensor(runner("pe"))
        blk.vector(runner("dve"))
        blk.scalar(runner("act"))
        blk.gpsimd(runner("pool"))
        blk.sync(runner("sp"))
        blk_g.__exit__(None, None, None)



def quant_ladder(p, eng, buf_io, src_ap, dst_ap, inv_ap, sc_ap, t_ap, b_ap,
                 nlv=7, span=32.0):
    reads, writes = buf_io
    lv = [2 ** (nlv - 1 - i) for i in range(nlv)]

    def f_mul(e):
        return e.tensor_tensor(out=t_ap, in0=src_ap, in1=inv_ap, op=Alu.mult)
    p.emit(eng, f_mul, reads=reads, writes=[writes[0]])

    def f_add(e):
        return e.tensor_scalar(out=t_ap, in0=t_ap, scalar1=span + 0.5,
                               scalar2=None, op0=Alu.add)
    p.emit(eng, f_add, reads=[writes[0]], writes=[writes[0]])
    for c in lv:
        def f_ge(e, c=c):
            return e.tensor_scalar(out=b_ap, in0=t_ap, scalar1=float(c),
                                   scalar2=None, op0=Alu.is_ge)
        p.emit(eng, f_ge, reads=[writes[0]], writes=[writes[1]])

        def f_up(e, c=c):
            return e.scalar_tensor_tensor(out=t_ap, in0=b_ap, scalar=float(-c),
                                          in1=t_ap, op0=Alu.mult, op1=Alu.add)
        p.emit(eng, f_up, reads=[writes[1], writes[0]], writes=[writes[0]])

    def f_u2(e):
        return e.tensor_tensor(out=b_ap, in0=src_ap, in1=inv_ap, op=Alu.mult)
    p.emit(eng, f_u2, reads=reads + [writes[0]], writes=[writes[1]])

    def f_d(e):
        return e.scalar_tensor_tensor(out=t_ap, in0=b_ap, scalar=0.5,
                                      in1=t_ap, op0=Alu.add, op1=Alu.subtract)
    p.emit(eng, f_d, reads=[writes[1], writes[0]], writes=[writes[0]])

    def f_q(e):
        return e.tensor_tensor(out=dst_ap, in0=t_ap, in1=sc_ap, op=Alu.mult)
    p.emit(eng, f_q, reads=[writes[0]], writes=[writes[2]])


def exp_ladder(p, eng, rw, lg_ap, e_out_ap, t_ap, b_ap, nlv=9, span=256.0):
    reads, wT, wB, wE = rw
    def f0(e):
        return e.tensor_scalar(out=t_ap, in0=lg_ap, scalar1=span, scalar2=None,
                               op0=Alu.add)
    p.emit(eng, f0, reads=reads, writes=[wT])
    for i in range(nlv):
        c = 2 ** (nlv - 1 - i)
        def f_ge(e, c=c):
            return e.tensor_scalar(out=b_ap, in0=t_ap, scalar1=float(c),
                                   scalar2=None, op0=Alu.is_ge)
        p.emit(eng, f_ge, reads=[wT], writes=[wB])

        def f_up(e, c=c):
            return e.scalar_tensor_tensor(out=t_ap, in0=b_ap, scalar=float(-c),
                                          in1=t_ap, op0=Alu.mult, op1=Alu.add)
        p.emit(eng, f_up, reads=[wB, wT], writes=[wT])
    def f_e(e):
        return e.scalar_tensor_tensor(out=e_out_ap, in0=lg_ap, scalar=1.0,
                                      in1=t_ap, op0=Alu.mult, op1=Alu.subtract)
    p.emit(eng, f_e, reads=reads + [wT], writes=[wE])



def build_program(n_sweeps=8, sweep=512):
    NT = sweep // 128
    NCH = sweep // CH
    TOK = n_sweeps * sweep
    HSL = HID // 512
    PSL = D // 512

    nc = bass.Bass("TRN2", target_bir_lowering=False)
    p = Prog(nc)
    dt_ = nc.dram_tensor

    x_in = dt_("x", [TOK, D], F32, kind="ExternalInput").ap()
    w_in_t = dt_("w_in_t", [2, D, DPROJ], F32, kind="ExternalInput").ap()
    w_out_t = dt_("w_out_t", [2, DIN, D], F32, kind="ExternalInput").ap()
    conv_w = dt_("conv_w", [2, CONVDIM, 4], F32, kind="ExternalInput").ap()
    conv_b = dt_("conv_b", [2, CONVDIM], F32, kind="ExternalInput").ap()
    dt_bias = dt_("dt_bias", [2, NH], F32, kind="ExternalInput").ap()
    a_log = dt_("a_log", [2, NH], F32, kind="ExternalInput").ap()
    d_vec = dt_("d_vec", [2, NH], F32, kind="ExternalInput").ap()
    rms_w = dt_("rms_w", [2, DIN], F32, kind="ExternalInput").ap()
    ln_w = dt_("ln_w", [2, D], F32, kind="ExternalInput").ap()
    ln_b = dt_("ln_b", [2, D], F32, kind="ExternalInput").ap()
    w_fc = dt_("w_fc", [HID, D], F32, kind="ExternalInput").ap()
    b_fc = dt_("b_fc", [HID], F32, kind="ExternalInput").ap()
    w_proj = dt_("w_proj", [D, HID], F32, kind="ExternalInput").ap()
    b_proj = dt_("b_proj", [D], F32, kind="ExternalInput").ap()
    y_out = dt_("y", [TOK, D], F32, kind="ExternalOutput").ap()

    cin = dt_("cin", [2, NDC, KD, 2, 128, 128], BF16).ap()
    cin_z = dt_("cin_z", [2, 4, KD // 2, 128, 2048], BF16).ap()
    cout_z = dt_("cout_z", [2, 2, KDIN // 2, 128, 2048], BF16).ap()
    cout_ = dt_("cout", [2, NOC, KDIN, 2, 128, 128], BF16).ap()
    cfc = dt_("cfc", [KD, HSL, 128, 512], BF16).ap()
    cproj = dt_("cproj", [HID // 128, PSL, 128, 512], BF16).ap()
    cbv = dt_("cbv", [2, NDC * 128], F32).ap()
    dr_a = dt_("dr_a", [NH, sweep], F32).ap()
    dr_ea = dt_("dr_ea", [NH, sweep], F32).ap()
    dr_rs = dt_("dr_rs", [sweep], F32).ap()
    B_cin, B_cout, B_cfc, B_cproj, B_cbv = (Buf(), Buf(), Buf(), Buf(), Buf())
    B_dra, B_drea, B_drrs, B_y = Buf(), Buf(), Buf(), Buf()

    sb = p.sbuf
    h_t = sb("h_t", [128, NT * D], F32); B_ht = Buf()
    state = sb("state", [128, 2 * DIN], F32); B_st = [Buf(), Buf()]
    stk = sb("stk", [96, sweep], F32); B_stk = Buf()
    dts = sb("dts", [32, 3 * sweep], F32)
    B_dt0 = Buf(); B_dt1 = Buf(); B_dt2 = Buf()
    ident = sb("ident", [128, 128], F32); B_id = Buf()
    mask01 = sb("mask01", [128, CH], F32); B_mk = Buf()
    imask = sb("imask", [128, CH], mybir.dt.int32); B_imk = Buf()
    ccol = sb("ccol", [128, 16], F32); B_cc = Buf()
    drep = sb("drep", [128, 2 * NHT], F32); B_dr = Buf()
    cbcol = sb("cbcol", [128, 2 * NXT], F32); B_cb = Buf()
    bvcol = sb("bvcol", [128, 2 * NDC], F32); B_bv = Buf()
    convc = sb("convc", [128, 2 * 2 * NXT * 4], F32)
    B_cvc = [[Buf(), Buf()], [Buf(), Buf()]]
    cw = sb("cw", [128, 2 * NXT * 4], F32); B_cw = Buf()
    B_exp = Buf("exp")
    misc = sb("misc", [128, 1024], F32); B_misc = Buf()
    B_bt = Buf()
    stat = sb("stat", [128, 768], F32); B_stat = Buf()
    aeb = sb("aeb", [128, 64], F32); B_aeb = Buf()
    wTt = sb("wTt", [128, 32], F32); B_wT = Buf()
    abg = sb("abg", [128, 512], F32); B_abg = [Buf()]
    uf_hi = sb("uf_hi", [128, KD * sweep], BF16); B_ufh = Buf()
    uf_lo = sb("uf_lo", [128, KD * sweep], BF16); B_ufl = Buf()
    if KD * sweep >= HID:
        hqs = uf_lo[:, 0:HID]
    else:
        hqs = sb("hqs", [128, HID], BF16)
    wring = sb("wring", [128, 4096], BF16)
    B_wr = [Buf(), Buf()]
    wf = wring[:, 0:2048]
    B_wf = [Buf() for _ in range(4)]
    B_hqs = Buf()
    cst = sb("cst", [128, 4], F32)
    B_cst = Buf()
    drep2 = sb("drep2", [128, 2 * NH], F32)
    B_dr2 = Buf()
    mb = sb("mb", [128, 16384], BF16); B_mb_hqT = Buf()
    uh = mb[:, 0:NT * D]; B_uh = Buf()
    ul = mb[:, NT * D:2 * NT * D]; B_ul = Buf()
    MB_XHH, MB_XHL = 0, 2048
    MB_XTH, MB_XTL = 4096, 6144
    MB_YGH, MB_YGL = 8192, 10240
    MB_BPK, MB_CPA, MB_CPB = 12288, 12416, 12544
    MB_STH, MB_STL = 12672, 13184
    B_xhh = [Buf(), Buf()]; B_xhl = [Buf(), Buf()]
    B_xth = [Buf(), Buf()]; B_xtl = [Buf(), Buf()]
    B_ygh = Buf(); B_ygl = Buf()
    B_bpk = Buf(); B_cpa = Buf(); B_cpb = Buf()
    B_sth = [Buf() for _ in range(4)]; B_stl = [Buf() for _ in range(4)]
    mf = sb("mf", [128, 8192], F32)
    MF_XBS, MF_XTF, MF_BW = 0, 2176, 4224
    MF_STF, MF_ST0, MF_BCT, MF_ADT, MF_WRK = 4736, 5248, 5376, 5440, 5568
    B_xbs = [Buf(), Buf()]; B_xtf = [Buf(), Buf()]; B_ea = Buf(); B_bw = Buf()
    B_stf = [Buf() for _ in range(4)]; B_st0 = Buf(); B_bcT = Buf()
    B_adt = [Buf(), Buf()]; B_wrk = Buf()
    xbc_r = sb("xbc_r", [128, NXT * (sweep + 4)], F32); B_xbr = Buf()
    dec = sb("dec", [128, DIN], F32); B_dec = Buf()
    s_ch = sb("s_ch", [128, 512], F32); B_sch = Buf()
    yg = sb("yg", [128, NHT * CH], F32); B_yg = Buf()
    zt = sb("zt", [128, 512], F32); B_zt = Buf()
    btile = zt
    sg = sb("sg", [128, 1024], F32); B_sg = Buf()
    xr = sb("xr", [128, 1024], F32); B_xr = Buf()

    ps = [p.psum(f"ps{i}", [128, 512]) for i in range(8)]
    PB = [Buf(f"ps{i}") for i in range(8)]

    E = dict(locals())
    E["p"] = p
    E["n_sweeps"] = n_sweeps
    E["sweep"] = sweep
    E["NT"] = NT
    E["NCH"] = NCH
    E["HSL"] = HSL
    E["PSL"] = PSL
    return E


def emit_all(E):
    emit_consts(E)
    emit_transform(E)
    E["p"].barrier()
    n_sweeps = E["n_sweeps"]
    for sw in range(n_sweeps):
        b = 0 if sw < n_sweeps // 2 else 1
        emit_sweep(E, sw, b)
    p = E["p"]
    p.emit("sp", lambda e: e.nop(), reads=[E["B_y"]])
    p.build()


def emit_consts(E):
    p = E["p"]
    g = lambda k: E[k]
    imask, mask01, ident = g("imask"), g("mask01"), g("ident")
    B_imk, B_mk, B_id = g("B_imk"), g("B_mk"), g("B_id")
    ccol, B_cc = g("ccol"), g("B_cc")
    drep, B_dr = g("drep"), g("B_dr")
    cbcol, B_cb = g("cbcol"), g("B_cb")
    misc, B_misc = g("misc"), g("B_misc")
    convc = g("convc")
    state, B_st = g("state"), g("B_st")
    stat, B_stat = g("stat"), g("B_stat")
    a_log, dt_bias, d_vec, conv_b, b_proj = (
        g("a_log"), g("dt_bias"), g("d_vec"), g("conv_b"), g("b_proj"))

    cst, B_cst = g("cst"), g("B_cst")
    p.emit("dve", lambda e: e.memset(cst[:, 0:1], 1.0), writes=[B_cst])
    p.emit("dve", lambda e: e.memset(cst[:, 1:2], float(3.0 * LN2)),
           writes=[B_cst])
    p.emit("dve", lambda e: e.memset(cst[:, 2:3], float(-3.0 * LN2)),
           writes=[B_cst])
    p.emit("dve", lambda e: e.memset(cst[:, 3:4], float(EPS)), writes=[B_cst])
    p.emit("pool", lambda e: e.iota(
        imask[:], pattern=[[1, CH]], base=0, channel_multiplier=-1),
        writes=[B_imk])
    p.emit("dve", lambda e: e.tensor_scalar(
        out=mask01[:], in0=imask[:], scalar1=0.0, scalar2=None, op0=Alu.is_ge),
        reads=[B_imk], writes=[B_mk])
    p.emit("dve", lambda e: e.tensor_scalar(
        out=ident[:], in0=imask[:], scalar1=0.0, scalar2=None, op0=Alu.is_equal),
        reads=[B_imk], writes=[B_id])
    p.emit("dve", lambda e: e.memset(state[:], 0.0), writes=[B_st[0], B_st[1]])
    p.emit("dve", lambda e: e.memset(convc[:], 0.0),
           writes=[b for lb in E["B_cvc"] for b in lb])
    for l in range(2):
        p.emit("sp", lambda e, l=l: [
            e.dma_start(out=ccol[0:NH, 4 + l:5 + l],
                        in_=a_log[l, :].unsqueeze(1)),
            e.dma_start(out=ccol[0:NH, 6 + l:7 + l],
                        in_=dt_bias[l, :].unsqueeze(1)),
        ], writes=[E["B_cc"]], dma=2)
        p.emit("act", lambda e, l=l: e.activation(
            out=ccol[0:NH, 2 + l:3 + l], in_=ccol[0:NH, 4 + l:5 + l],
            func=Act.Exp), reads=[B_cc], writes=[B_cc])
        p.emit("dve", lambda e, l=l: e.tensor_scalar(
            out=ccol[0:NH, l:l + 1], in0=ccol[0:NH, 2 + l:3 + l],
            scalar1=-1.0, scalar2=None, op0=Alu.mult),
            reads=[B_cc], writes=[B_cc])
        p.emit("sp", lambda e, l=l: [
            e.dma_start(
                out=drep[0:64, l * NHT:(l + 1) * NHT],
                in_=d_vec[l, :].rearrange("(t h) -> h t", h=2)[0]
                .unsqueeze(0).broadcast_to([64, NHT]),
                allow_slow_non_contiguous=True),
            e.dma_start(
                out=drep[64:128, l * NHT:(l + 1) * NHT],
                in_=d_vec[l, :].rearrange("(t h) -> h t", h=2)[1]
                .unsqueeze(0).broadcast_to([64, NHT]),
                allow_slow_non_contiguous=True)],
            writes=[B_dr], dma=2)
        p.emit("sp", lambda e, l=l: e.dma_start(
            out=g("drep2")[:, l * NH:(l + 1) * NH],
            in_=d_vec[l, :].unsqueeze(0).broadcast_to([128, NH])),
            writes=[g("B_dr2")], dma=1)
        p.emit("sp", lambda e, l=l: e.dma_start(
            out=cbcol[:, l * NXT:(l + 1) * NXT],
            in_=conv_b[l, :].rearrange("(a b) -> b a", b=128),
            allow_slow_non_contiguous=True),
            writes=[B_cb], dma=1)
        p.emit("sp", lambda e, l=l: e.dma_start(
            out=g("cw")[:, l * NXT * 4:(l + 1) * NXT * 4]
            .rearrange("p (t k) -> p t k", k=4),
            in_=g("conv_w")[l].rearrange("(t p) k -> p t k", p=128)),
            writes=[g("B_cw")], dma=1)
    p.emit("sp", lambda e: e.dma_start(
        out=misc[:, 0:D],
        in_=b_proj[:].unsqueeze(0).broadcast_to([128, D])),
        writes=[B_misc], dma=1)


def _wsplit_store(p, E, src_f32_ap, ncols, scol_ap, hi_ap, lo_ap,
                  rbufs, wbufs):
    mf, B = E["mf"], None

    def f_sc(e):
        return e.tensor_scalar(out=src_f32_ap, in0=src_f32_ap,
                               scalar1=scol_ap, scalar2=None, op0=Alu.mult)
    p.emit("dve", f_sc, reads=rbufs, writes=[wbufs[0]])

    def f_hi(e):
        return e.copy(out=hi_ap, in_=src_f32_ap)
    p.emit("act", f_hi, reads=[wbufs[0]], writes=[wbufs[1]])

    def f_lo(e):
        return e.scalar_tensor_tensor(out=lo_ap, in0=src_f32_ap, scalar=1.0,
                                      in1=hi_ap, op0=Alu.mult, op1=Alu.subtract)
    p.emit("dve", f_lo, reads=[wbufs[0], wbufs[1]], writes=[wbufs[2]])


def emit_transform(E):
    p = E["p"]
    g = lambda k: E[k]
    nc = g("nc")
    mf, mb, xbc_r, stat = g("mf"), g("mb"), g("xbc_r"), g("stat")
    B_wrk, B_xbr, B_stat = g("B_wrk"), g("B_xbr"), g("B_stat")
    B_mbs = Buf("mb_scr")
    ps, PB = g("ps"), g("PB")
    w_in_t, ln_w, ln_b = g("w_in_t"), g("ln_w"), g("ln_b")
    w_out_t, rms_w = g("w_out_t"), g("rms_w")
    w_fc, w_proj = g("w_fc"), g("w_proj")
    cin, cout_, cfc, cproj, cbv = g("cin"), g("cout_"), g("cfc"), g("cproj"), g("cbv")
    cin_z, cout_z = g("cin_z"), g("cout_z")
    B_cin, B_cout, B_cfc, B_cproj, B_cbv = (
        g("B_cin"), g("B_cout"), g("B_cfc"), g("B_cproj"), g("B_cbv"))
    bvcol, B_bv = g("bvcol"), g("B_bv")
    ccol, B_cc = g("ccol"), g("B_cc")
    s_ch, B_sch = g("s_ch"), g("B_sch")
    dec, B_dec = g("dec"), g("B_dec")

    p.emit("dve", lambda e: e.memset(mb[:, 4256:4352], 0.0), writes=[B_mbs])
    p.emit("dve", lambda e: e.memset(mb[:, 4352 + 4256:4352 + 4352], 0.0),
           writes=[B_mbs])

    for l in range(2):
        for k in range(KD):
            p.emit("sp", lambda e, l=l, k=k: e.dma_start(
                out=mf[:, 0:DPROJ], in_=w_in_t[l, k * 128:(k + 1) * 128, :]),
                writes=[B_wrk], dma=1)
            p.emit("sp", lambda e, l=l, k=k: [
                e.dma_start(out=stat[:, 0:1],
                            in_=ln_w[l, k * 128:(k + 1) * 128].unsqueeze(1)),
                e.dma_start(out=stat[:, 1:2],
                            in_=ln_b[l, k * 128:(k + 1) * 128].unsqueeze(1))],
                writes=[B_stat], dma=2)

            def f_bv(e, l=l, k=k):
                r = None
                for nt in range(8):
                    r = e.matmul(ps[nt][0:1, :], stat[:, 1:2],
                                 mf[:, nt * 512:(nt + 1) * 512],
                                 start=(k == 0), stop=(k == KD - 1))
                return r
            p.emit("pe", f_bv, reads=[B_wrk, B_stat],
                   writes=[PB[nt] for nt in range(8)])

            _wsplit_store(p, E, mf[:, 0:DPROJ], DPROJ, stat[:, 0:1],
                          mb[:, 0:DPROJ], mb[:, 4352:4352 + DPROJ],
                          [B_stat], [B_wrk, B_mbs, B_mbs])

            def f_st(e, l=l, k=k):
                ops = []
                for h in range(2):
                    hoff = 4352 * h
                    a0 = (k % 2) * 2 + h
                    ops.append(e.dma_start(
                        out=cin_z[l, :, k // 2, :, a0 * 512:(a0 + 1) * 512]
                        .rearrange("g p c -> p g c"),
                        in_=mb[:, hoff:hoff + 2048]
                        .rearrange("p (g c) -> p g c", c=512)))
                    ops.append(e.dma_start(
                        out=cin[l, 16:NDC, k, h].rearrange("c p n -> p c n"),
                        in_=mb[:, hoff + 2048:hoff + 4352]
                        .rearrange("p (c n) -> p c n", n=128)))
                return ops
            p.emit("sp", f_st, reads=[B_mbs], writes=[B_cin], dma=4)
        for nt in range(8):
            def f_ev(e, nt=nt):
                return e.tensor_copy(s_ch[0:1, :], ps[nt][0:1, :])
            p.emit("dve", f_ev, reads=[PB[nt]], writes=[B_sch])

            def f_dst(e, l=l, nt=nt):
                return e.dma_start(out=cbv[l, nt * 512:(nt + 1) * 512]
                                   .unsqueeze(0), in_=s_ch[0:1, :])
            p.emit("act", f_dst, reads=[B_sch], writes=[B_cbv], dma=1)
        for k in range(KD):
            p.emit("sp", lambda e, l=l, k=k: e.dma_start(
                out=dec[:, 0:160],
                in_=w_in_t[l, k * 128:(k + 1) * 128, 4096:4256]),
                writes=[B_dec], dma=1)
            p.emit("sp", lambda e, l=l, k=k: e.dma_start(
                out=stat[:, 1:2],
                in_=ln_b[l, k * 128:(k + 1) * 128].unsqueeze(1)),
                writes=[B_stat], dma=1)

            def f_bv2(e, l=l, k=k):
                return e.matmul(ps[0][0:1, 0:160], stat[:, 1:2], dec[:, 0:160],
                                start=(k == 0), stop=(k == KD - 1))
            p.emit("pe", f_bv2, reads=[B_dec, B_stat], writes=[PB[0]])
        p.emit("dve", lambda e: e.tensor_copy(s_ch[0:1, 0:160],
                                              ps[0][0:1, 0:160]),
               reads=[PB[0]], writes=[B_sch])
        p.emit("act", lambda e, l=l: e.dma_start(
            out=cbv[l, 4096:4256].unsqueeze(0), in_=s_ch[0:1, 0:160]),
            reads=[B_sch], writes=[B_cbv], dma=1)
        p.emit("dve", lambda e: e.memset(s_ch[0:1, 160:256], 0.0),
               writes=[B_sch])
        p.emit("act", lambda e, l=l: e.dma_start(
            out=cbv[l, 4256:4352].unsqueeze(0), in_=s_ch[0:1, 160:256]),
            reads=[B_sch], writes=[B_cbv], dma=1)
        p.emit("sp", lambda e, l=l: e.dma_start(
            out=bvcol[:, l * NDC:(l + 1) * NDC],
            in_=cbv[l, :].rearrange("(a b) -> b a", b=128),
            allow_slow_non_contiguous=True),
            reads=[B_cbv], writes=[B_bv], dma=1)
        p.emit("dve", lambda e, l=l: e.tensor_scalar(
            out=ccol[0:NH, 8 + l:9 + l], in0=bvcol[0:NH, l * NDC + 33:l * NDC + 34],
            scalar1=ccol[0:NH, 6 + l:7 + l], scalar2=None, op0=Alu.add),
            reads=[B_bv, B_cc], writes=[B_cc])

        for k in range(KDIN):
            p.emit("sp", lambda e, l=l, k=k: e.dma_start(
                out=mf[:, 0:D], in_=w_out_t[l, k * 128:(k + 1) * 128, :]),
                writes=[B_wrk], dma=1)
            p.emit("sp", lambda e, l=l, k=k: e.dma_start(
                out=stat[:, 0:1],
                in_=rms_w[l, k * 128:(k + 1) * 128].unsqueeze(1)),
                writes=[B_stat], dma=1)
            _wsplit_store(p, E, mf[:, 0:D], D, stat[:, 0:1],
                          mb[:, 0:D], mb[:, 4352:4352 + D],
                          [B_stat], [B_wrk, B_mbs, B_mbs])

            def f_so(e, l=l, k=k):
                ops = []
                for h in range(2):
                    hoff = 4352 * h
                    a0 = (k % 2) * 2 + h
                    ops.append(e.dma_start(
                        out=cout_z[l, :, k // 2, :, a0 * 512:(a0 + 1) * 512]
                        .rearrange("s p c -> p s c"),
                        in_=mb[:, hoff:hoff + D]
                        .rearrange("p (s c) -> p s c", c=512)))
                return ops
            p.emit("sp", f_so, reads=[B_mbs], writes=[B_cout], dma=2)

    for which, wap, rows, blocks, cdst, B_c in (
            ("fc", w_fc, HID // 128, D // BK, cfc, B_cfc),
            ("proj", w_proj, D // 128, HID // BK, cproj, B_cproj)):
        ncols = D if which == "fc" else HID
        nkt = ncols // 128
        for r in range(rows):
            p.emit("sp", lambda e, wap=wap, r=r, ncols=ncols: e.dma_start(
                out=mf[:, 0:ncols], in_=wap[r * 128:(r + 1) * 128, :]),
                writes=[B_wrk], dma=1)
            for q in range(ncols // 1024):
                emit_quant(E, "dve", mf[:, q * 1024:(q + 1) * 1024],
                           mb[:, q * 1024:(q + 1) * 1024], 1024,
                           [B_wrk], B_mbs,
                           stat, B_stat, mf_scr(E), B_wrk2(E))
            for kt in range(nkt):
                eng = "sp" if kt % 2 == 0 else "act"
                p.emit(eng, lambda e, kt=kt, ncols=ncols: e.dma_start_transpose(
                    mb[:, 8704 + kt * 128:8704 + (kt + 1) * 128],
                    mb[:, kt * 128:(kt + 1) * 128]),
                    reads=[B_mbs], writes=[B_mbs], dma=1)
            def f_str(e, r=r, cdst=cdst, nkt=nkt):
                return e.dma_start(
                    out=cdst[:, r // 4, :, (r % 4) * 128:(r % 4) * 128 + 128]
                    .rearrange("c p n -> p c n"),
                    in_=mb[:, 8704:8704 + nkt * 128]
                    .rearrange("p (c n) -> p c n", n=128))
            p.emit("sp", f_str, reads=[B_mbs], writes=[B_c], dma=1)


def mf_scr(E):
    return E["mf"][:, 4608:6656]


def B_wrk2(E):
    return E["B_wrk"]


def emit_quant(E, eng, src_ap, dst_ap, ncols, rbufs, B_dst, stat, B_stat,
               scr, B_scr):
    p = E["p"]
    nb = ncols // BK
    am = stat[:, 0:nb]
    am2 = stat[:, nb:2 * nb]
    lgc = stat[:, 2 * nb:3 * nb]
    ev = stat[:, 3 * nb:4 * nb]
    tt = stat[:, 4 * nb:5 * nb]
    bb = stat[:, 5 * nb:6 * nb]
    inv = stat[:, 6 * nb:7 * nb]
    sc = stat[:, 7 * nb:8 * nb]

    def f_am(e):
        return e.tensor_reduce(out=am, in_=src_ap.rearrange(
            "p (b k) -> p b k", k=BK), axis=AX.X, op=Alu.max,
            apply_absolute_value=True)
    p.emit(eng, f_am, reads=rbufs, writes=[B_stat])

    def f_am2(e):
        return e.tensor_scalar(out=am2, in0=am, scalar1=float(2.0 ** -126),
                               scalar2=None, op0=Alu.max)
    p.emit(eng, f_am2, reads=[B_stat], writes=[B_stat])

    def f_lg(e):
        return e.activation(out=lgc, in_=am2, func=Act.Ln)
    p.emit("act", f_lg, reads=[B_stat], writes=[B_stat])

    def f_lgc(e):
        return e.tensor_scalar(out=lgc, in0=lgc, scalar1=float(1.0 / LN2),
                               scalar2=None, op0=Alu.mult)
    p.emit(eng, f_lgc, reads=[B_stat], writes=[B_stat])

    exp_ladder(p, eng, ([B_stat], B_stat, B_stat, B_stat), lgc, ev, tt, bb)

    cst, B_cst = E["cst"], E["B_cst"]

    def f_inv(e):
        return e.activation(out=inv, in_=ev, func=Act.Exp, scale=-LN2,
                            bias=cst[:, 1:2])
    p.emit("act", f_inv, reads=[B_stat, B_cst], writes=[B_stat])

    def f_sc(e):
        return e.activation(out=sc, in_=ev, func=Act.Exp, scale=LN2,
                            bias=cst[:, 2:3])
    p.emit("act", f_sc, reads=[B_stat, B_cst], writes=[B_stat])

    inv_b = inv.unsqueeze(2).broadcast_to([128, nb, BK])
    sc_b = sc.unsqueeze(2).broadcast_to([128, nb, BK])
    src3 = src_ap.rearrange("p (b k) -> p b k", k=BK)
    dst3 = dst_ap.rearrange("p (b k) -> p b k", k=BK)
    quant_ladder(p, eng, (rbufs + [B_stat], [B_scr, B_scr, B_dst]),
                 src3, dst3, inv_b, sc_b,
                 scr[:, 0:ncols].rearrange("p (b k) -> p b k", k=BK),
                 scr[:, ncols:2 * ncols].rearrange("p (b k) -> p b k", k=BK))


def emit_ln_stats(E, src_ap, rbufs, nt_col, u_out_ap, wbufs, x2_scr_ap, B_x2):
    p = E["p"]
    stat, B_stat = E["stat"], E["B_stat"]
    c0 = 512 + nt_col * 8
    sx = stat[:, c0:c0 + 1]
    sxx = stat[:, c0 + 1:c0 + 2]
    m = stat[:, c0 + 2:c0 + 3]
    b4 = stat[:, c0 + 3:c0 + 4]
    rstd = stat[:, c0 + 4:c0 + 5]

    p.emit("dve", lambda e: e.tensor_reduce(
        out=sx, in_=src_ap, axis=AX.X, op=Alu.add),
        reads=rbufs, writes=[B_stat])
    p.emit("act", lambda e: e.activation(
        out=x2_scr_ap, in_=src_ap, func=Act.Square, accum_out=sxx),
        reads=rbufs, writes=[B_x2, B_stat])
    p.emit("dve", lambda e: e.tensor_scalar(
        out=m, in0=sx, scalar1=float(1.0 / D), scalar2=None, op0=Alu.mult),
        reads=[B_stat], writes=[B_stat])
    p.emit("dve", lambda e: e.tensor_scalar(
        out=b4, in0=m, scalar1=m, scalar2=-1.0, op0=Alu.mult, op1=Alu.mult),
        reads=[B_stat], writes=[B_stat])
    p.emit("dve", lambda e: e.tensor_scalar(
        out=b4, in0=b4, scalar1=EPS, scalar2=None, op0=Alu.add),
        reads=[B_stat], writes=[B_stat])
    p.emit("act", lambda e: e.activation(
        out=rstd, in_=sxx, func=Act.Sqrt, scale=float(1.0 / D), bias=b4),
        reads=[B_stat], writes=[B_stat])
    p.emit("dve", lambda e: e.reciprocal(out=rstd, in_=rstd),
        reads=[B_stat], writes=[B_stat])
    p.emit("dve", lambda e: e.tensor_scalar(
        out=u_out_ap, in0=src_ap, scalar1=m, scalar2=rstd,
        op0=Alu.subtract, op1=Alu.mult),
        reads=rbufs + [B_stat], writes=wbufs)
    return rstd


def emit_sweep(E, sw, b):
    p = E["p"]
    g = lambda k: E[k]
    sweep, NT, NCH = g("sweep"), g("NT"), g("NCH")
    n_sweeps = g("n_sweeps")
    x_in, y_out = g("x_in"), g("y_out")
    B_y = g("B_y")
    h_t, B_ht = g("h_t"), g("B_ht")
    uh, ul, B_uh, B_ul = g("uh"), g("ul"), g("B_uh"), g("B_ul")
    uf_hi, uf_lo, B_ufh, B_ufl = g("uf_hi"), g("uf_lo"), g("B_ufh"), g("B_ufl")
    sg, B_sg = g("sg"), g("B_sg")
    yg, B_yg = g("yg"), g("B_yg")
    stat, B_stat = g("stat"), g("B_stat")
    wring, B_wr = g("wring"), g("B_wr")
    cin, cout_, B_cin, B_cout = g("cin"), g("cout_"), g("B_cin"), g("B_cout")
    ps, PB = g("ps"), g("PB")
    xbc_r, B_xbr = g("xbc_r"), g("B_xbr")
    bvcol, B_bv = g("bvcol"), g("B_bv")
    cbcol, B_cb = g("cbcol"), g("B_cb")
    ccol, B_cc = g("ccol"), g("B_cc")
    drep, B_dr = g("drep"), g("B_dr")
    dts = g("dts")
    B_dt0, B_dt1, B_dt2 = g("B_dt0"), g("B_dt1"), g("B_dt2")
    stk, B_stk = g("stk"), g("B_stk")
    dr_a, dr_ea, dr_rs = g("dr_a"), g("dr_ea"), g("dr_rs")
    B_dra, B_drea, B_drrs = g("B_dra"), g("B_drea"), g("B_drrs")
    mb, mf = g("mb"), g("mf")
    convc = g("convc")
    B_cvc = g("B_cvc")
    state, B_st = g("state"), g("B_st")
    ident, B_id = g("ident"), g("B_id")
    mask01, B_mk = g("mask01"), g("B_mk")
    aeb, B_aeb = g("aeb"), g("B_aeb")
    wTt, B_wT = g("wTt"), g("B_wT")
    abg, B_abg = g("abg"), g("B_abg")
    dec, B_dec = g("dec"), g("B_dec")
    yg, B_yg = g("yg"), g("B_yg")
    zt, B_zt = g("zt"), g("B_zt")
    s_ch, B_sch = g("s_ch"), g("B_sch")
    misc, B_misc = g("misc"), g("B_misc")
    btile, B_bt = g("btile"), g("B_bt")
    xr, B_xr = g("xr"), g("B_xr")
    xq_t, B_xq = g("uh"), g("B_uh")
    hq_t, B_hq = g("hqs"), g("B_hqs")
    xqT, B_xqT = g("uf_hi"), g("B_ufh")
    wf, B_wf = g("wf"), g("B_wf")
    hqT, B_hqT = g("mb"), g("B_mb_hqT")
    cfc, cproj, B_cfc, B_cproj = g("cfc"), g("cproj"), g("B_cfc"), g("B_cproj")
    b_fc = g("b_fc")
    B_xbs, B_xtf, B_ea, B_bw = g("B_xbs"), g("B_xtf"), g("B_ea"), g("B_bw")
    B_xhh, B_xhl, B_xth, B_xtl = g("B_xhh"), g("B_xhl"), g("B_xth"), g("B_xtl")
    B_ygh, B_ygl = g("B_ygh"), g("B_ygl")
    B_bpk, B_cpa, B_cpb = g("B_bpk"), g("B_cpa"), g("B_cpb")
    B_sth, B_stl, B_stf = g("B_sth"), g("B_stl"), g("B_stf")
    B_st0, B_bcT, B_adt, B_wrk = g("B_st0"), g("B_bcT"), g("B_adt"), g("B_wrk")
    MB_XHH, MB_XHL = g("MB_XHH"), g("MB_XHL")
    MB_XTH, MB_XTL = g("MB_XTH"), g("MB_XTL")
    MB_YGH, MB_YGL = g("MB_YGH"), g("MB_YGL")
    MB_BPK, MB_CPA, MB_CPB = g("MB_BPK"), g("MB_CPA"), g("MB_CPB")
    MB_SCR = 12672
    MB_STH, MB_STL = 12800, 13312
    MF_XBS, MF_XTF, MF_BW = g("MF_XBS"), g("MF_XTF"), g("MF_BW")
    MF_STF, MF_ST0, MF_BCT, MF_ADT, MF_WRK = (
        g("MF_STF"), g("MF_ST0"), g("MF_BCT"), g("MF_ADT"), g("MF_WRK"))
    MF_EXP = 6080
    HSL, PSL = g("HSL"), g("PSL")
    row0 = sw * sweep

    for nt in range(NT):
        r0 = row0 + nt * 128
        p.emit("sp", lambda e, r0=r0: e.dma_start(
            out=sg[:], in_=x_in[r0:r0 + 128, :]), writes=[B_sg], dma=1)
        emit_ln_stats(E, sg[:], [B_sg], nt,
                      h_t[:, nt * D:(nt + 1) * D], [B_ht], yg[:, 0:D], B_yg)

    if sw == n_sweeps // 2:
        state, B_st = g("state"), g("B_st")
        p.emit("dve", lambda e: e.memset(state[:], 0.0),
               writes=[B_st[0], B_st[1]])

    for l in range(_DBG_NL):
        emit_mamba_layer(E, sw, b, l)

    p.barrier()
    if not _DBG_FFN:
        for nt in range(NT):
            r0 = row0 + nt * 128
            p.emit("dve", lambda e, nt=nt: e.tensor_copy(
                sg[:], h_t[:, nt * D:(nt + 1) * D]),
                reads=[B_ht], writes=[B_sg])
            p.emit("act", lambda e, r0=r0: e.dma_start(
                out=y_out[r0:r0 + 128, :], in_=sg[:]),
                reads=[B_sg], writes=[B_y], dma=1)
        p.barrier()
        return
    for nt in range(NT):
        r0 = row0 + nt * 128
        p.emit("sp", lambda e, r0=r0: e.dma_start(
            out=xr[:], in_=x_in[r0:r0 + 128, :]), writes=[B_xr], dma=1)
        emit_ln_stats(E, xr[:], [B_xr], nt, sg[:], [B_sg], yg[:, 0:D], B_yg)
        emit_quant(E, "dve", sg[:], xq_t[:, nt * D:(nt + 1) * D], D,
                   [B_sg], B_xq, dec, B_dec, mf, B_wrk)
        def f_txq(e, nt=nt):
            return [e.dma_start_transpose(
                xqT[:, k * sweep + nt * 128:k * sweep + (nt + 1) * 128],
                xq_t[:, nt * D + k * 128:nt * D + (k + 1) * 128])
                for k in range(KD)]
        p.emit("sp", f_txq, reads=[B_xq], writes=[B_xqT], dma=KD)
    harea = [(mf, B_wrk, 0), (mf, B_wrk, HID),
             (xbc_r, B_xbr, 0), (xbc_r, B_xbr, HID)]
    for sl in range(HSL):
        p.emit("sp", lambda e, sl=sl: e.dma_start(
            out=btile[:], in_=b_fc[sl * 512:(sl + 1) * 512]
            .unsqueeze(0).broadcast_to([128, 512])),
            writes=[B_bt], dma=1)
        for k in range(KD):
            s4 = (sl * KD + k) % 4
            p.emit("sp", lambda e, sl=sl, k=k, s4=s4: e.dma_start(
                out=wf[:, s4 * 512:(s4 + 1) * 512], in_=cfc[k, sl]),
                reads=[B_cfc], writes=[B_wf[s4]], dma=1)

            def f_mm(e, sl=sl, k=k, s4=s4):
                r = None
                for nt in range(NT):
                    r = e.matmul(
                        ps[nt][:, :],
                        xqT[:, k * sweep + nt * 128:k * sweep + (nt + 1) * 128],
                        wf[:, s4 * 512:(s4 + 1) * 512],
                        start=(k == 0), stop=(k == KD - 1))
                return r
            p.emit("pe", f_mm, reads=[B_xqT, B_wf[s4]],
                   writes=[PB[nt] for nt in range(NT)])
        for nt in range(NT):
            ar, abuf, aoff = harea[nt]

            def f_bias(e, nt=nt, ar=ar, aoff=aoff, sl=sl):
                return e.tensor_tensor(
                    out=ar[:, aoff + sl * 512:aoff + (sl + 1) * 512],
                    in0=ps[nt][:, :], in1=btile[:], op=Alu.add)
            p.emit("dve", f_bias, reads=[PB[nt], B_bt], writes=[abuf])

            def f_gelu(e, nt=nt, ar=ar, aoff=aoff, sl=sl):
                return e.activation(
                    out=ar[:, aoff + sl * 512:aoff + (sl + 1) * 512],
                    in_=ar[:, aoff + sl * 512:aoff + (sl + 1) * 512],
                    func=Act.Gelu)
            p.emit("act", f_gelu, reads=[abuf], writes=[abuf])
    for nt in range(NT):
        ar, abuf, aoff = harea[nt]
        for q in range(4):
            emit_quant(E, "dve",
                       ar[:, aoff + q * 1024:aoff + (q + 1) * 1024],
                       hq_t[:, q * 1024:(q + 1) * 1024],
                       1024, [abuf], B_hq, dec, B_dec, yg, B_yg)
        def f_thq(e, nt=nt, par=0):
            return [e.dma_start_transpose(
                hqT[:, kt * sweep + nt * 128:kt * sweep + (nt + 1) * 128],
                hq_t[:, kt * 128:(kt + 1) * 128])
                for kt in range(par, HID // 128, 2)]
        p.emit("sp", lambda e, nt=nt: f_thq(e, nt, 0),
               reads=[B_hq], writes=[B_hqT], dma=HID // 256)
        p.emit("act", lambda e, nt=nt: f_thq(e, nt, 1),
               reads=[B_hq], writes=[B_hqT], dma=HID // 256)
    for psl in range(PSL):
        for kt in range(HID // 128):
            s4 = (psl * (HID // 128) + kt) % 4
            p.emit("sp", lambda e, psl=psl, kt=kt, s4=s4: e.dma_start(
                out=wf[:, s4 * 512:(s4 + 1) * 512], in_=cproj[kt, psl]),
                reads=[B_cproj], writes=[B_wf[s4]], dma=1)

            def f_pm(e, psl=psl, kt=kt, s4=s4):
                r = None
                for nt in range(NT):
                    bank = (4 + nt) if psl == 0 else nt
                    r = e.matmul(
                        ps[bank][:, :],
                        hqT[:, kt * sweep + nt * 128:kt * sweep + (nt + 1) * 128],
                        wf[:, s4 * 512:(s4 + 1) * 512],
                        start=(kt == 0), stop=(kt == HID // 128 - 1))
                return r
            p.emit("pe", f_pm, reads=[B_hqT, B_wf[s4]],
                   writes=[PB[(4 + nt) if psl == 0 else nt] for nt in range(NT)])
    for nt in range(NT):
        r0 = row0 + nt * 128
        p.emit("sp", lambda e, r0=r0: e.dma_start(
            out=xr[:], in_=x_in[r0:r0 + 128, :]), writes=[B_xr], dma=1)
        for psl in range(PSL):
            bank = (4 + nt) if psl == 0 else nt

            def f_gb(e, nt=nt, psl=psl, bank=bank):
                return e.tensor_tensor(
                    out=sg[:, psl * 512:(psl + 1) * 512], in0=ps[bank][:, :],
                    in1=misc[:, psl * 512:(psl + 1) * 512],
                    op=Alu.add)
            p.emit("dve", f_gb, reads=[PB[bank], B_misc], writes=[B_sg])
        p.emit("act", lambda e: e.activation(out=sg[:], in_=sg[:],
                                             func=Act.Silu),
               reads=[B_sg], writes=[B_sg])
        p.emit("dve", lambda e, nt=nt: e.tensor_tensor(
            out=sg[:], in0=sg[:], in1=h_t[:, nt * D:(nt + 1) * D],
            op=Alu.mult), reads=[B_ht, B_sg], writes=[B_sg])
        p.emit("dve", lambda e: e.tensor_tensor(
            out=sg[:], in0=sg[:], in1=xr[:], op=Alu.add),
            reads=[B_xr, B_sg], writes=[B_sg])
        p.emit("act", lambda e, r0=r0: e.dma_start(
            out=y_out[r0:r0 + 128, :], in_=sg[:]),
            reads=[B_sg], writes=[B_y], dma=1)
    p.barrier()


def emit_mamba_layer(E, sw, b, l):
    p = E["p"]
    g = lambda k: E[k]
    sweep, NT, NCH = g("sweep"), g("NT"), g("NCH")
    h_t, B_ht = g("h_t"), g("B_ht")
    uh, ul, B_uh, B_ul = g("uh"), g("ul"), g("B_uh"), g("B_ul")
    uf_hi, uf_lo, B_ufh, B_ufl = g("uf_hi"), g("uf_lo"), g("B_ufh"), g("B_ufl")
    sg, B_sg = g("sg"), g("B_sg")
    stat, B_stat = g("stat"), g("B_stat")
    wring, B_wr = g("wring"), g("B_wr")
    cin, B_cin = g("cin"), g("B_cin")
    ps, PB = g("ps"), g("PB")
    xbc_r, B_xbr = g("xbc_r"), g("B_xbr")
    bvcol, B_bv = g("bvcol"), g("B_bv")
    ccol, B_cc = g("ccol"), g("B_cc")
    dts = g("dts")
    B_dt0, B_dt1, B_dt2 = g("B_dt0"), g("B_dt1"), g("B_dt2")
    stk, B_stk = g("stk"), g("B_stk")
    dr_a, dr_ea, dr_rs = g("dr_a"), g("dr_ea"), g("dr_rs")
    B_dra, B_drea, B_drrs = g("B_dra"), g("B_drea"), g("B_drrs")
    convc, B_cvc = g("convc"), g("B_cvc")
    misc, B_misc = g("misc"), g("B_misc")
    cw, B_cw = g("cw"), g("B_cw")
    S = sweep
    SP4 = S + 4

    B_xhh, B_xhl = g("B_xhh"), g("B_xhl")
    B_xth2, B_xtl2 = g("B_xth"), g("B_xtl")
    for nt in range(NT):
        src_ap, src_bufs = h_t[:, nt * D:(nt + 1) * D], [g("B_ht")]
        emit_ln_stats(E, src_ap, src_bufs, nt, sg[:], [B_sg],
                      g("yg")[:, 0:D], g("B_yg"))
        uh_w = [B_uh] + ([B_xhh[0], B_xhl[0]] if nt == 0 else [])
        ul_w = [B_ul] + ([B_xth2[0], B_xtl2[0]] if nt == 0 else [])
        p.emit("act", lambda e, nt=nt: e.copy(
            out=uh[:, nt * D:(nt + 1) * D], in_=sg[:]),
            reads=[B_sg], writes=uh_w)
        p.emit("dve", lambda e, nt=nt: e.scalar_tensor_tensor(
            out=ul[:, nt * D:(nt + 1) * D], in0=sg[:], scalar=1.0,
            in1=uh[:, nt * D:(nt + 1) * D], op0=Alu.mult, op1=Alu.subtract),
            reads=[B_sg, B_uh], writes=ul_w)
    pairs = [(nt, k) for nt in range(NT) for k in range(KD)]

    def f_tuh(e, par):
        return [e.dma_start_transpose(
            uf_hi[:, k * S + nt * 128:k * S + (nt + 1) * 128],
            uh[:, nt * D + k * 128:nt * D + (k + 1) * 128])
            for (nt, k) in pairs[par::2]]

    def f_tul(e, par):
        return [e.dma_start_transpose(
            uf_lo[:, k * S + nt * 128:k * S + (nt + 1) * 128],
            ul[:, nt * D + k * 128:nt * D + (k + 1) * 128])
            for (nt, k) in pairs[par::2]]
    p.emit("sp", lambda e: f_tuh(e, 0), reads=[B_uh], writes=[B_ufh],
           dma=len(pairs) // 2)
    p.emit("act", lambda e: f_tuh(e, 1), reads=[B_uh], writes=[B_ufh],
           dma=len(pairs) - len(pairs) // 2)
    p.emit("sp", lambda e: f_tul(e, 0), reads=[B_ul], writes=[B_ufl],
           dma=len(pairs) // 2)
    p.emit("act", lambda e: f_tul(e, 1), reads=[B_ul], writes=[B_ufl],
           dma=len(pairs) - len(pairs) // 2)

    cslot = (l * 2 + b) * NXT * 4
    p.emit("dve", lambda e, cslot=cslot: e.tensor_copy(
        xbc_r[:].rearrange("p (t c) -> p t c", c=SP4)[:, :, 1:4],
        convc[:, cslot:cslot + NXT * 4]
        .rearrange("p (t c) -> p t c", c=4)[:, :, 1:4]),
        reads=[B_cvc[l][b]], writes=[B_xbr])

    for i, dc in enumerate(range(16, NDC)):
        slot = i % 2
        p.emit("sp", lambda e, dc=dc, slot=slot: e.dma_start(
            out=wring[:, slot * 2048:slot * 2048 + 2048]
            .rearrange("p (c n) -> p c n", n=128),
            in_=cin[l, dc].rearrange("k h p n -> p (k h) n")),
            reads=[B_cin], writes=[B_wr[slot]], dma=1)

        def f_mm(e, dc=dc, slot=slot):
            r = None
            for k in range(KD):
                wh = wring[:, slot * 2048 + (k * 2) * 128:slot * 2048 + (k * 2 + 1) * 128]
                wl = wring[:, slot * 2048 + (k * 2 + 1) * 128:slot * 2048 + (k * 2 + 2) * 128]
                xh_ = uf_hi[:, k * S:(k + 1) * S]
                xl_ = uf_lo[:, k * S:(k + 1) * S]
                r = e.matmul(ps[slot][:, 0:S], wh, xh_, start=(k == 0), stop=False)
                r = e.matmul(ps[slot][:, 0:S], wh, xl_, start=False, stop=False)
                r = e.matmul(ps[slot][:, 0:S], wl, xh_, start=False,
                             stop=(k == KD - 1))
            return r
        p.emit("pe", f_mm, reads=[B_wr[slot], B_ufh, B_ufl],
               writes=[PB[slot]])
        if dc < 33:
            xt = dc - 16

            def f_ev(e, dc=dc, slot=slot, xt=xt):
                return e.activation(
                    out=xbc_r[:, xt * SP4 + 4:xt * SP4 + 4 + S],
                    in_=ps[slot][:, 0:S], func=Act.Identity,
                    bias=bvcol[:, l * NDC + dc:l * NDC + dc + 1])
            p.emit("act", f_ev, reads=[PB[slot], B_bv], writes=[B_xbr])
        else:
            def f_ev(e, slot=slot):
                return e.activation(
                    out=dts[:, 0:S], in_=ps[slot][0:NH, 0:S], func=Act.Identity,
                    bias=ccol[0:NH, 8 + l:9 + l])
            p.emit("act", f_ev, reads=[PB[slot], B_cc], writes=[B_dt0])

    q0, q1, q2 = dts[:, 0:S], dts[:, S:2 * S], dts[:, 2 * S:3 * S]
    p.emit("act", lambda e: e.activation(out=q2, in_=q0, func=Act.Abs),
           reads=[B_dt0], writes=[B_dt2])
    p.emit("act", lambda e: e.activation(out=q2, in_=q2, func=Act.Exp,
                                         scale=-1.0),
           reads=[B_dt2], writes=[B_dt2])
    p.emit("act", lambda e: e.activation(out=q2, in_=q2, func=Act.Ln,
                                         bias=g("cst")[0:NH, 0:1]),
           reads=[B_dt2, g("B_cst")], writes=[B_dt2])
    p.emit("dve", lambda e: e.scalar_tensor_tensor(
        out=q1, in0=q0, scalar=0.0, in1=q2, op0=Alu.max, op1=Alu.add),
        reads=[B_dt0, B_dt2], writes=[B_dt1])
    p.emit("act", lambda e: e.activation(out=q2, in_=q1, func=Act.Ln),
           reads=[B_dt1], writes=[B_dt2])
    p.emit("act", lambda e: [
        e.dma_start(out=stk[32:64, :], in_=q1),
        e.dma_start(out=stk[64:96, :], in_=q2)],
        reads=[B_dt1, B_dt2], writes=[B_stk], dma=2)
    p.emit("dve", lambda e, l=l: e.tensor_scalar(
        out=q0, in0=q1, scalar1=ccol[0:NH, l:l + 1], scalar2=None,
        op0=Alu.mult), reads=[B_dt1, g("B_cc")], writes=[B_dt0])
    for ck in range(NCH):
        p.emit("dve", lambda e, ck=ck: e.tensor_tensor_scan(
            out=stk[0:32, ck * CH:(ck + 1) * CH],
            data0=q0[:, ck * CH:(ck + 1) * CH],
            data1=q0[:, ck * CH:(ck + 1) * CH],
            initial=0.0, op0=Alu.add, op1=Alu.bypass),
            reads=[B_dt0], writes=[B_stk])
    p.emit("act", lambda e: e.activation(out=q0, in_=stk[0:32, :],
                                         func=Act.Exp),
           reads=[B_stk], writes=[B_dt0])
    p.emit("act", lambda e: e.dma_start(out=dr_a[:, :], in_=stk[0:32, :]),
           reads=[B_stk], writes=[B_dra], dma=1)
    p.emit("act", lambda e: e.dma_start(out=dr_ea[:, :], in_=q0),
           reads=[B_dt0], writes=[B_drea], dma=1)

    p.emit("sp", lambda e, l=l: e.dma_start(
        out=g("dec")[:, 0:DIN],
        in_=g("cbv")[l, 0:DIN].unsqueeze(0).broadcast_to([128, DIN])),
        reads=[g("B_cbv")], writes=[g("B_dec")], dma=1)

    for ck in range(NCH):
        emit_chunk(E, sw, b, l, ck)

    p.emit("dve", lambda e, cslot=cslot: e.tensor_copy(
        convc[:, cslot:cslot + NXT * 4]
        .rearrange("p (t c) -> p t c", c=4)[:, :, 1:4],
        xbc_r[:].rearrange("p (t c) -> p t c", c=SP4)[:, :, S + 1:S + 4]),
        reads=[B_xbr], writes=[B_cvc[l][b]])


def emit_chunk(E, sw, b, l, ck):
    p = E["p"]
    g = lambda k: E[k]
    sweep, NCH = g("sweep"), g("NCH")
    S, SP4 = sweep, sweep + 4
    xbc_r, B_xbr = g("xbc_r"), g("B_xbr")
    mb, mf = g("mb"), g("mf")
    stk, B_stk = g("stk"), g("B_stk")
    dr_a, B_dra = g("dr_a"), g("B_dra")
    ps, PB = g("ps"), g("PB")
    ident, B_id = g("ident"), g("B_id")
    mask01, B_mk = g("mask01"), g("B_mk")
    aeb, B_aeb = g("aeb"), g("B_aeb")
    wTt, B_wT = g("wTt"), g("B_wT")
    abg, B_abg = g("abg"), g("B_abg")
    s_ch, B_sch = g("s_ch"), g("B_sch")
    cbcol, B_cb = g("cbcol"), g("B_cb")
    cw, B_cw = g("cw"), g("B_cw")
    B_xbs, B_xtf, B_bw = g("B_xbs"), g("B_xtf"), g("B_bw")
    B_xhh, B_xhl, B_xth, B_xtl = g("B_xhh"), g("B_xhl"), g("B_xth"), g("B_xtl")
    B_bpk, B_cpa, B_cpb = g("B_bpk"), g("B_cpa"), g("B_cpb")
    B_st0, B_bcT, B_adt, B_wrk = g("B_st0"), g("B_bcT"), g("B_adt"), g("B_wrk")
    MB_XHH, MB_XHL = g("MB_XHH"), g("MB_XHL")
    MB_XTH, MB_XTL = g("MB_XTH"), g("MB_XTL")
    MB_BPK, MB_CPA, MB_CPB = g("MB_BPK"), g("MB_CPA"), g("MB_CPB")
    MB_SCR = 12672
    MF_XBS, MF_XTF, MF_BW = g("MF_XBS"), g("MF_XTF"), g("MF_BW")
    MF_STF, MF_ST0, MF_BCT, MF_ADT, MF_WRK = (
        g("MF_STF"), g("MF_ST0"), g("MF_BCT"), g("MF_ADT"), g("MF_WRK"))
    MF_EXP = 6080

    xbs = lambda t: mf[:, MF_XBS + t * 128:MF_XBS + (t + 1) * 128]
    t16 = lambda rs, re_: mf[rs:re_, MF_XBS + 16 * 128:MF_XBS + 17 * 128]
    acc3 = mf[:, MF_XBS:MF_XBS + DIN].rearrange("p (t n) -> p t n", n=128)
    tmp3 = mf[:, MF_XTF:MF_XTF + DIN].rearrange("p (t n) -> p t n", n=128)

    def win3(kk, ck=ck):
        return bass.AP(tensor=xbc_r.tensor, offset=1 + kk + ck * CH,
                       ap=[[NXT * SP4, 128], [SP4, 16], [1, CH]])             if False else             xbc_r[:, 0:16 * SP4].rearrange("p (t n) -> p t n", n=SP4)[
                :, :, 1 + kk + ck * CH:1 + kk + ck * CH + CH]

    def cw3(kk):
        return cw[:, l * NXT * 4 + kk:l * NXT * 4 + kk + 16 * 4:1]             .rearrange("p (t o) -> p t o", o=4)[:, :, 0:1]             .broadcast_to([128, 16, CH])

    def cb3():
        return cbcol[:, l * NXT:l * NXT + 16].unsqueeze(2)             .broadcast_to([128, 16, CH])

    def f_cb0(e):
        return e.tensor_tensor(out=acc3, in0=win3(0), in1=cw3(0), op=Alu.mult)
    p.emit("dve", f_cb0, reads=[B_xbr, B_cw], writes=[B_xbs[0]])
    for kk in range(1, 4):
        def f_cm(e, kk=kk):
            return e.tensor_tensor(out=tmp3, in0=win3(kk), in1=cw3(kk),
                                   op=Alu.mult)
        p.emit("dve", f_cm, reads=[B_xbr, B_cw], writes=[B_xtf[0]])

        def f_ca(e):
            return e.tensor_tensor(out=acc3, in0=acc3, in1=tmp3, op=Alu.add)
        p.emit("dve", f_ca, reads=[B_xtf[0], B_xbs[0]], writes=[B_xbs[0]])

    def f_cba(e):
        return e.tensor_tensor(out=acc3, in0=acc3, in1=cb3(), op=Alu.add)
    p.emit("dve", f_cba, reads=[B_cb, B_xbs[0]], writes=[B_xbs[0]])
    p.emit("act", lambda e: e.activation(
        out=mf[:, MF_XBS:MF_XBS + DIN], in_=mf[:, MF_XBS:MF_XBS + DIN],
        func=Act.Silu), reads=[B_xbs[0]], writes=[B_xbs[0]])
    xt = 16
    def f_c0(e, xt=xt):
        return e.tensor_scalar(
            out=xbs(xt), in0=xbc_r[:, xt * SP4 + 1 + ck * CH:
                                   xt * SP4 + 1 + ck * CH + CH],
            scalar1=cw[:, (l * NXT + xt) * 4:(l * NXT + xt) * 4 + 1],
            scalar2=None, op0=Alu.mult)
    p.emit("dve", f_c0, reads=[B_xbr, B_cw], writes=[B_xbs[0]])
    for kk in range(1, 4):
        def f_cr(e, xt=xt, kk=kk):
            return e.scalar_tensor_tensor(
                out=xbs(xt),
                in0=xbc_r[:, xt * SP4 + 1 + kk + ck * CH:
                          xt * SP4 + 1 + kk + ck * CH + CH],
                scalar=cw[:, (l * NXT + xt) * 4 + kk:
                          (l * NXT + xt) * 4 + kk + 1],
                in1=xbs(xt), op0=Alu.mult, op1=Alu.add)
        p.emit("dve", f_cr, reads=[B_xbr, B_cw, B_xbs[0]], writes=[B_xbs[0]])
    def f_si(e, xt=xt):
        return e.activation(out=xbs(xt), in_=xbs(xt), func=Act.Silu,
                            bias=cbcol[:, l * NXT + xt:l * NXT + xt + 1])
    p.emit("act", f_si, reads=[B_xbs[0], B_cb], writes=[B_xbs[0]])

    p.emit("act", lambda e: e.copy(out=mb[:, MB_XHH:MB_XHH + DIN],
                                   in_=mf[:, MF_XBS:MF_XBS + DIN]),
           reads=[B_xbs[0]], writes=[B_xhh[0]])
    p.emit("dve", lambda e: e.scalar_tensor_tensor(
        out=mb[:, MB_XHL:MB_XHL + DIN], in0=mf[:, MF_XBS:MF_XBS + DIN],
        scalar=1.0, in1=mb[:, MB_XHH:MB_XHH + DIN],
        op0=Alu.mult, op1=Alu.subtract),
        reads=[B_xbs[0], B_xhh[0]], writes=[B_xhl[0]])
    p.emit("act", lambda e: e.copy(out=mb[0:64, MB_BPK:MB_BPK + 128],
                                   in_=t16(0, 64)),
           reads=[B_xbs[0]], writes=[B_bpk])
    p.emit("dve", lambda e: e.scalar_tensor_tensor(
        out=mb[0:64, MB_SCR:MB_SCR + 128], in0=t16(0, 64), scalar=1.0,
        in1=mb[0:64, MB_BPK:MB_BPK + 128], op0=Alu.mult, op1=Alu.subtract),
        reads=[B_xbs[0], B_bpk], writes=[B_wrk])
    p.emit("act", lambda e: e.dma_start(
        out=mb[64:128, MB_BPK:MB_BPK + 128],
        in_=mb[0:64, MB_SCR:MB_SCR + 128]),
        reads=[B_wrk], writes=[B_bpk], dma=1)
    p.emit("act", lambda e: e.copy(out=mb[64:128, MB_CPB:MB_CPB + 128],
                                   in_=t16(64, 128)),
           reads=[B_xbs[0]], writes=[B_cpb])
    p.emit("dve", lambda e: e.scalar_tensor_tensor(
        out=mb[64:128, MB_CPA:MB_CPA + 128], in0=t16(64, 128), scalar=1.0,
        in1=mb[64:128, MB_CPB:MB_CPB + 128], op0=Alu.mult, op1=Alu.subtract),
        reads=[B_xbs[0], B_cpb], writes=[B_cpa])
    p.emit("act", lambda e: e.dma_start(
        out=mb[0:64, MB_CPA:MB_CPA + 128],
        in_=mb[64:128, MB_CPB:MB_CPB + 128]),
        reads=[B_cpb], writes=[B_cpa], dma=1)
    p.emit("act", lambda e: e.dma_start(
        out=mb[0:64, MB_CPB:MB_CPB + 128],
        in_=mb[64:128, MB_CPA:MB_CPA + 128]),
        reads=[B_cpa], writes=[B_cpb], dma=1)

    def f_trh(e, par):
        return [e.dma_start_transpose(
            mb[:, MB_XTH + ht * 128:MB_XTH + (ht + 1) * 128],
            mb[:, MB_XHH + ht * 128:MB_XHH + (ht + 1) * 128])
            for ht in range(par, NHT, 2)]
    def f_trl(e, par):
        return [e.dma_start_transpose(
            mb[:, MB_XTL + ht * 128:MB_XTL + (ht + 1) * 128],
            mb[:, MB_XHL + ht * 128:MB_XHL + (ht + 1) * 128])
            for ht in range(par, NHT, 2)]
    p.emit("sp", lambda e: f_trh(e, 0), reads=[B_xhh[0]], writes=[B_xth[0]],
           dma=NHT // 2)
    p.emit("act", lambda e: f_trh(e, 1), reads=[B_xhh[0]], writes=[B_xth[0]],
           dma=NHT // 2)
    p.emit("sp", lambda e: f_trl(e, 0), reads=[B_xhl[0]], writes=[B_xtl[0]],
           dma=NHT // 2)
    p.emit("act", lambda e: f_trl(e, 1), reads=[B_xhl[0]], writes=[B_xtl[0]],
           dma=NHT // 2)
    p.emit("dve", lambda e: e.tensor_tensor(
        out=mf[:, MF_XTF:MF_XTF + DIN], in0=mb[:, MB_XTH:MB_XTH + DIN],
        in1=mb[:, MB_XTL:MB_XTL + DIN], op=Alu.add),
        reads=[B_xth[0], B_xtl[0]], writes=[B_xtf[0]])

    p.emit("pe", lambda e: e.transpose(
        ps[6][:, 128:192], t16(0, 64), ident[0:64, 0:64]),
        reads=[B_xbs[0], B_id], writes=[PB[6]])
    p.emit("dve", lambda e: e.tensor_copy(
        mf[:, MF_BCT:MF_BCT + 64], ps[6][:, 128:192]),
        reads=[PB[6]], writes=[B_bcT])
    p.emit("pe", lambda e, ck=ck: e.transpose(
        ps[6][:, 192:288], stk[:, ck * CH:(ck + 1) * CH],
        ident[0:96, 0:96]),
        reads=[B_stk, B_id], writes=[PB[6]])
    p.emit("dve", lambda e: e.tensor_copy(
        mf[:, MF_ADT:MF_ADT + 96], ps[6][:, 192:288]),
        reads=[PB[6]], writes=[B_adt[0]])

    p.emit("sp", lambda e, ck=ck: e.dma_start(
        out=aeb[:, 0:32],
        in_=dr_a[:, ck * CH + CH - 1].unsqueeze(0).broadcast_to([128, 32]),
        allow_slow_non_contiguous=True),
        reads=[B_dra], writes=[B_aeb], dma=1)
    p.emit("dve", lambda e: e.tensor_tensor(
        out=aeb[:, 32:64], in0=aeb[:, 0:32],
        in1=mf[:, MF_ADT:MF_ADT + 32], op=Alu.subtract),
        reads=[B_aeb, B_adt[0]], writes=[B_aeb])
    p.emit("dve", lambda e: e.tensor_tensor(
        out=aeb[:, 32:64], in0=aeb[:, 32:64],
        in1=mf[:, MF_ADT + 64:MF_ADT + 96], op=Alu.add),
        reads=[B_aeb, B_adt[0]], writes=[B_aeb])
    p.emit("act", lambda e: e.activation(out=wTt[:], in_=aeb[:, 32:64],
                                         func=Act.Exp),
           reads=[B_aeb], writes=[B_wT])
    p.emit("act", lambda e: e.activation(
        out=mf[:, MF_ADT + 96:MF_ADT + 128], in_=mf[:, MF_ADT:MF_ADT + 32],
        func=Act.Exp), reads=[B_adt[0]], writes=[B_adt[0]])
    p.emit("act", lambda e: e.activation(
        out=aeb[:, 0:32], in_=aeb[:, 0:32], func=Act.Exp),
        reads=[B_aeb], writes=[B_aeb])

    def f_st0(e):
        r = e.matmul(ps[7][:, 0:128], mb[:, MB_BPK:MB_BPK + 128],
                     mb[:, MB_CPA:MB_CPA + 128], start=True, stop=False,
                     skip_group_check=True)
        r = e.matmul(ps[7][:, 0:128], mb[:, MB_BPK:MB_BPK + 128],
                     mb[:, MB_CPB:MB_CPB + 128], start=False, stop=True,
                     skip_group_check=True)
        return r
    p.emit("pe", f_st0, reads=[B_bpk, B_cpa, B_cpb], writes=[PB[7]])
    p.emit("dve", lambda e: e.tensor_tensor(
        out=mf[:, MF_ST0:MF_ST0 + 128], in0=ps[7][:, 0:128], in1=mask01[:],
        op=Alu.mult), reads=[PB[7], B_mk], writes=[B_st0])

    for g4 in range(8):
        g8 = g4 // 2
        p.emit("sp", lambda e, g4=g4, ck=ck: e.dma_start(
            out=abg[:].rearrange("p (i n) -> p i n", n=128),
            in_=dr_a[4 * g4:4 * g4 + 4, ck * CH:(ck + 1) * CH]
            .unsqueeze(0).broadcast_to([128, 4, CH])),
            reads=[B_dra], writes=[B_abg[0]], dma=1)

        def f_ts4(e, g4=g4):
            r = None
            for i4 in range(4):
                h = 4 * g4 + i4
                r = e.tensor_scalar(
                    out=mf[:, MF_WRK + i4 * 128:MF_WRK + (i4 + 1) * 128],
                    in0=abg[:, i4 * 128:(i4 + 1) * 128],
                    scalar1=mf[:, MF_ADT + h:MF_ADT + h + 1], scalar2=0.0,
                    op0=Alu.subtract, op1=Alu.min)
            return r
        p.emit("dve", f_ts4, reads=[B_abg[0], B_adt[0]], writes=[B_wrk])
        p.emit("act", lambda e: e.activation(
            out=mf[:, MF_EXP:MF_EXP + 512], in_=mf[:, MF_WRK:MF_WRK + 512],
            func=Act.Exp), reads=[B_wrk], writes=[g("B_exp")])
        if g4 % 2 == 0:
            p.emit("dve", lambda e, g8=g8: e.tensor_tensor(
                out=mf[:, MF_BW:MF_BW + 512].rearrange(
                    "p (i n) -> p i n", n=64),
                in0=mf[:, MF_XTF + g8 * 512:MF_XTF + (g8 + 1) * 512]
                .rearrange("p (i n) -> p i n", n=64),
                in1=wTt[:, g8 * 8:(g8 + 1) * 8].unsqueeze(2)
                .broadcast_to([128, 8, 64]),
                op=Alu.mult),
                reads=[B_xtf[0], B_wT], writes=[B_bw])
            p.emit("pe", lambda e: e.matmul(
                ps[5][64:128, 0:512], mf[:, MF_BCT:MF_BCT + 64],
                mf[:, MF_BW:MF_BW + 512], start=True, stop=True,
                skip_group_check=True),
                reads=[B_bcT, B_bw], writes=[PB[5]])
        emit_head_group(E, sw, b, l, ck, g4)
        if g4 % 2 == 1:
            emit_group_tail(E, sw, b, l, ck, g8)

    emit_outproj_chunk(E, sw, b, l, ck)


def emit_head_group(E, sw, b, l, ck, g4):
    p = E["p"]
    g = lambda k: E[k]
    mb, mf = g("mb"), g("mf")
    ps, PB = g("ps"), g("PB")
    state, B_st = g("state"), g("B_st")
    s_ch, B_sch = g("s_ch"), g("B_sch")
    B_sth, B_stl, B_stf = g("B_sth"), g("B_stl"), g("B_stf")
    B_st0, B_adt, B_wrk = g("B_st0"), g("B_adt"), g("B_wrk")
    B_exp, B_bw = g("B_exp"), g("B_bw")
    B_xth, B_xtl, B_xtf, B_xbs = g("B_xth"), g("B_xtl"), g("B_xtf"), g("B_xbs")
    MB_XTH, MB_XTL = g("MB_XTH"), g("MB_XTL")
    MB_STH, MB_STL = 12800, 13312
    MF_STF, MF_ST0, MF_ADT, MF_BW = (
        g("MF_STF"), g("MF_ST0"), g("MF_ADT"), g("MF_BW"))
    MF_XTF, MF_XBS = g("MF_XTF"), g("MF_XBS")
    MF_EXP = 6080
    hs = [4 * g4 + i for i in range(4)]

    def f_st4(e):
        r = None
        for i4, h in enumerate(hs):
            r = e.scalar_tensor_tensor(
                out=mf[:, MF_STF + i4 * 128:MF_STF + (i4 + 1) * 128],
                in0=mf[:, MF_EXP + i4 * 128:MF_EXP + (i4 + 1) * 128],
                scalar=mf[:, MF_ADT + 32 + h:MF_ADT + 33 + h],
                in1=mf[:, MF_ST0:MF_ST0 + 128], op0=Alu.mult, op1=Alu.mult)
        return r
    p.emit("dve", f_st4, reads=[B_exp, B_adt[0], B_st0],
           writes=[B_stf[0]])

    def f_hi4(e):
        return e.copy(out=mb[:, MB_STH:MB_STH + 512],
                      in_=mf[:, MF_STF:MF_STF + 512])
    p.emit("act", f_hi4, reads=[B_stf[0]], writes=[B_sth[0]])

    def f_lo4(e):
        return e.scalar_tensor_tensor(
            out=mb[:, MB_STL:MB_STL + 512], in0=mf[:, MF_STF:MF_STF + 512],
            scalar=1.0, in1=mb[:, MB_STH:MB_STH + 512],
            op0=Alu.mult, op1=Alu.subtract)
    p.emit("dve", f_lo4, reads=[B_stf[0], B_sth[0]], writes=[B_stl[0]])

    def f_y(e, g4=g4):
        r = None
        for i4, h in enumerate(hs):
            cs = (h % 8) * 64
            sth = mb[:, MB_STH + i4 * 128:MB_STH + (i4 + 1) * 128]
            stl = mb[:, MB_STL + i4 * 128:MB_STL + (i4 + 1) * 128]
            lh = mb[:, MB_XTH + h * 64:MB_XTH + (h + 1) * 64]
            ll = mb[:, MB_XTL + h * 64:MB_XTL + (h + 1) * 64]
            r = e.matmul(ps[2][:, cs:cs + 64], sth, lh,
                         start=True, stop=False, skip_group_check=True)
            r = e.matmul(ps[2][:, cs:cs + 64], stl, lh,
                         start=False, stop=False, skip_group_check=True)
            r = e.matmul(ps[2][:, cs:cs + 64], sth, ll,
                         start=False, stop=True, skip_group_check=True)
        return r
    p.emit("pe", f_y, reads=[B_xth[0], B_xtl[0], B_sth[0], B_stl[0]],
           writes=[PB[2]])

    t16c = mf[64:128, MF_XBS + 16 * 128:MF_XBS + 17 * 128]

    def f_y2(e, l=l, g4=g4):
        cs = (g4 % 2) * 256
        return e.matmul(
            ps[3][:, cs:cs + 256], t16c,
            state[64:128, l * DIN + g4 * 256:l * DIN + (g4 + 1) * 256],
            start=True, stop=True, skip_group_check=True)
    p.emit("pe", f_y2, reads=[B_st[l], B_xbs[0]], writes=[PB[3]])


def emit_group_tail(E, sw, b, l, ck, g8):
    p = E["p"]
    g = lambda k: E[k]
    mb, mf = g("mb"), g("mf")
    ps, PB = g("ps"), g("PB")
    state, B_st = g("state"), g("B_st")
    aeb, B_aeb = g("aeb"), g("B_aeb")
    s_ch, B_sch = g("s_ch"), g("B_sch")
    yg, B_yg = g("yg"), g("B_yg")
    zt, B_zt = g("zt"), g("B_zt")
    wring, B_wr = g("wring"), g("B_wr")
    cin, B_cin = g("cin"), g("B_cin")
    uf_hi, uf_lo, B_ufh, B_ufl = g("uf_hi"), g("uf_lo"), g("B_ufh"), g("B_ufl")
    bvcol, B_bv = g("bvcol"), g("B_bv")
    drep2, B_dr2 = g("drep2"), g("B_dr2")
    B_xtf, B_adt = g("B_xtf"), g("B_adt")
    MF_XTF, MF_ADT, MF_BW = g("MF_XTF"), g("MF_ADT"), g("MF_BW")
    sweep = g("sweep")
    S = sweep

    p.emit("dve", lambda e, l=l, g8=g8: e.tensor_tensor(
        out=s_ch[64:128, :].rearrange("p (i n) -> p i n", n=64),
        in0=state[64:128, l * DIN + g8 * 512:l * DIN + (g8 + 1) * 512]
        .rearrange("p (i n) -> p i n", n=64),
        in1=aeb[64:128, g8 * 8:(g8 + 1) * 8].unsqueeze(2)
        .broadcast_to([64, 8, 64]),
        op=Alu.mult),
        reads=[B_st[l], B_aeb], writes=[B_sch])
    p.emit("dve", lambda e, l=l, g8=g8: e.tensor_tensor(
        out=state[64:128, l * DIN + g8 * 512:l * DIN + (g8 + 1) * 512],
        in0=s_ch[64:128, :], in1=ps[5][64:128, :], op=Alu.add),
        reads=[B_sch, PB[5]], writes=[B_st[l]])

    zb = g8 % 2
    for k2 in range(KD // 2):
        slot = E.setdefault("wslot", 0)
        E["wslot"] = 1 - slot
        p.emit("sp", lambda e, l=l, g8=g8, k2=k2, slot=slot: e.dma_start(
            out=wring[:, slot * 2048:slot * 2048 + 2048],
            in_=g("cin_z")[l, g8, k2]),
            reads=[B_cin], writes=[B_wr[slot]], dma=1)

        def f_zmm(e, k2=k2, slot=slot, zb=zb, ck=ck):
            r = None
            for kl in range(2):
                k = 2 * k2 + kl
                base = slot * 2048 + kl * 1024
                wh = wring[:, base:base + 512]
                wl = wring[:, base + 512:base + 1024]
                xh_ = uf_hi[:, k * S + ck * CH:k * S + (ck + 1) * CH]
                xl_ = uf_lo[:, k * S + ck * CH:k * S + (ck + 1) * CH]
                r = e.matmul(ps[zb][:, 0:512], xh_, wh,
                             start=(k == 0), stop=False,
                             skip_group_check=True)
                r = e.matmul(ps[zb][:, 0:512], xl_, wh, start=False,
                             stop=False, skip_group_check=True)
                r = e.matmul(ps[zb][:, 0:512], xh_, wl, start=False,
                             stop=(k == KD - 1), skip_group_check=True)
            return r
        p.emit("pe", f_zmm, reads=[B_wr[slot], B_ufh, B_ufl],
               writes=[PB[zb]])
    dec_t, B_dec = g("dec"), g("B_dec")
    p.emit("dve", lambda e, g8=g8, zb=zb: e.tensor_tensor(
        out=zt[:, 0:512], in0=ps[zb][:, 0:512],
        in1=dec_t[:, g8 * 512:(g8 + 1) * 512], op=Alu.add),
        reads=[PB[zb], B_dec], writes=[B_zt])
    p.emit("act", lambda e: e.activation(
        out=zt[:, 0:512], in_=zt[:, 0:512], func=Act.Silu),
        reads=[B_zt], writes=[B_zt])

    for kt in range(4):
        ht = g8 * 4 + kt
        ygs = yg[:, ht * 128:(ht + 1) * 128]
        ygs3 = ygs.rearrange("p (a b) -> p a b", b=64)
        eab = (mf[:, MF_ADT + 96 + 2 * ht:MF_ADT + 96 + 2 * ht + 2]
               .unsqueeze(2).broadcast_to([128, 2, 64]))
        dab = (drep2[:, l * NH + 2 * ht:l * NH + 2 * ht + 2]
               .unsqueeze(2).broadcast_to([128, 2, 64]))
        p.emit("dve", lambda e, kt=kt, ygs3=ygs3, eab=eab: e.tensor_tensor(
            out=ygs3, in0=ps[3][:, kt * 128:(kt + 1) * 128]
            .rearrange("p (a b) -> p a b", b=64), in1=eab, op=Alu.mult),
            reads=[PB[3], B_adt[0]], writes=[B_yg])
        p.emit("dve", lambda e, kt=kt, ygs=ygs: e.tensor_tensor(
            out=ygs, in0=ygs, in1=ps[2][:, kt * 128:(kt + 1) * 128],
            op=Alu.add),
            reads=[PB[2], B_yg], writes=[B_yg])
        p.emit("dve", lambda e, ht=ht, dab=dab: e.tensor_tensor(
            out=s_ch[:, 0:128].rearrange("p (a b) -> p a b", b=64),
            in0=mf[:, MF_XTF + ht * 128:MF_XTF + (ht + 1) * 128]
            .rearrange("p (a b) -> p a b", b=64),
            in1=dab, op=Alu.mult),
            reads=[B_xtf[0], B_dr2], writes=[B_sch])
        p.emit("pool", lambda e, ygs=ygs: e.tensor_tensor(
            out=ygs, in0=ygs, in1=s_ch[:, 0:128], op=Alu.add),
            reads=[B_sch, B_yg], writes=[B_yg])
        p.emit("dve", lambda e, kt=kt, ygs=ygs: e.tensor_tensor(
            out=ygs, in0=ygs, in1=zt[:, kt * 128:(kt + 1) * 128],
            op=Alu.mult),
            reads=[B_zt, B_yg], writes=[B_yg])


def emit_outproj_chunk(E, sw, b, l, ck):
    p = E["p"]
    g = lambda k: E[k]
    mb, mf = g("mb"), g("mf")
    ps, PB = g("ps"), g("PB")
    wring, B_wr = g("wring"), g("B_wr")
    cout_, B_cout = g("cout_"), g("B_cout")
    s_ch, B_sch = g("s_ch"), g("B_sch")
    h_t, B_ht = g("h_t"), g("B_ht")
    ident, B_id = g("ident"), g("B_id")
    yg, B_yg = g("yg"), g("B_yg")
    stat, B_stat = g("stat"), g("B_stat")
    cst, B_cst = g("cst"), g("B_cst")
    B_ygh, B_ygl = g("B_ygh"), g("B_ygl")
    B_ygfh, B_ygfl = g("B_xth"), g("B_xtl")
    MB_YGH, MB_YGL = g("MB_YGH"), g("MB_YGL")
    MB_XTH, MB_XTL = g("MB_XTH"), g("MB_XTL")

    B_xbs = g("B_xbs")
    MF_XBS = g("MF_XBS")
    p.emit("act", lambda e: e.activation(
        out=g("mf")[:, MF_XBS:MF_XBS + DIN], in_=yg[:, 0:DIN],
        func=Act.Square, accum_out=stat[:, 520:521]),
        reads=[B_yg], writes=[B_xbs[0], B_stat])
    p.emit("act", lambda e: e.activation(
        out=stat[:, 521:522], in_=stat[:, 520:521], func=Act.Sqrt,
        scale=float(1.0 / DIN), bias=cst[:, 3:4]),
        reads=[B_stat, B_cst], writes=[B_stat])
    p.emit("dve", lambda e: e.reciprocal(
        out=stat[:, 522:523], in_=stat[:, 521:522]),
        reads=[B_stat], writes=[B_stat])
    p.emit("dve", lambda e: e.tensor_scalar(
        out=yg[:, 0:DIN], in0=yg[:, 0:DIN], scalar1=stat[:, 522:523],
        scalar2=None, op0=Alu.mult),
        reads=[B_stat, B_yg], writes=[B_yg])
    p.emit("act", lambda e: e.copy(out=mb[:, MB_YGH:MB_YGH + DIN],
                                   in_=yg[:, 0:DIN]),
           reads=[B_yg], writes=[B_ygh])
    p.emit("dve", lambda e: e.scalar_tensor_tensor(
        out=mb[:, MB_YGL:MB_YGL + DIN], in0=yg[:, 0:DIN], scalar=1.0,
        in1=mb[:, MB_YGH:MB_YGH + DIN], op0=Alu.mult, op1=Alu.subtract),
        reads=[B_yg, B_ygh], writes=[B_ygl])
    def f_tgh(e, par):
        return [e.dma_start_transpose(
            mb[:, MB_XTH + ht * 128:MB_XTH + (ht + 1) * 128],
            mb[:, MB_YGH + ht * 128:MB_YGH + (ht + 1) * 128])
            for ht in range(par, NHT, 2)]
    def f_tgl(e, par):
        return [e.dma_start_transpose(
            mb[:, MB_XTL + ht * 128:MB_XTL + (ht + 1) * 128],
            mb[:, MB_YGL + ht * 128:MB_YGL + (ht + 1) * 128])
            for ht in range(par, NHT, 2)]
    p.emit("sp", lambda e: f_tgh(e, 0), reads=[B_ygh], writes=[B_ygfh[0]],
           dma=NHT // 2)
    p.emit("act", lambda e: f_tgh(e, 1), reads=[B_ygh], writes=[B_ygfh[0]],
           dma=NHT // 2)
    p.emit("sp", lambda e: f_tgl(e, 0), reads=[B_ygl], writes=[B_ygfl[0]],
           dma=NHT // 2)
    p.emit("act", lambda e: f_tgl(e, 1), reads=[B_ygl], writes=[B_ygfl[0]],
           dma=NHT // 2)

    for slab in range(2):
        pb = slab
        for k2 in range(KDIN // 2):
            slot = E.setdefault("wslot", 0)
            E["wslot"] = 1 - slot
            p.emit("sp", lambda e, l=l, slab=slab, k2=k2, slot=slot:
                   e.dma_start(
                       out=wring[:, slot * 2048:slot * 2048 + 2048],
                       in_=g("cout_z")[l, slab, k2]),
                   reads=[B_cout], writes=[B_wr[slot]], dma=1)

            def f_mm(e, k2=k2, slot=slot, pb=pb):
                r = None
                for kl in range(2):
                    k = 2 * k2 + kl
                    base = slot * 2048 + kl * 1024
                    wh = wring[:, base:base + 512]
                    wl = wring[:, base + 512:base + 1024]
                    yh = mb[:, MB_XTH + k * 128:MB_XTH + (k + 1) * 128]
                    yl = mb[:, MB_XTL + k * 128:MB_XTL + (k + 1) * 128]
                    r = e.matmul(ps[pb][:, 0:512], yh, wh,
                                 start=(k == 0), stop=False,
                                 skip_group_check=True)
                    r = e.matmul(ps[pb][:, 0:512], yl, wh, start=False,
                                 stop=False, skip_group_check=True)
                    r = e.matmul(ps[pb][:, 0:512], yh, wl, start=False,
                                 stop=(k == KDIN - 1),
                                 skip_group_check=True)
                return r
            p.emit("pe", f_mm, reads=[B_wr[slot], B_ygfh[0], B_ygfl[0]],
                   writes=[PB[pb]])

        def f_add(e, slab=slab, pb=pb, ck=ck):
            return e.scalar_tensor_tensor(
                out=h_t[:, ck * D + slab * 512:ck * D + (slab + 1) * 512],
                in0=ps[pb][:, 0:512], scalar=1.0,
                in1=h_t[:, ck * D + slab * 512:ck * D + (slab + 1) * 512],
                op0=Alu.mult, op1=Alu.add)
        p.emit("dve", f_add, reads=[PB[pb], B_ht], writes=[B_ht])


def build_full(n_sweeps=8, sweep=512):
    E = build_program(n_sweeps, sweep)
    emit_all(E)
    E["nc"].finalize()
    return E["nc"]


def kernel(**inputs):
    xs = np.ascontiguousarray(np.asarray(inputs["L0_attn_x"], dtype=np.float32))
    in_t = np.ascontiguousarray(
        np.transpose(np.asarray(inputs["L0_attn_in_proj_w"]), (0, 2, 1)))
    out_t = np.ascontiguousarray(
        np.transpose(np.asarray(inputs["L0_attn_out_proj_w"]), (0, 2, 1)))
    common = {
        "w_in_t": in_t,
        "w_out_t": out_t,
        "conv_w": np.ascontiguousarray(inputs["L0_attn_conv_w"]),
        "conv_b": np.ascontiguousarray(inputs["L0_attn_conv_b"]),
        "dt_bias": np.ascontiguousarray(inputs["L0_attn_dt_bias"]),
        "a_log": np.ascontiguousarray(inputs["L0_attn_A_log"]),
        "d_vec": np.ascontiguousarray(inputs["L0_attn_D"]),
        "rms_w": np.ascontiguousarray(inputs["L0_attn_rms_w"]),
        "ln_w": np.ascontiguousarray(inputs["L0_attn_ln_w"]),
        "ln_b": np.ascontiguousarray(inputs["L0_attn_ln_b"]),
        "w_fc": np.ascontiguousarray(inputs["L0_ffn_w_fc"]),
        "b_fc": np.ascontiguousarray(inputs["L0_ffn_b_fc"]),
        "w_proj": np.ascontiguousarray(inputs["L0_ffn_w_proj"]),
        "b_proj": np.ascontiguousarray(inputs["L0_ffn_b_proj"]),
    }
    in_maps = []
    for c in range(N_CORES):
        m = dict(common)
        m["x"] = np.ascontiguousarray(xs[2 * c:2 * c + 2].reshape(2 * L_FULL, D))
        in_maps.append(m)
    nc = build_full()
    res = run_bass_kernel_spmd(nc, in_maps, core_ids=list(range(N_CORES)))
    out = np.concatenate(
        [np.asarray(r["y"]).reshape(2, L_FULL, D) for r in res.results], axis=0)
    return out


# version 05
# speedup vs baseline: 1.1386x; 1.1386x over previous
import os as _os
import numpy as np
import ml_dtypes

_DBG_NL = int(_os.environ.get("KDBG_NL", "2"))
_DBG_FFN = int(_os.environ.get("KDBG_FFN", "1"))

import concourse.bass as bass
import concourse.mybir as mybir
from concourse.bass_utils import run_bass_kernel_spmd

F32 = mybir.dt.float32
BF16 = mybir.dt.bfloat16
Act = mybir.ActivationFunctionType
Alu = mybir.AluOpType
AX = mybir.AxisListType

D = 1024
DIN = 2048
DST = 64
NH = 32
HD = 64
CONVDIM = DIN + 2 * DST
DPROJ = 2 * DIN + 2 * DST + NH
HID = 4096
BK = 32
EPS = 1e-5
LN2 = float(np.log(2.0))
N_CORES = 8
B_FULL, L_FULL = 16, 2048
CH = 128

KD = D // 128
NDC = (DPROJ + 127) // 128
KDIN = DIN // 128
NOC = D // 128
NXT = (CONVDIM + 127) // 128
NHT = DIN // 128



class Step:
    __slots__ = ("fn", "waits", "inc")

    def __init__(self, fn, waits, inc):
        self.fn, self.waits, self.inc = fn, waits, inc


class Buf:
    __slots__ = ("writer", "readers", "name")

    def __init__(self, name=""):
        self.writer = None
        self.readers = []
        self.name = name


class Prog:
    ENGS = ("pe", "dve", "act", "pool", "sp")

    def __init__(self, nc):
        self.nc = nc
        self.steps = {e: [] for e in self.ENGS}
        self.count = {e: 0 for e in self.ENGS}
        self.sems = {}
        self._guards = []
        self.bufsem = {}
        self.nbufsem = 0
        self.waited = {e: {} for e in self.ENGS}

    def sbuf(self, name, shape, dtype):
        g = self.nc.sbuf_tensor(name, list(shape), dtype)
        t = g.__enter__()
        self._guards.append(g)
        return t

    def psum(self, name, shape, dtype=F32):
        g = self.nc.psum_tensor(name, list(shape), dtype)
        t = g.__enter__()
        self._guards.append(g)
        return t

    def emit(self, eng, fn, reads=(), writes=(), dma=0):
        waits = {}
        wd = self.waited[eng]

        def add(dep):
            if dep is None:
                return
            de, dv = dep
            if wd.get(de, 0) >= dv:
                return
            if waits.get(de, 0) < dv:
                waits[de] = dv

        for b in reads:
            add(b.writer)
        for b in writes:
            add(b.writer)
            for r in b.readers:
                add(r)
        for de, dv in waits.items():
            wd[de] = dv
        if dma:
            assert len(writes) == 1, "dma step must write exactly one Buf"
            bkey = id(writes[0])
            sname = self.bufsem.get(bkey)
            if sname is None:
                sname = "D%d" % self.nbufsem
                self.nbufsem += 1
                self.bufsem[bkey] = sname
                self.count[sname] = 0
                self.steps.setdefault(sname, None)
            inc = 16 * dma
            self.count[sname] += inc
            tag = (sname, self.count[sname])
            for b in reads:
                b.readers.append(tag)
            writes[0].writer = tag
            writes[0].readers = []
            self.steps[eng].append(Step(fn, waits, (sname, inc)))
            return tag
        self.count[eng] += 1
        tag = (eng, self.count[eng])
        for b in reads:
            b.readers.append(tag)
        for b in writes:
            b.writer = tag
            b.readers = []
        self.steps[eng].append(Step(fn, waits, 1))
        return tag

    def barrier(self):
        snap = dict(self.count)
        for e in self.ENGS:
            waits = {d: v for d, v in snap.items() if v > 0}
            self.count[e] += 1
            self.steps[e].append(Step(lambda en: en.nop(), waits, 1))
            self.waited[e] = dict(waits)

    def build(self):
        nc = self.nc
        for e in list(self.count):
            g = nc.semaphore("S" + e)
            self.sems[e] = g.__enter__()
            self._guards.append(g)
        blk_g = nc.Block()
        blk = blk_g.__enter__()

        def runner(name):
            def run(e):
                cnt = 0
                for st in self.steps[name]:
                    if not st.waits:
                        e.wait_ge(self.sems[name], cnt)
                    for de, dv in sorted(st.waits.items()):
                        e.wait_ge(self.sems[de], dv)
                    if not isinstance(st.inc, tuple):
                        cnt += st.inc
                    last = st.fn(e)
                    if isinstance(st.inc, tuple):
                        sname, inc = st.inc
                        insts = last if isinstance(last, (list, tuple)) else [last]
                        assert len(insts) * 16 == inc
                        for i in insts:
                            i.then_inc(self.sems[sname], 16)
                    else:
                        if isinstance(last, (list, tuple)):
                            last = last[-1]
                        last.then_inc(self.sems[name], 1)
            return run

        blk.tensor(runner("pe"))
        blk.vector(runner("dve"))
        blk.scalar(runner("act"))
        blk.gpsimd(runner("pool"))
        blk.sync(runner("sp"))
        blk_g.__exit__(None, None, None)



def quant_ladder(p, eng, buf_io, src_ap, dst_ap, inv_ap, sc_ap, t_ap, b_ap,
                 nlv=7, span=32.0):
    reads, writes = buf_io
    lv = [2 ** (nlv - 1 - i) for i in range(nlv)]

    def f_mul(e):
        return e.tensor_tensor(out=t_ap, in0=src_ap, in1=inv_ap, op=Alu.mult)
    p.emit(eng, f_mul, reads=reads, writes=[writes[0]])

    def f_add(e):
        return e.tensor_scalar(out=t_ap, in0=t_ap, scalar1=span + 0.5,
                               scalar2=None, op0=Alu.add)
    p.emit(eng, f_add, reads=[writes[0]], writes=[writes[0]])
    for c in lv:
        def f_ge(e, c=c):
            return e.tensor_scalar(out=b_ap, in0=t_ap, scalar1=float(c),
                                   scalar2=None, op0=Alu.is_ge)
        p.emit(eng, f_ge, reads=[writes[0]], writes=[writes[1]])

        def f_up(e, c=c):
            return e.scalar_tensor_tensor(out=t_ap, in0=b_ap, scalar=float(-c),
                                          in1=t_ap, op0=Alu.mult, op1=Alu.add)
        p.emit(eng, f_up, reads=[writes[1], writes[0]], writes=[writes[0]])

    def f_u2(e):
        return e.tensor_tensor(out=b_ap, in0=src_ap, in1=inv_ap, op=Alu.mult)
    p.emit(eng, f_u2, reads=reads + [writes[0]], writes=[writes[1]])

    def f_d(e):
        return e.scalar_tensor_tensor(out=t_ap, in0=b_ap, scalar=0.5,
                                      in1=t_ap, op0=Alu.add, op1=Alu.subtract)
    p.emit(eng, f_d, reads=[writes[1], writes[0]], writes=[writes[0]])

    def f_q(e):
        return e.tensor_tensor(out=dst_ap, in0=t_ap, in1=sc_ap, op=Alu.mult)
    p.emit(eng, f_q, reads=[writes[0]], writes=[writes[2]])


def exp_ladder(p, eng, rw, lg_ap, e_out_ap, t_ap, b_ap, nlv=9, span=256.0):
    reads, wT, wB, wE = rw
    def f0(e):
        return e.tensor_scalar(out=t_ap, in0=lg_ap, scalar1=span, scalar2=None,
                               op0=Alu.add)
    p.emit(eng, f0, reads=reads, writes=[wT])
    for i in range(nlv):
        c = 2 ** (nlv - 1 - i)
        def f_ge(e, c=c):
            return e.tensor_scalar(out=b_ap, in0=t_ap, scalar1=float(c),
                                   scalar2=None, op0=Alu.is_ge)
        p.emit(eng, f_ge, reads=[wT], writes=[wB])

        def f_up(e, c=c):
            return e.scalar_tensor_tensor(out=t_ap, in0=b_ap, scalar=float(-c),
                                          in1=t_ap, op0=Alu.mult, op1=Alu.add)
        p.emit(eng, f_up, reads=[wB, wT], writes=[wT])
    def f_e(e):
        return e.scalar_tensor_tensor(out=e_out_ap, in0=lg_ap, scalar=1.0,
                                      in1=t_ap, op0=Alu.mult, op1=Alu.subtract)
    p.emit(eng, f_e, reads=reads + [wT], writes=[wE])



def build_program(n_sweeps=8, sweep=512):
    NT = sweep // 128
    NCH = sweep // CH
    TOK = n_sweeps * sweep
    HSL = HID // 512
    PSL = D // 512

    nc = bass.Bass("TRN2", target_bir_lowering=False)
    p = Prog(nc)
    dt_ = nc.dram_tensor

    x_in = dt_("x", [TOK, D], F32, kind="ExternalInput").ap()
    w_in_t = dt_("w_in_t", [2, D, DPROJ], F32, kind="ExternalInput").ap()
    w_out_t = dt_("w_out_t", [2, DIN, D], F32, kind="ExternalInput").ap()
    conv_w = dt_("conv_w", [2, CONVDIM, 4], F32, kind="ExternalInput").ap()
    conv_b = dt_("conv_b", [2, CONVDIM], F32, kind="ExternalInput").ap()
    dt_bias = dt_("dt_bias", [2, NH], F32, kind="ExternalInput").ap()
    a_log = dt_("a_log", [2, NH], F32, kind="ExternalInput").ap()
    d_vec = dt_("d_vec", [2, NH], F32, kind="ExternalInput").ap()
    rms_w = dt_("rms_w", [2, DIN], F32, kind="ExternalInput").ap()
    ln_w = dt_("ln_w", [2, D], F32, kind="ExternalInput").ap()
    ln_b = dt_("ln_b", [2, D], F32, kind="ExternalInput").ap()
    w_fc = dt_("w_fc", [HID, D], F32, kind="ExternalInput").ap()
    b_fc = dt_("b_fc", [HID], F32, kind="ExternalInput").ap()
    w_proj = dt_("w_proj", [D, HID], F32, kind="ExternalInput").ap()
    b_proj = dt_("b_proj", [D], F32, kind="ExternalInput").ap()
    y_out = dt_("y", [TOK, D], F32, kind="ExternalOutput").ap()

    cin = dt_("cin", [2, NDC, KD, 2, 128, 128], BF16).ap()
    cin_z = dt_("cin_z", [2, 4, KD // 2, 128, 2048], BF16).ap()
    cout_z = dt_("cout_z", [2, 2, KDIN // 2, 128, 2048], BF16).ap()
    cout_ = dt_("cout", [2, NOC, KDIN, 2, 128, 128], BF16).ap()
    cfc = dt_("cfc", [KD, HSL, 128, 512], BF16).ap()
    cproj = dt_("cproj", [HID // 128, PSL, 128, 512], BF16).ap()
    cbv = dt_("cbv", [2, NDC * 128], F32).ap()
    dr_a = dt_("dr_a", [NH, sweep], F32).ap()
    dr_ea = dt_("dr_ea", [NH, sweep], F32).ap()
    dr_rs = dt_("dr_rs", [sweep], F32).ap()
    B_cin, B_cout, B_cfc, B_cproj, B_cbv = (Buf(), Buf(), Buf(), Buf(), Buf())
    B_dra, B_drea, B_drrs, B_y = Buf(), Buf(), Buf(), Buf()

    sb = p.sbuf
    h_t = sb("h_t", [128, NT * D], F32); B_ht = Buf()
    state = sb("state", [128, 2 * DIN], F32); B_st = [Buf(), Buf()]
    stk = sb("stk", [96, sweep], F32); B_stk = Buf()
    dts = sb("dts", [32, 3 * sweep], F32)
    B_dt0 = Buf(); B_dt1 = Buf(); B_dt2 = Buf()
    ident = sb("ident", [128, 128], F32); B_id = Buf()
    mask01 = sb("mask01", [128, CH], F32); B_mk = Buf()
    imask = sb("imask", [128, CH], mybir.dt.int32); B_imk = Buf()
    ccol = sb("ccol", [128, 16], F32); B_cc = Buf()
    drep = sb("drep", [128, 2 * NHT], F32); B_dr = Buf()
    cbcol = sb("cbcol", [128, 2 * NXT], F32); B_cb = Buf()
    bvcol = sb("bvcol", [128, 2 * NDC], F32); B_bv = Buf()
    convc = sb("convc", [128, 2 * 2 * NXT * 4], F32)
    B_cvc = [[Buf(), Buf()], [Buf(), Buf()]]
    cw = sb("cw", [128, 2 * NXT * 4], F32); B_cw = Buf()
    B_exp = Buf("exp")
    misc = sb("misc", [128, 1024], F32); B_misc = Buf()
    B_bt = Buf()
    stat = sb("stat", [128, 768], F32); B_stat = Buf()
    aeb = sb("aeb", [128, 64], F32); B_aeb = Buf()
    wTt = sb("wTt", [128, 32], F32); B_wT = Buf()
    abg = sb("abg", [128, 512], F32); B_abg = [Buf()]
    uf_hi = sb("uf_hi", [128, KD * sweep], BF16); B_ufh = Buf()
    uf_lo = sb("uf_lo", [128, KD * sweep], BF16); B_ufl = Buf()
    if KD * sweep >= HID:
        hqs = uf_lo[:, 0:HID]
    else:
        hqs = sb("hqs", [128, HID], BF16)
    wring = sb("wring", [128, 4096], BF16)
    B_wr = [Buf(), Buf()]
    wf = wring[:, 0:2048]
    B_wf = [Buf() for _ in range(4)]
    B_hqs = Buf()
    cst = sb("cst", [128, 4], F32)
    B_cst = Buf()
    drep2 = sb("drep2", [128, 2 * NH], F32)
    B_dr2 = Buf()
    mb = sb("mb", [128, 16384], BF16); B_mb_hqT = Buf()
    uh = mb[:, 0:NT * D]; B_uh = Buf()
    ul = mb[:, NT * D:2 * NT * D]; B_ul = Buf()
    MB_XHH, MB_XHL = 0, 2048
    MB_XTH, MB_XTL = 4096, 6144
    MB_YGH, MB_YGL = 8192, 10240
    MB_BPK, MB_CPA, MB_CPB = 12288, 12416, 12544
    MB_STH, MB_STL = 12672, 13184
    B_xhh = [Buf(), Buf()]; B_xhl = [Buf(), Buf()]
    B_xth = [Buf(), Buf()]; B_xtl = [Buf(), Buf()]
    B_ygh = Buf(); B_ygl = Buf()
    B_bpk = Buf(); B_cpa = Buf(); B_cpb = Buf()
    B_sth = [Buf() for _ in range(4)]; B_stl = [Buf() for _ in range(4)]
    mf = sb("mf", [128, 8192], F32)
    MF_XBS, MF_XTF, MF_BW = 0, 2176, 4224
    MF_STF, MF_ST0, MF_BCT, MF_ADT, MF_WRK = 4736, 5248, 5376, 5440, 5568
    B_xbs = [Buf(), Buf()]; B_xtf = [Buf(), Buf()]; B_ea = Buf(); B_bw = Buf()
    B_stf = [Buf() for _ in range(4)]; B_st0 = Buf(); B_bcT = Buf()
    B_adt = [Buf(), Buf()]; B_wrk = Buf()
    xbc_r = sb("xbc_r", [128, NXT * (sweep + 4)], F32); B_xbr = Buf()
    dec = sb("dec", [128, DIN], F32); B_dec = Buf()
    s_ch = sb("s_ch", [128, 512], F32); B_sch = Buf()
    yg = sb("yg", [128, NHT * CH], F32); B_yg = Buf()
    zt = sb("zt", [128, 512], F32); B_zt = Buf()
    btile = zt
    sg = sb("sg", [128, 1024], F32); B_sg = Buf()
    xr = sb("xr", [128, 1024], F32); B_xr = Buf()

    ps = [p.psum(f"ps{i}", [128, 512]) for i in range(8)]
    PB = [Buf(f"ps{i}") for i in range(8)]

    E = dict(locals())
    E["p"] = p
    E["n_sweeps"] = n_sweeps
    E["sweep"] = sweep
    E["NT"] = NT
    E["NCH"] = NCH
    E["HSL"] = HSL
    E["PSL"] = PSL
    return E


def emit_all(E):
    emit_consts(E)
    emit_transform(E)
    E["p"].barrier()
    n_sweeps = E["n_sweeps"]
    for sw in range(n_sweeps):
        b = 0 if sw < n_sweeps // 2 else 1
        emit_sweep(E, sw, b)
    p = E["p"]
    p.emit("sp", lambda e: e.nop(), reads=[E["B_y"]])
    p.build()


def emit_consts(E):
    p = E["p"]
    g = lambda k: E[k]
    imask, mask01, ident = g("imask"), g("mask01"), g("ident")
    B_imk, B_mk, B_id = g("B_imk"), g("B_mk"), g("B_id")
    ccol, B_cc = g("ccol"), g("B_cc")
    drep, B_dr = g("drep"), g("B_dr")
    cbcol, B_cb = g("cbcol"), g("B_cb")
    misc, B_misc = g("misc"), g("B_misc")
    convc = g("convc")
    state, B_st = g("state"), g("B_st")
    stat, B_stat = g("stat"), g("B_stat")
    a_log, dt_bias, d_vec, conv_b, b_proj = (
        g("a_log"), g("dt_bias"), g("d_vec"), g("conv_b"), g("b_proj"))

    cst, B_cst = g("cst"), g("B_cst")
    p.emit("dve", lambda e: e.memset(cst[:, 0:1], 1.0), writes=[B_cst])
    p.emit("dve", lambda e: e.memset(cst[:, 1:2], float(3.0 * LN2)),
           writes=[B_cst])
    p.emit("dve", lambda e: e.memset(cst[:, 2:3], float(-3.0 * LN2)),
           writes=[B_cst])
    p.emit("dve", lambda e: e.memset(cst[:, 3:4], float(EPS)), writes=[B_cst])
    p.emit("pool", lambda e: e.iota(
        imask[:], pattern=[[1, CH]], base=0, channel_multiplier=-1),
        writes=[B_imk])
    p.emit("dve", lambda e: e.tensor_scalar(
        out=mask01[:], in0=imask[:], scalar1=0.0, scalar2=None, op0=Alu.is_ge),
        reads=[B_imk], writes=[B_mk])
    p.emit("dve", lambda e: e.tensor_scalar(
        out=ident[:], in0=imask[:], scalar1=0.0, scalar2=None, op0=Alu.is_equal),
        reads=[B_imk], writes=[B_id])
    p.emit("dve", lambda e: e.memset(state[:], 0.0), writes=[B_st[0], B_st[1]])
    p.emit("dve", lambda e: e.memset(convc[:], 0.0),
           writes=[b for lb in E["B_cvc"] for b in lb])
    for l in range(2):
        p.emit("sp", lambda e, l=l: [
            e.dma_start(out=ccol[0:NH, 4 + l:5 + l],
                        in_=a_log[l, :].unsqueeze(1)),
            e.dma_start(out=ccol[0:NH, 6 + l:7 + l],
                        in_=dt_bias[l, :].unsqueeze(1)),
        ], writes=[E["B_cc"]], dma=2)
        p.emit("act", lambda e, l=l: e.activation(
            out=ccol[0:NH, 2 + l:3 + l], in_=ccol[0:NH, 4 + l:5 + l],
            func=Act.Exp), reads=[B_cc], writes=[B_cc])
        p.emit("dve", lambda e, l=l: e.tensor_scalar(
            out=ccol[0:NH, l:l + 1], in0=ccol[0:NH, 2 + l:3 + l],
            scalar1=-1.0, scalar2=None, op0=Alu.mult),
            reads=[B_cc], writes=[B_cc])
        p.emit("sp", lambda e, l=l: [
            e.dma_start(
                out=drep[0:64, l * NHT:(l + 1) * NHT],
                in_=d_vec[l, :].rearrange("(t h) -> h t", h=2)[0]
                .unsqueeze(0).broadcast_to([64, NHT]),
                allow_slow_non_contiguous=True),
            e.dma_start(
                out=drep[64:128, l * NHT:(l + 1) * NHT],
                in_=d_vec[l, :].rearrange("(t h) -> h t", h=2)[1]
                .unsqueeze(0).broadcast_to([64, NHT]),
                allow_slow_non_contiguous=True)],
            writes=[B_dr], dma=2)
        p.emit("sp", lambda e, l=l: e.dma_start(
            out=g("drep2")[:, l * NH:(l + 1) * NH],
            in_=d_vec[l, :].unsqueeze(0).broadcast_to([128, NH])),
            writes=[g("B_dr2")], dma=1)
        p.emit("sp", lambda e, l=l: e.dma_start(
            out=cbcol[:, l * NXT:(l + 1) * NXT],
            in_=conv_b[l, :].rearrange("(a b) -> b a", b=128),
            allow_slow_non_contiguous=True),
            writes=[B_cb], dma=1)
        p.emit("sp", lambda e, l=l: e.dma_start(
            out=g("cw")[:, l * NXT * 4:(l + 1) * NXT * 4]
            .rearrange("p (t k) -> p t k", k=4),
            in_=g("conv_w")[l].rearrange("(t p) k -> p t k", p=128)),
            writes=[g("B_cw")], dma=1)
    p.emit("sp", lambda e: e.dma_start(
        out=misc[:, 0:D],
        in_=b_proj[:].unsqueeze(0).broadcast_to([128, D])),
        writes=[B_misc], dma=1)


def _wsplit_store(p, E, src_f32_ap, ncols, scol_ap, hi_ap, lo_ap,
                  rbufs, wbufs):
    mf, B = E["mf"], None

    def f_sc(e):
        return e.tensor_scalar(out=src_f32_ap, in0=src_f32_ap,
                               scalar1=scol_ap, scalar2=None, op0=Alu.mult)
    p.emit("dve", f_sc, reads=rbufs, writes=[wbufs[0]])

    def f_hi(e):
        return e.copy(out=hi_ap, in_=src_f32_ap)
    p.emit("act", f_hi, reads=[wbufs[0]], writes=[wbufs[1]])

    def f_lo(e):
        return e.scalar_tensor_tensor(out=lo_ap, in0=src_f32_ap, scalar=1.0,
                                      in1=hi_ap, op0=Alu.mult, op1=Alu.subtract)
    p.emit("dve", f_lo, reads=[wbufs[0], wbufs[1]], writes=[wbufs[2]])


def emit_transform(E):
    p = E["p"]
    g = lambda k: E[k]
    nc = g("nc")
    mf, mb, xbc_r, stat = g("mf"), g("mb"), g("xbc_r"), g("stat")
    B_wrk, B_xbr, B_stat = g("B_wrk"), g("B_xbr"), g("B_stat")
    B_mbs = Buf("mb_scr")
    ps, PB = g("ps"), g("PB")
    w_in_t, ln_w, ln_b = g("w_in_t"), g("ln_w"), g("ln_b")
    w_out_t, rms_w = g("w_out_t"), g("rms_w")
    w_fc, w_proj = g("w_fc"), g("w_proj")
    cin, cout_, cfc, cproj, cbv = g("cin"), g("cout_"), g("cfc"), g("cproj"), g("cbv")
    cin_z, cout_z = g("cin_z"), g("cout_z")
    B_cin, B_cout, B_cfc, B_cproj, B_cbv = (
        g("B_cin"), g("B_cout"), g("B_cfc"), g("B_cproj"), g("B_cbv"))
    bvcol, B_bv = g("bvcol"), g("B_bv")
    ccol, B_cc = g("ccol"), g("B_cc")
    s_ch, B_sch = g("s_ch"), g("B_sch")
    dec, B_dec = g("dec"), g("B_dec")

    p.emit("dve", lambda e: e.memset(mb[:, 4256:4352], 0.0), writes=[B_mbs])
    p.emit("dve", lambda e: e.memset(mb[:, 4352 + 4256:4352 + 4352], 0.0),
           writes=[B_mbs])

    for l in range(2):
        for k in range(KD):
            p.emit("sp", lambda e, l=l, k=k: e.dma_start(
                out=mf[:, 0:DPROJ], in_=w_in_t[l, k * 128:(k + 1) * 128, :]),
                writes=[B_wrk], dma=1)
            p.emit("sp", lambda e, l=l, k=k: [
                e.dma_start(out=stat[:, 0:1],
                            in_=ln_w[l, k * 128:(k + 1) * 128].unsqueeze(1)),
                e.dma_start(out=stat[:, 1:2],
                            in_=ln_b[l, k * 128:(k + 1) * 128].unsqueeze(1))],
                writes=[B_stat], dma=2)

            def f_bv(e, l=l, k=k):
                r = None
                for nt in range(8):
                    r = e.matmul(ps[nt][0:1, :], stat[:, 1:2],
                                 mf[:, nt * 512:(nt + 1) * 512],
                                 start=(k == 0), stop=(k == KD - 1))
                return r
            p.emit("pe", f_bv, reads=[B_wrk, B_stat],
                   writes=[PB[nt] for nt in range(8)])

            _wsplit_store(p, E, mf[:, 0:DPROJ], DPROJ, stat[:, 0:1],
                          mb[:, 0:DPROJ], mb[:, 4352:4352 + DPROJ],
                          [B_stat], [B_wrk, B_mbs, B_mbs])

            def f_st(e, l=l, k=k):
                ops = []
                for h in range(2):
                    hoff = 4352 * h
                    a0 = (k % 2) * 2 + h
                    ops.append(e.dma_start(
                        out=cin_z[l, :, k // 2, :, a0 * 512:(a0 + 1) * 512]
                        .rearrange("g p c -> p g c"),
                        in_=mb[:, hoff:hoff + 2048]
                        .rearrange("p (g c) -> p g c", c=512)))
                    ops.append(e.dma_start(
                        out=cin[l, 16:NDC, k, h].rearrange("c p n -> p c n"),
                        in_=mb[:, hoff + 2048:hoff + 4352]
                        .rearrange("p (c n) -> p c n", n=128)))
                return ops
            p.emit("sp", f_st, reads=[B_mbs], writes=[B_cin], dma=4)
        for nt in range(8):
            def f_ev(e, nt=nt):
                return e.tensor_copy(s_ch[0:1, :], ps[nt][0:1, :])
            p.emit("dve", f_ev, reads=[PB[nt]], writes=[B_sch])

            def f_dst(e, l=l, nt=nt):
                return e.dma_start(out=cbv[l, nt * 512:(nt + 1) * 512]
                                   .unsqueeze(0), in_=s_ch[0:1, :])
            p.emit("act", f_dst, reads=[B_sch], writes=[B_cbv], dma=1)
        for k in range(KD):
            p.emit("sp", lambda e, l=l, k=k: e.dma_start(
                out=dec[:, 0:160],
                in_=w_in_t[l, k * 128:(k + 1) * 128, 4096:4256]),
                writes=[B_dec], dma=1)
            p.emit("sp", lambda e, l=l, k=k: e.dma_start(
                out=stat[:, 1:2],
                in_=ln_b[l, k * 128:(k + 1) * 128].unsqueeze(1)),
                writes=[B_stat], dma=1)

            def f_bv2(e, l=l, k=k):
                return e.matmul(ps[0][0:1, 0:160], stat[:, 1:2], dec[:, 0:160],
                                start=(k == 0), stop=(k == KD - 1))
            p.emit("pe", f_bv2, reads=[B_dec, B_stat], writes=[PB[0]])
        p.emit("dve", lambda e: e.tensor_copy(s_ch[0:1, 0:160],
                                              ps[0][0:1, 0:160]),
               reads=[PB[0]], writes=[B_sch])
        p.emit("act", lambda e, l=l: e.dma_start(
            out=cbv[l, 4096:4256].unsqueeze(0), in_=s_ch[0:1, 0:160]),
            reads=[B_sch], writes=[B_cbv], dma=1)
        p.emit("dve", lambda e: e.memset(s_ch[0:1, 160:256], 0.0),
               writes=[B_sch])
        p.emit("act", lambda e, l=l: e.dma_start(
            out=cbv[l, 4256:4352].unsqueeze(0), in_=s_ch[0:1, 160:256]),
            reads=[B_sch], writes=[B_cbv], dma=1)
        p.emit("sp", lambda e, l=l: e.dma_start(
            out=bvcol[:, l * NDC:(l + 1) * NDC],
            in_=cbv[l, :].rearrange("(a b) -> b a", b=128),
            allow_slow_non_contiguous=True),
            reads=[B_cbv], writes=[B_bv], dma=1)
        p.emit("dve", lambda e, l=l: e.tensor_scalar(
            out=ccol[0:NH, 8 + l:9 + l], in0=bvcol[0:NH, l * NDC + 33:l * NDC + 34],
            scalar1=ccol[0:NH, 6 + l:7 + l], scalar2=None, op0=Alu.add),
            reads=[B_bv, B_cc], writes=[B_cc])

        for k in range(KDIN):
            p.emit("sp", lambda e, l=l, k=k: e.dma_start(
                out=mf[:, 0:D], in_=w_out_t[l, k * 128:(k + 1) * 128, :]),
                writes=[B_wrk], dma=1)
            p.emit("sp", lambda e, l=l, k=k: e.dma_start(
                out=stat[:, 0:1],
                in_=rms_w[l, k * 128:(k + 1) * 128].unsqueeze(1)),
                writes=[B_stat], dma=1)
            _wsplit_store(p, E, mf[:, 0:D], D, stat[:, 0:1],
                          mb[:, 0:D], mb[:, 4352:4352 + D],
                          [B_stat], [B_wrk, B_mbs, B_mbs])

            def f_so(e, l=l, k=k):
                ops = []
                for h in range(2):
                    hoff = 4352 * h
                    a0 = (k % 2) * 2 + h
                    ops.append(e.dma_start(
                        out=cout_z[l, :, k // 2, :, a0 * 512:(a0 + 1) * 512]
                        .rearrange("s p c -> p s c"),
                        in_=mb[:, hoff:hoff + D]
                        .rearrange("p (s c) -> p s c", c=512)))
                return ops
            p.emit("sp", f_so, reads=[B_mbs], writes=[B_cout], dma=2)

    for which, wap, rows, blocks, cdst, B_c in (
            ("fc", w_fc, HID // 128, D // BK, cfc, B_cfc),
            ("proj", w_proj, D // 128, HID // BK, cproj, B_cproj)):
        ncols = D if which == "fc" else HID
        nkt = ncols // 128
        for r in range(rows):
            p.emit("sp", lambda e, wap=wap, r=r, ncols=ncols: e.dma_start(
                out=mf[:, 0:ncols], in_=wap[r * 128:(r + 1) * 128, :]),
                writes=[B_wrk], dma=1)
            for q in range(ncols // 1024):
                emit_quant(E, "dve", mf[:, q * 1024:(q + 1) * 1024],
                           mb[:, q * 1024:(q + 1) * 1024], 1024,
                           [B_wrk], B_mbs,
                           stat, B_stat, mf_scr(E), B_wrk2(E))
            for kt in range(nkt):
                eng = "sp" if kt % 2 == 0 else "act"
                p.emit(eng, lambda e, kt=kt, ncols=ncols: e.dma_start_transpose(
                    mb[:, 8704 + kt * 128:8704 + (kt + 1) * 128],
                    mb[:, kt * 128:(kt + 1) * 128]),
                    reads=[B_mbs], writes=[B_mbs], dma=1)
            def f_str(e, r=r, cdst=cdst, nkt=nkt):
                return e.dma_start(
                    out=cdst[:, r // 4, :, (r % 4) * 128:(r % 4) * 128 + 128]
                    .rearrange("c p n -> p c n"),
                    in_=mb[:, 8704:8704 + nkt * 128]
                    .rearrange("p (c n) -> p c n", n=128))
            p.emit("sp", f_str, reads=[B_mbs], writes=[B_c], dma=1)


def mf_scr(E):
    return E["mf"][:, 4608:6656]


def B_wrk2(E):
    return E["B_wrk"]


def emit_quant(E, eng, src_ap, dst_ap, ncols, rbufs, B_dst, stat, B_stat,
               scr, B_scr):
    p = E["p"]
    nb = ncols // BK
    am = stat[:, 0:nb]
    am2 = stat[:, nb:2 * nb]
    lgc = stat[:, 2 * nb:3 * nb]
    ev = stat[:, 3 * nb:4 * nb]
    tt = stat[:, 4 * nb:5 * nb]
    bb = stat[:, 5 * nb:6 * nb]
    inv = stat[:, 6 * nb:7 * nb]
    sc = stat[:, 7 * nb:8 * nb]

    def f_am(e):
        return e.tensor_reduce(out=am, in_=src_ap.rearrange(
            "p (b k) -> p b k", k=BK), axis=AX.X, op=Alu.max,
            apply_absolute_value=True)
    p.emit(eng, f_am, reads=rbufs, writes=[B_stat])

    def f_am2(e):
        return e.tensor_scalar(out=am2, in0=am, scalar1=float(2.0 ** -126),
                               scalar2=None, op0=Alu.max)
    p.emit(eng, f_am2, reads=[B_stat], writes=[B_stat])

    def f_lg(e):
        return e.activation(out=lgc, in_=am2, func=Act.Ln)
    p.emit("act", f_lg, reads=[B_stat], writes=[B_stat])

    def f_lgc(e):
        return e.tensor_scalar(out=lgc, in0=lgc, scalar1=float(1.0 / LN2),
                               scalar2=None, op0=Alu.mult)
    p.emit(eng, f_lgc, reads=[B_stat], writes=[B_stat])

    exp_ladder(p, eng, ([B_stat], B_stat, B_stat, B_stat), lgc, ev, tt, bb)

    cst, B_cst = E["cst"], E["B_cst"]

    def f_inv(e):
        return e.activation(out=inv, in_=ev, func=Act.Exp, scale=-LN2,
                            bias=cst[:, 1:2])
    p.emit("act", f_inv, reads=[B_stat, B_cst], writes=[B_stat])

    def f_sc(e):
        return e.activation(out=sc, in_=ev, func=Act.Exp, scale=LN2,
                            bias=cst[:, 2:3])
    p.emit("act", f_sc, reads=[B_stat, B_cst], writes=[B_stat])

    inv_b = inv.unsqueeze(2).broadcast_to([128, nb, BK])
    sc_b = sc.unsqueeze(2).broadcast_to([128, nb, BK])
    src3 = src_ap.rearrange("p (b k) -> p b k", k=BK)
    dst3 = dst_ap.rearrange("p (b k) -> p b k", k=BK)
    quant_ladder(p, eng, (rbufs + [B_stat], [B_scr, B_scr, B_dst]),
                 src3, dst3, inv_b, sc_b,
                 scr[:, 0:ncols].rearrange("p (b k) -> p b k", k=BK),
                 scr[:, ncols:2 * ncols].rearrange("p (b k) -> p b k", k=BK))


def emit_ln_stats(E, src_ap, rbufs, nt_col, u_out_ap, wbufs, x2_scr_ap, B_x2):
    p = E["p"]
    stat, B_stat = E["stat"], E["B_stat"]
    c0 = 512 + nt_col * 8
    sx = stat[:, c0:c0 + 1]
    sxx = stat[:, c0 + 1:c0 + 2]
    m = stat[:, c0 + 2:c0 + 3]
    b4 = stat[:, c0 + 3:c0 + 4]
    rstd = stat[:, c0 + 4:c0 + 5]

    p.emit("dve", lambda e: e.tensor_reduce(
        out=sx, in_=src_ap, axis=AX.X, op=Alu.add),
        reads=rbufs, writes=[B_stat])
    p.emit("act", lambda e: e.activation(
        out=x2_scr_ap, in_=src_ap, func=Act.Square, accum_out=sxx),
        reads=rbufs, writes=[B_x2, B_stat])
    p.emit("dve", lambda e: e.tensor_scalar(
        out=m, in0=sx, scalar1=float(1.0 / D), scalar2=None, op0=Alu.mult),
        reads=[B_stat], writes=[B_stat])
    p.emit("dve", lambda e: e.tensor_scalar(
        out=b4, in0=m, scalar1=m, scalar2=-1.0, op0=Alu.mult, op1=Alu.mult),
        reads=[B_stat], writes=[B_stat])
    p.emit("dve", lambda e: e.tensor_scalar(
        out=b4, in0=b4, scalar1=EPS, scalar2=None, op0=Alu.add),
        reads=[B_stat], writes=[B_stat])
    p.emit("act", lambda e: e.activation(
        out=rstd, in_=sxx, func=Act.Sqrt, scale=float(1.0 / D), bias=b4),
        reads=[B_stat], writes=[B_stat])
    p.emit("dve", lambda e: e.reciprocal(out=rstd, in_=rstd),
        reads=[B_stat], writes=[B_stat])
    p.emit("dve", lambda e: e.tensor_scalar(
        out=u_out_ap, in0=src_ap, scalar1=m, scalar2=rstd,
        op0=Alu.subtract, op1=Alu.mult),
        reads=rbufs + [B_stat], writes=wbufs)
    return rstd


def emit_sweep(E, sw, b):
    p = E["p"]
    g = lambda k: E[k]
    sweep, NT, NCH = g("sweep"), g("NT"), g("NCH")
    n_sweeps = g("n_sweeps")
    x_in, y_out = g("x_in"), g("y_out")
    B_y = g("B_y")
    h_t, B_ht = g("h_t"), g("B_ht")
    uh, ul, B_uh, B_ul = g("uh"), g("ul"), g("B_uh"), g("B_ul")
    uf_hi, uf_lo, B_ufh, B_ufl = g("uf_hi"), g("uf_lo"), g("B_ufh"), g("B_ufl")
    sg, B_sg = g("sg"), g("B_sg")
    yg, B_yg = g("yg"), g("B_yg")
    stat, B_stat = g("stat"), g("B_stat")
    wring, B_wr = g("wring"), g("B_wr")
    cin, cout_, B_cin, B_cout = g("cin"), g("cout_"), g("B_cin"), g("B_cout")
    ps, PB = g("ps"), g("PB")
    xbc_r, B_xbr = g("xbc_r"), g("B_xbr")
    bvcol, B_bv = g("bvcol"), g("B_bv")
    cbcol, B_cb = g("cbcol"), g("B_cb")
    ccol, B_cc = g("ccol"), g("B_cc")
    drep, B_dr = g("drep"), g("B_dr")
    dts = g("dts")
    B_dt0, B_dt1, B_dt2 = g("B_dt0"), g("B_dt1"), g("B_dt2")
    stk, B_stk = g("stk"), g("B_stk")
    dr_a, dr_ea, dr_rs = g("dr_a"), g("dr_ea"), g("dr_rs")
    B_dra, B_drea, B_drrs = g("B_dra"), g("B_drea"), g("B_drrs")
    mb, mf = g("mb"), g("mf")
    convc = g("convc")
    B_cvc = g("B_cvc")
    state, B_st = g("state"), g("B_st")
    ident, B_id = g("ident"), g("B_id")
    mask01, B_mk = g("mask01"), g("B_mk")
    aeb, B_aeb = g("aeb"), g("B_aeb")
    wTt, B_wT = g("wTt"), g("B_wT")
    abg, B_abg = g("abg"), g("B_abg")
    dec, B_dec = g("dec"), g("B_dec")
    yg, B_yg = g("yg"), g("B_yg")
    zt, B_zt = g("zt"), g("B_zt")
    s_ch, B_sch = g("s_ch"), g("B_sch")
    misc, B_misc = g("misc"), g("B_misc")
    btile, B_bt = g("btile"), g("B_bt")
    xr, B_xr = g("xr"), g("B_xr")
    xq_t, B_xq = g("uh"), g("B_uh")
    hq_t, B_hq = g("hqs"), g("B_hqs")
    xqT, B_xqT = g("uf_hi"), g("B_ufh")
    wf, B_wf = g("wf"), g("B_wf")
    hqT, B_hqT = g("mb"), g("B_mb_hqT")
    cfc, cproj, B_cfc, B_cproj = g("cfc"), g("cproj"), g("B_cfc"), g("B_cproj")
    b_fc = g("b_fc")
    B_xbs, B_xtf, B_ea, B_bw = g("B_xbs"), g("B_xtf"), g("B_ea"), g("B_bw")
    B_xhh, B_xhl, B_xth, B_xtl = g("B_xhh"), g("B_xhl"), g("B_xth"), g("B_xtl")
    B_ygh, B_ygl = g("B_ygh"), g("B_ygl")
    B_bpk, B_cpa, B_cpb = g("B_bpk"), g("B_cpa"), g("B_cpb")
    B_sth, B_stl, B_stf = g("B_sth"), g("B_stl"), g("B_stf")
    B_st0, B_bcT, B_adt, B_wrk = g("B_st0"), g("B_bcT"), g("B_adt"), g("B_wrk")
    MB_XHH, MB_XHL = g("MB_XHH"), g("MB_XHL")
    MB_XTH, MB_XTL = g("MB_XTH"), g("MB_XTL")
    MB_YGH, MB_YGL = g("MB_YGH"), g("MB_YGL")
    MB_BPK, MB_CPA, MB_CPB = g("MB_BPK"), g("MB_CPA"), g("MB_CPB")
    MB_SCR = 12672
    MB_STH, MB_STL = 12800, 13312
    MF_XBS, MF_XTF, MF_BW = g("MF_XBS"), g("MF_XTF"), g("MF_BW")
    MF_STF, MF_ST0, MF_BCT, MF_ADT, MF_WRK = (
        g("MF_STF"), g("MF_ST0"), g("MF_BCT"), g("MF_ADT"), g("MF_WRK"))
    MF_EXP = 6080
    HSL, PSL = g("HSL"), g("PSL")
    row0 = sw * sweep

    for nt in range(NT):
        r0 = row0 + nt * 128
        p.emit("sp", lambda e, r0=r0: e.dma_start(
            out=sg[:], in_=x_in[r0:r0 + 128, :]), writes=[B_sg], dma=1)
        emit_ln_stats(E, sg[:], [B_sg], nt,
                      h_t[:, nt * D:(nt + 1) * D], [B_ht], yg[:, 0:D], B_yg)

    if sw == n_sweeps // 2:
        state, B_st = g("state"), g("B_st")
        p.emit("dve", lambda e: e.memset(state[:], 0.0),
               writes=[B_st[0], B_st[1]])

    for l in range(_DBG_NL):
        emit_mamba_layer(E, sw, b, l)

    p.barrier()
    if not _DBG_FFN:
        for nt in range(NT):
            r0 = row0 + nt * 128
            p.emit("dve", lambda e, nt=nt: e.tensor_copy(
                sg[:], h_t[:, nt * D:(nt + 1) * D]),
                reads=[B_ht], writes=[B_sg])
            p.emit("act", lambda e, r0=r0: e.dma_start(
                out=y_out[r0:r0 + 128, :], in_=sg[:]),
                reads=[B_sg], writes=[B_y], dma=1)
        p.barrier()
        return
    for nt in range(NT):
        r0 = row0 + nt * 128
        p.emit("sp", lambda e, r0=r0: e.dma_start(
            out=xr[:], in_=x_in[r0:r0 + 128, :]), writes=[B_xr], dma=1)
        emit_ln_stats(E, xr[:], [B_xr], nt, sg[:], [B_sg], yg[:, 0:D], B_yg)
        emit_quant(E, "dve", sg[:], xq_t[:, nt * D:(nt + 1) * D], D,
                   [B_sg], B_xq, dec, B_dec, mf, B_wrk)
        def f_txq(e, nt=nt):
            return [e.dma_start_transpose(
                xqT[:, k * sweep + nt * 128:k * sweep + (nt + 1) * 128],
                xq_t[:, nt * D + k * 128:nt * D + (k + 1) * 128])
                for k in range(KD)]
        p.emit("sp", f_txq, reads=[B_xq], writes=[B_xqT], dma=KD)
    harea = [(mf, B_wrk, 0), (mf, B_wrk, HID),
             (xbc_r, B_xbr, 0), (xbc_r, B_xbr, HID)]
    for sl in range(HSL):
        p.emit("sp", lambda e, sl=sl: e.dma_start(
            out=btile[:], in_=b_fc[sl * 512:(sl + 1) * 512]
            .unsqueeze(0).broadcast_to([128, 512])),
            writes=[B_bt], dma=1)
        for k in range(KD):
            s4 = (sl * KD + k) % 4
            p.emit("sp", lambda e, sl=sl, k=k, s4=s4: e.dma_start(
                out=wf[:, s4 * 512:(s4 + 1) * 512], in_=cfc[k, sl]),
                reads=[B_cfc], writes=[B_wf[s4]], dma=1)

            def f_mm(e, sl=sl, k=k, s4=s4):
                r = None
                for nt in range(NT):
                    r = e.matmul(
                        ps[nt][:, :],
                        xqT[:, k * sweep + nt * 128:k * sweep + (nt + 1) * 128],
                        wf[:, s4 * 512:(s4 + 1) * 512],
                        start=(k == 0), stop=(k == KD - 1))
                return r
            p.emit("pe", f_mm, reads=[B_xqT, B_wf[s4]],
                   writes=[PB[nt] for nt in range(NT)])
        for nt in range(NT):
            ar, abuf, aoff = harea[nt]

            def f_bias(e, nt=nt, ar=ar, aoff=aoff, sl=sl):
                return e.tensor_tensor(
                    out=ar[:, aoff + sl * 512:aoff + (sl + 1) * 512],
                    in0=ps[nt][:, :], in1=btile[:], op=Alu.add)
            p.emit("dve", f_bias, reads=[PB[nt], B_bt], writes=[abuf])

            def f_gelu(e, nt=nt, ar=ar, aoff=aoff, sl=sl):
                return e.activation(
                    out=ar[:, aoff + sl * 512:aoff + (sl + 1) * 512],
                    in_=ar[:, aoff + sl * 512:aoff + (sl + 1) * 512],
                    func=Act.Gelu)
            p.emit("act", f_gelu, reads=[abuf], writes=[abuf])
    for nt in range(NT):
        ar, abuf, aoff = harea[nt]
        for q in range(4):
            emit_quant(E, "dve",
                       ar[:, aoff + q * 1024:aoff + (q + 1) * 1024],
                       hq_t[:, q * 1024:(q + 1) * 1024],
                       1024, [abuf], B_hq, dec, B_dec, yg, B_yg)
        def f_thq(e, nt=nt, par=0):
            return [e.dma_start_transpose(
                hqT[:, kt * sweep + nt * 128:kt * sweep + (nt + 1) * 128],
                hq_t[:, kt * 128:(kt + 1) * 128])
                for kt in range(par, HID // 128, 2)]
        p.emit("sp", lambda e, nt=nt: f_thq(e, nt, 0),
               reads=[B_hq], writes=[B_hqT], dma=HID // 256)
        p.emit("act", lambda e, nt=nt: f_thq(e, nt, 1),
               reads=[B_hq], writes=[B_hqT], dma=HID // 256)
    for psl in range(PSL):
        for kt in range(HID // 128):
            s4 = (psl * (HID // 128) + kt) % 4
            p.emit("sp", lambda e, psl=psl, kt=kt, s4=s4: e.dma_start(
                out=wf[:, s4 * 512:(s4 + 1) * 512], in_=cproj[kt, psl]),
                reads=[B_cproj], writes=[B_wf[s4]], dma=1)

            def f_pm(e, psl=psl, kt=kt, s4=s4):
                r = None
                for nt in range(NT):
                    bank = (4 + nt) if psl == 0 else nt
                    r = e.matmul(
                        ps[bank][:, :],
                        hqT[:, kt * sweep + nt * 128:kt * sweep + (nt + 1) * 128],
                        wf[:, s4 * 512:(s4 + 1) * 512],
                        start=(kt == 0), stop=(kt == HID // 128 - 1))
                return r
            p.emit("pe", f_pm, reads=[B_hqT, B_wf[s4]],
                   writes=[PB[(4 + nt) if psl == 0 else nt] for nt in range(NT)])
    for nt in range(NT):
        r0 = row0 + nt * 128
        p.emit("sp", lambda e, r0=r0: e.dma_start(
            out=xr[:], in_=x_in[r0:r0 + 128, :]), writes=[B_xr], dma=1)
        for psl in range(PSL):
            bank = (4 + nt) if psl == 0 else nt

            def f_gb(e, nt=nt, psl=psl, bank=bank):
                return e.tensor_tensor(
                    out=sg[:, psl * 512:(psl + 1) * 512], in0=ps[bank][:, :],
                    in1=misc[:, psl * 512:(psl + 1) * 512],
                    op=Alu.add)
            p.emit("dve", f_gb, reads=[PB[bank], B_misc], writes=[B_sg])
        p.emit("act", lambda e: e.activation(out=sg[:], in_=sg[:],
                                             func=Act.Silu),
               reads=[B_sg], writes=[B_sg])
        p.emit("dve", lambda e, nt=nt: e.tensor_tensor(
            out=sg[:], in0=sg[:], in1=h_t[:, nt * D:(nt + 1) * D],
            op=Alu.mult), reads=[B_ht, B_sg], writes=[B_sg])
        p.emit("dve", lambda e: e.tensor_tensor(
            out=sg[:], in0=sg[:], in1=xr[:], op=Alu.add),
            reads=[B_xr, B_sg], writes=[B_sg])
        p.emit("act", lambda e, r0=r0: e.dma_start(
            out=y_out[r0:r0 + 128, :], in_=sg[:]),
            reads=[B_sg], writes=[B_y], dma=1)
    p.barrier()


def emit_mamba_layer(E, sw, b, l):
    p = E["p"]
    g = lambda k: E[k]
    sweep, NT, NCH = g("sweep"), g("NT"), g("NCH")
    h_t, B_ht = g("h_t"), g("B_ht")
    uh, ul, B_uh, B_ul = g("uh"), g("ul"), g("B_uh"), g("B_ul")
    uf_hi, uf_lo, B_ufh, B_ufl = g("uf_hi"), g("uf_lo"), g("B_ufh"), g("B_ufl")
    sg, B_sg = g("sg"), g("B_sg")
    stat, B_stat = g("stat"), g("B_stat")
    wring, B_wr = g("wring"), g("B_wr")
    cin, B_cin = g("cin"), g("B_cin")
    ps, PB = g("ps"), g("PB")
    xbc_r, B_xbr = g("xbc_r"), g("B_xbr")
    bvcol, B_bv = g("bvcol"), g("B_bv")
    ccol, B_cc = g("ccol"), g("B_cc")
    dts = g("dts")
    B_dt0, B_dt1, B_dt2 = g("B_dt0"), g("B_dt1"), g("B_dt2")
    stk, B_stk = g("stk"), g("B_stk")
    dr_a, dr_ea, dr_rs = g("dr_a"), g("dr_ea"), g("dr_rs")
    B_dra, B_drea, B_drrs = g("B_dra"), g("B_drea"), g("B_drrs")
    convc, B_cvc = g("convc"), g("B_cvc")
    misc, B_misc = g("misc"), g("B_misc")
    cw, B_cw = g("cw"), g("B_cw")
    S = sweep
    SP4 = S + 4

    ps, PB = g("ps"), g("PB")
    ident, B_id = g("ident"), g("B_id")
    mf = g("mf")
    MF_XBS = g("MF_XBS")
    B_uff = g("B_xbs")[0]
    for nt in range(NT):
        src_ap, src_bufs = h_t[:, nt * D:(nt + 1) * D], [g("B_ht")]
        emit_ln_stats(E, src_ap, src_bufs, nt, sg[:], [B_sg],
                      g("yg")[:, 0:D], g("B_yg"))
        for tq in range(2):
            def f_tu(e, tq=tq):
                r = None
                for i in range(4):
                    k = tq * 4 + i
                    r = e.transpose(ps[6 + tq][:, i * 128:(i + 1) * 128],
                                    sg[:, k * 128:(k + 1) * 128], ident[:])
                return r
            p.emit("pe", f_tu, reads=[B_sg, B_id], writes=[PB[6 + tq]])
            p.emit("dve", lambda e, tq=tq: e.tensor_copy(
                mf[:, MF_XBS + tq * 512:MF_XBS + (tq + 1) * 512],
                ps[6 + tq][:, 0:512]),
                reads=[PB[6 + tq]], writes=[B_uff])
        ufh3 = uf_hi[:, 0:KD * S].rearrange("p (k t) -> p k t", t=S)[
            :, :, nt * 128:nt * 128 + 128]
        ufl3 = uf_lo[:, 0:KD * S].rearrange("p (k t) -> p k t", t=S)[
            :, :, nt * 128:nt * 128 + 128]
        uff3 = mf[:, MF_XBS:MF_XBS + 1024].rearrange(
            "p (k t) -> p k t", t=128)
        p.emit("act", lambda e, ufh3=ufh3, uff3=uff3: e.copy(
            out=ufh3, in_=uff3), reads=[B_uff], writes=[B_ufh])
        p.emit("dve", lambda e, ufh3=ufh3, ufl3=ufl3, uff3=uff3:
               e.scalar_tensor_tensor(out=ufl3, in0=uff3, scalar=1.0,
                                      in1=ufh3, op0=Alu.mult,
                                      op1=Alu.subtract),
               reads=[B_uff, B_ufh], writes=[B_ufl])

    cslot = (l * 2 + b) * NXT * 4
    p.emit("dve", lambda e, cslot=cslot: e.tensor_copy(
        xbc_r[:].rearrange("p (t c) -> p t c", c=SP4)[:, :, 1:4],
        convc[:, cslot:cslot + NXT * 4]
        .rearrange("p (t c) -> p t c", c=4)[:, :, 1:4]),
        reads=[B_cvc[l][b]], writes=[B_xbr])

    for i, dc in enumerate(range(16, NDC)):
        slot = i % 2
        p.emit("sp", lambda e, dc=dc, slot=slot: e.dma_start(
            out=wring[:, slot * 2048:slot * 2048 + 2048]
            .rearrange("p (c n) -> p c n", n=128),
            in_=cin[l, dc].rearrange("k h p n -> p (k h) n")),
            reads=[B_cin], writes=[B_wr[slot]], dma=1)

        def f_mm(e, dc=dc, slot=slot):
            r = None
            for k in range(KD):
                wh = wring[:, slot * 2048 + (k * 2) * 128:slot * 2048 + (k * 2 + 1) * 128]
                wl = wring[:, slot * 2048 + (k * 2 + 1) * 128:slot * 2048 + (k * 2 + 2) * 128]
                xh_ = uf_hi[:, k * S:(k + 1) * S]
                xl_ = uf_lo[:, k * S:(k + 1) * S]
                r = e.matmul(ps[slot][:, 0:S], wh, xh_, start=(k == 0), stop=False)
                r = e.matmul(ps[slot][:, 0:S], wh, xl_, start=False, stop=False)
                r = e.matmul(ps[slot][:, 0:S], wl, xh_, start=False,
                             stop=(k == KD - 1))
            return r
        p.emit("pe", f_mm, reads=[B_wr[slot], B_ufh, B_ufl],
               writes=[PB[slot]])
        if dc < 33:
            xt = dc - 16

            def f_ev(e, dc=dc, slot=slot, xt=xt):
                return e.activation(
                    out=xbc_r[:, xt * SP4 + 4:xt * SP4 + 4 + S],
                    in_=ps[slot][:, 0:S], func=Act.Identity,
                    bias=bvcol[:, l * NDC + dc:l * NDC + dc + 1])
            p.emit("act", f_ev, reads=[PB[slot], B_bv], writes=[B_xbr])
        else:
            def f_ev(e, slot=slot):
                return e.activation(
                    out=dts[:, 0:S], in_=ps[slot][0:NH, 0:S], func=Act.Identity,
                    bias=ccol[0:NH, 8 + l:9 + l])
            p.emit("act", f_ev, reads=[PB[slot], B_cc], writes=[B_dt0])

    q0, q1, q2 = dts[:, 0:S], dts[:, S:2 * S], dts[:, 2 * S:3 * S]
    p.emit("act", lambda e: e.activation(out=q2, in_=q0, func=Act.Abs),
           reads=[B_dt0], writes=[B_dt2])
    p.emit("act", lambda e: e.activation(out=q2, in_=q2, func=Act.Exp,
                                         scale=-1.0),
           reads=[B_dt2], writes=[B_dt2])
    p.emit("act", lambda e: e.activation(out=q2, in_=q2, func=Act.Ln,
                                         bias=g("cst")[0:NH, 0:1]),
           reads=[B_dt2, g("B_cst")], writes=[B_dt2])
    p.emit("dve", lambda e: e.scalar_tensor_tensor(
        out=q1, in0=q0, scalar=0.0, in1=q2, op0=Alu.max, op1=Alu.add),
        reads=[B_dt0, B_dt2], writes=[B_dt1])
    p.emit("act", lambda e: e.activation(out=q2, in_=q1, func=Act.Ln),
           reads=[B_dt1], writes=[B_dt2])
    p.emit("act", lambda e: [
        e.dma_start(out=stk[32:64, :], in_=q1),
        e.dma_start(out=stk[64:96, :], in_=q2)],
        reads=[B_dt1, B_dt2], writes=[B_stk], dma=2)
    p.emit("dve", lambda e, l=l: e.tensor_scalar(
        out=q0, in0=q1, scalar1=ccol[0:NH, l:l + 1], scalar2=None,
        op0=Alu.mult), reads=[B_dt1, g("B_cc")], writes=[B_dt0])
    for ck in range(NCH):
        p.emit("dve", lambda e, ck=ck: e.tensor_tensor_scan(
            out=stk[0:32, ck * CH:(ck + 1) * CH],
            data0=q0[:, ck * CH:(ck + 1) * CH],
            data1=q0[:, ck * CH:(ck + 1) * CH],
            initial=0.0, op0=Alu.add, op1=Alu.bypass),
            reads=[B_dt0], writes=[B_stk])
    p.emit("act", lambda e: e.activation(out=q0, in_=stk[0:32, :],
                                         func=Act.Exp),
           reads=[B_stk], writes=[B_dt0])
    p.emit("act", lambda e: e.dma_start(out=dr_a[:, :], in_=stk[0:32, :]),
           reads=[B_stk], writes=[B_dra], dma=1)
    p.emit("act", lambda e: e.dma_start(out=dr_ea[:, :], in_=q0),
           reads=[B_dt0], writes=[B_drea], dma=1)

    p.emit("sp", lambda e, l=l: e.dma_start(
        out=g("dec")[:, 0:DIN],
        in_=g("cbv")[l, 0:DIN].unsqueeze(0).broadcast_to([128, DIN])),
        reads=[g("B_cbv")], writes=[g("B_dec")], dma=1)

    for ck in range(NCH):
        emit_chunk(E, sw, b, l, ck)

    p.emit("dve", lambda e, cslot=cslot: e.tensor_copy(
        convc[:, cslot:cslot + NXT * 4]
        .rearrange("p (t c) -> p t c", c=4)[:, :, 1:4],
        xbc_r[:].rearrange("p (t c) -> p t c", c=SP4)[:, :, S + 1:S + 4]),
        reads=[B_xbr], writes=[B_cvc[l][b]])


def emit_chunk(E, sw, b, l, ck):
    p = E["p"]
    g = lambda k: E[k]
    sweep, NCH = g("sweep"), g("NCH")
    S, SP4 = sweep, sweep + 4
    xbc_r, B_xbr = g("xbc_r"), g("B_xbr")
    mb, mf = g("mb"), g("mf")
    stk, B_stk = g("stk"), g("B_stk")
    dr_a, B_dra = g("dr_a"), g("B_dra")
    ps, PB = g("ps"), g("PB")
    ident, B_id = g("ident"), g("B_id")
    mask01, B_mk = g("mask01"), g("B_mk")
    aeb, B_aeb = g("aeb"), g("B_aeb")
    wTt, B_wT = g("wTt"), g("B_wT")
    abg, B_abg = g("abg"), g("B_abg")
    s_ch, B_sch = g("s_ch"), g("B_sch")
    cbcol, B_cb = g("cbcol"), g("B_cb")
    cw, B_cw = g("cw"), g("B_cw")
    B_xbs, B_xtf, B_bw = g("B_xbs"), g("B_xtf"), g("B_bw")
    B_xhh, B_xhl, B_xth, B_xtl = g("B_xhh"), g("B_xhl"), g("B_xth"), g("B_xtl")
    B_bpk, B_cpa, B_cpb = g("B_bpk"), g("B_cpa"), g("B_cpb")
    B_st0, B_bcT, B_adt, B_wrk = g("B_st0"), g("B_bcT"), g("B_adt"), g("B_wrk")
    MB_XHH, MB_XHL = g("MB_XHH"), g("MB_XHL")
    MB_XTH, MB_XTL = g("MB_XTH"), g("MB_XTL")
    MB_BPK, MB_CPA, MB_CPB = g("MB_BPK"), g("MB_CPA"), g("MB_CPB")
    MB_SCR = 12672
    MF_XBS, MF_XTF, MF_BW = g("MF_XBS"), g("MF_XTF"), g("MF_BW")
    MF_STF, MF_ST0, MF_BCT, MF_ADT, MF_WRK = (
        g("MF_STF"), g("MF_ST0"), g("MF_BCT"), g("MF_ADT"), g("MF_WRK"))
    MF_EXP = 6080

    xbs = lambda t: mf[:, MF_XBS + t * 128:MF_XBS + (t + 1) * 128]
    t16 = lambda rs, re_: mf[rs:re_, MF_XBS + 16 * 128:MF_XBS + 17 * 128]
    acc3 = mf[:, MF_XBS:MF_XBS + DIN].rearrange("p (t n) -> p t n", n=128)
    tmp3 = mf[:, MF_XTF:MF_XTF + DIN].rearrange("p (t n) -> p t n", n=128)

    def win3(kk, ck=ck):
        return bass.AP(tensor=xbc_r.tensor, offset=1 + kk + ck * CH,
                       ap=[[NXT * SP4, 128], [SP4, 16], [1, CH]])             if False else             xbc_r[:, 0:16 * SP4].rearrange("p (t n) -> p t n", n=SP4)[
                :, :, 1 + kk + ck * CH:1 + kk + ck * CH + CH]

    def cw3(kk):
        return cw[:, l * NXT * 4 + kk:l * NXT * 4 + kk + 16 * 4:1]             .rearrange("p (t o) -> p t o", o=4)[:, :, 0:1]             .broadcast_to([128, 16, CH])

    def cb3():
        return cbcol[:, l * NXT:l * NXT + 16].unsqueeze(2)             .broadcast_to([128, 16, CH])

    def f_cb0(e):
        return e.tensor_tensor(out=acc3, in0=win3(0), in1=cw3(0), op=Alu.mult)
    p.emit("dve", f_cb0, reads=[B_xbr, B_cw], writes=[B_xbs[0]])
    for kk in range(1, 4):
        def f_cm(e, kk=kk):
            return e.tensor_tensor(out=tmp3, in0=win3(kk), in1=cw3(kk),
                                   op=Alu.mult)
        p.emit("dve", f_cm, reads=[B_xbr, B_cw], writes=[B_xtf[0]])

        def f_ca(e):
            return e.tensor_tensor(out=acc3, in0=acc3, in1=tmp3, op=Alu.add)
        p.emit("dve", f_ca, reads=[B_xtf[0], B_xbs[0]], writes=[B_xbs[0]])

    def f_cba(e):
        return e.tensor_tensor(out=acc3, in0=acc3, in1=cb3(), op=Alu.add)
    p.emit("dve", f_cba, reads=[B_cb, B_xbs[0]], writes=[B_xbs[0]])
    p.emit("act", lambda e: e.activation(
        out=mf[:, MF_XBS:MF_XBS + DIN], in_=mf[:, MF_XBS:MF_XBS + DIN],
        func=Act.Silu), reads=[B_xbs[0]], writes=[B_xbs[0]])
    xt = 16
    def f_c0(e, xt=xt):
        return e.tensor_scalar(
            out=xbs(xt), in0=xbc_r[:, xt * SP4 + 1 + ck * CH:
                                   xt * SP4 + 1 + ck * CH + CH],
            scalar1=cw[:, (l * NXT + xt) * 4:(l * NXT + xt) * 4 + 1],
            scalar2=None, op0=Alu.mult)
    p.emit("dve", f_c0, reads=[B_xbr, B_cw], writes=[B_xbs[0]])
    for kk in range(1, 4):
        def f_cr(e, xt=xt, kk=kk):
            return e.scalar_tensor_tensor(
                out=xbs(xt),
                in0=xbc_r[:, xt * SP4 + 1 + kk + ck * CH:
                          xt * SP4 + 1 + kk + ck * CH + CH],
                scalar=cw[:, (l * NXT + xt) * 4 + kk:
                          (l * NXT + xt) * 4 + kk + 1],
                in1=xbs(xt), op0=Alu.mult, op1=Alu.add)
        p.emit("dve", f_cr, reads=[B_xbr, B_cw, B_xbs[0]], writes=[B_xbs[0]])
    def f_si(e, xt=xt):
        return e.activation(out=xbs(xt), in_=xbs(xt), func=Act.Silu,
                            bias=cbcol[:, l * NXT + xt:l * NXT + xt + 1])
    p.emit("act", f_si, reads=[B_xbs[0], B_cb], writes=[B_xbs[0]])

    p.emit("act", lambda e: e.copy(out=mb[0:64, MB_BPK:MB_BPK + 128],
                                   in_=t16(0, 64)),
           reads=[B_xbs[0]], writes=[B_bpk])
    p.emit("dve", lambda e: e.scalar_tensor_tensor(
        out=mb[0:64, MB_SCR:MB_SCR + 128], in0=t16(0, 64), scalar=1.0,
        in1=mb[0:64, MB_BPK:MB_BPK + 128], op0=Alu.mult, op1=Alu.subtract),
        reads=[B_xbs[0], B_bpk], writes=[B_wrk])
    p.emit("act", lambda e: e.dma_start(
        out=mb[64:128, MB_BPK:MB_BPK + 128],
        in_=mb[0:64, MB_SCR:MB_SCR + 128]),
        reads=[B_wrk], writes=[B_bpk], dma=1)
    p.emit("act", lambda e: e.copy(out=mb[64:128, MB_CPB:MB_CPB + 128],
                                   in_=t16(64, 128)),
           reads=[B_xbs[0]], writes=[B_cpb])
    p.emit("dve", lambda e: e.scalar_tensor_tensor(
        out=mb[64:128, MB_CPA:MB_CPA + 128], in0=t16(64, 128), scalar=1.0,
        in1=mb[64:128, MB_CPB:MB_CPB + 128], op0=Alu.mult, op1=Alu.subtract),
        reads=[B_xbs[0], B_cpb], writes=[B_cpa])
    p.emit("act", lambda e: e.dma_start(
        out=mb[0:64, MB_CPA:MB_CPA + 128],
        in_=mb[64:128, MB_CPB:MB_CPB + 128]),
        reads=[B_cpb], writes=[B_cpa], dma=1)
    p.emit("act", lambda e: e.dma_start(
        out=mb[0:64, MB_CPB:MB_CPB + 128],
        in_=mb[64:128, MB_CPA:MB_CPA + 128]),
        reads=[B_cpa], writes=[B_cpb], dma=1)

    for tq in range(4):
        def f_tx(e, tq=tq):
            r = None
            for i in range(4):
                t_ = tq * 4 + i
                r = e.transpose(
                    ps[6 + tq % 2][:, i * 128:(i + 1) * 128],
                    mf[:, MF_XBS + t_ * 128:MF_XBS + (t_ + 1) * 128],
                    ident[:])
            return r
        p.emit("pe", f_tx, reads=[B_xbs[0], B_id], writes=[PB[6 + tq % 2]])
        p.emit("dve", lambda e, tq=tq: e.tensor_copy(
            mf[:, MF_XTF + tq * 512:MF_XTF + (tq + 1) * 512],
            ps[6 + tq % 2][:, 0:512]),
            reads=[PB[6 + tq % 2]], writes=[B_xtf[0]])
    p.emit("act", lambda e: e.copy(out=mb[:, MB_XTH:MB_XTH + DIN],
                                   in_=mf[:, MF_XTF:MF_XTF + DIN]),
           reads=[B_xtf[0]], writes=[B_xth[0]])
    p.emit("dve", lambda e: e.scalar_tensor_tensor(
        out=mb[:, MB_XTL:MB_XTL + DIN], in0=mf[:, MF_XTF:MF_XTF + DIN],
        scalar=1.0, in1=mb[:, MB_XTH:MB_XTH + DIN],
        op0=Alu.mult, op1=Alu.subtract),
        reads=[B_xtf[0], B_xth[0]], writes=[B_xtl[0]])

    p.emit("pe", lambda e: e.transpose(
        ps[6][:, 128:192], t16(0, 64), ident[0:64, 0:64]),
        reads=[B_xbs[0], B_id], writes=[PB[6]])
    p.emit("dve", lambda e: e.tensor_copy(
        mf[:, MF_BCT:MF_BCT + 64], ps[6][:, 128:192]),
        reads=[PB[6]], writes=[B_bcT])
    p.emit("pe", lambda e, ck=ck: e.transpose(
        ps[6][:, 192:288], stk[:, ck * CH:(ck + 1) * CH],
        ident[0:96, 0:96]),
        reads=[B_stk, B_id], writes=[PB[6]])
    p.emit("dve", lambda e: e.tensor_copy(
        mf[:, MF_ADT:MF_ADT + 96], ps[6][:, 192:288]),
        reads=[PB[6]], writes=[B_adt[0]])

    p.emit("sp", lambda e, ck=ck: e.dma_start(
        out=aeb[:, 0:32],
        in_=dr_a[:, ck * CH + CH - 1].unsqueeze(0).broadcast_to([128, 32]),
        allow_slow_non_contiguous=True),
        reads=[B_dra], writes=[B_aeb], dma=1)
    p.emit("dve", lambda e: e.tensor_tensor(
        out=aeb[:, 32:64], in0=aeb[:, 0:32],
        in1=mf[:, MF_ADT:MF_ADT + 32], op=Alu.subtract),
        reads=[B_aeb, B_adt[0]], writes=[B_aeb])
    p.emit("dve", lambda e: e.tensor_tensor(
        out=aeb[:, 32:64], in0=aeb[:, 32:64],
        in1=mf[:, MF_ADT + 64:MF_ADT + 96], op=Alu.add),
        reads=[B_aeb, B_adt[0]], writes=[B_aeb])
    p.emit("act", lambda e: e.activation(out=wTt[:], in_=aeb[:, 32:64],
                                         func=Act.Exp),
           reads=[B_aeb], writes=[B_wT])
    p.emit("act", lambda e: e.activation(
        out=mf[:, MF_ADT + 96:MF_ADT + 128], in_=mf[:, MF_ADT:MF_ADT + 32],
        func=Act.Exp), reads=[B_adt[0]], writes=[B_adt[0]])
    p.emit("act", lambda e: e.activation(
        out=aeb[:, 0:32], in_=aeb[:, 0:32], func=Act.Exp),
        reads=[B_aeb], writes=[B_aeb])

    def f_st0(e):
        r = e.matmul(ps[7][:, 0:128], mb[:, MB_BPK:MB_BPK + 128],
                     mb[:, MB_CPA:MB_CPA + 128], start=True, stop=False,
                     skip_group_check=True)
        r = e.matmul(ps[7][:, 0:128], mb[:, MB_BPK:MB_BPK + 128],
                     mb[:, MB_CPB:MB_CPB + 128], start=False, stop=True,
                     skip_group_check=True)
        return r
    p.emit("pe", f_st0, reads=[B_bpk, B_cpa, B_cpb], writes=[PB[7]])
    p.emit("dve", lambda e: e.tensor_tensor(
        out=mf[:, MF_ST0:MF_ST0 + 128], in0=ps[7][:, 0:128], in1=mask01[:],
        op=Alu.mult), reads=[PB[7], B_mk], writes=[B_st0])

    for g4 in range(8):
        g8 = g4 // 2
        p.emit("sp", lambda e, g4=g4, ck=ck: e.dma_start(
            out=abg[:].rearrange("p (i n) -> p i n", n=128),
            in_=dr_a[4 * g4:4 * g4 + 4, ck * CH:(ck + 1) * CH]
            .unsqueeze(0).broadcast_to([128, 4, CH])),
            reads=[B_dra], writes=[B_abg[0]], dma=1)

        def f_ts4(e, g4=g4):
            r = None
            for i4 in range(4):
                h = 4 * g4 + i4
                r = e.tensor_scalar(
                    out=mf[:, MF_WRK + i4 * 128:MF_WRK + (i4 + 1) * 128],
                    in0=abg[:, i4 * 128:(i4 + 1) * 128],
                    scalar1=mf[:, MF_ADT + h:MF_ADT + h + 1], scalar2=0.0,
                    op0=Alu.subtract, op1=Alu.min)
            return r
        p.emit("dve", f_ts4, reads=[B_abg[0], B_adt[0]], writes=[B_wrk])
        p.emit("act", lambda e: e.activation(
            out=mf[:, MF_EXP:MF_EXP + 512], in_=mf[:, MF_WRK:MF_WRK + 512],
            func=Act.Exp), reads=[B_wrk], writes=[g("B_exp")])
        if g4 % 2 == 0:
            p.emit("dve", lambda e, g8=g8: e.tensor_tensor(
                out=mf[:, MF_BW:MF_BW + 512].rearrange(
                    "p (i n) -> p i n", n=64),
                in0=mf[:, MF_XTF + g8 * 512:MF_XTF + (g8 + 1) * 512]
                .rearrange("p (i n) -> p i n", n=64),
                in1=wTt[:, g8 * 8:(g8 + 1) * 8].unsqueeze(2)
                .broadcast_to([128, 8, 64]),
                op=Alu.mult),
                reads=[B_xtf[0], B_wT], writes=[B_bw])
            p.emit("pe", lambda e: e.matmul(
                ps[5][64:128, 0:512], mf[:, MF_BCT:MF_BCT + 64],
                mf[:, MF_BW:MF_BW + 512], start=True, stop=True,
                skip_group_check=True),
                reads=[B_bcT, B_bw], writes=[PB[5]])
        emit_head_group(E, sw, b, l, ck, g4)
        if g4 % 2 == 1:
            emit_group_tail(E, sw, b, l, ck, g8)

    emit_outproj_chunk(E, sw, b, l, ck)


def emit_head_group(E, sw, b, l, ck, g4):
    p = E["p"]
    g = lambda k: E[k]
    mb, mf = g("mb"), g("mf")
    ps, PB = g("ps"), g("PB")
    state, B_st = g("state"), g("B_st")
    s_ch, B_sch = g("s_ch"), g("B_sch")
    B_sth, B_stl, B_stf = g("B_sth"), g("B_stl"), g("B_stf")
    B_st0, B_adt, B_wrk = g("B_st0"), g("B_adt"), g("B_wrk")
    B_exp, B_bw = g("B_exp"), g("B_bw")
    B_xth, B_xtl, B_xtf, B_xbs = g("B_xth"), g("B_xtl"), g("B_xtf"), g("B_xbs")
    MB_XTH, MB_XTL = g("MB_XTH"), g("MB_XTL")
    MB_STH, MB_STL = 12800, 13312
    MF_STF, MF_ST0, MF_ADT, MF_BW = (
        g("MF_STF"), g("MF_ST0"), g("MF_ADT"), g("MF_BW"))
    MF_XTF, MF_XBS = g("MF_XTF"), g("MF_XBS")
    MF_EXP = 6080
    hs = [4 * g4 + i for i in range(4)]

    def f_st4(e):
        r = None
        for i4, h in enumerate(hs):
            r = e.scalar_tensor_tensor(
                out=mf[:, MF_STF + i4 * 128:MF_STF + (i4 + 1) * 128],
                in0=mf[:, MF_EXP + i4 * 128:MF_EXP + (i4 + 1) * 128],
                scalar=mf[:, MF_ADT + 32 + h:MF_ADT + 33 + h],
                in1=mf[:, MF_ST0:MF_ST0 + 128], op0=Alu.mult, op1=Alu.mult)
        return r
    p.emit("dve", f_st4, reads=[B_exp, B_adt[0], B_st0],
           writes=[B_stf[0]])

    def f_hi4(e):
        return e.copy(out=mb[:, MB_STH:MB_STH + 512],
                      in_=mf[:, MF_STF:MF_STF + 512])
    p.emit("act", f_hi4, reads=[B_stf[0]], writes=[B_sth[0]])

    def f_lo4(e):
        return e.scalar_tensor_tensor(
            out=mb[:, MB_STL:MB_STL + 512], in0=mf[:, MF_STF:MF_STF + 512],
            scalar=1.0, in1=mb[:, MB_STH:MB_STH + 512],
            op0=Alu.mult, op1=Alu.subtract)
    p.emit("dve", f_lo4, reads=[B_stf[0], B_sth[0]], writes=[B_stl[0]])

    def f_y(e, g4=g4):
        r = None
        for i4, h in enumerate(hs):
            cs = (h % 8) * 64
            sth = mb[:, MB_STH + i4 * 128:MB_STH + (i4 + 1) * 128]
            stl = mb[:, MB_STL + i4 * 128:MB_STL + (i4 + 1) * 128]
            lh = mb[:, MB_XTH + h * 64:MB_XTH + (h + 1) * 64]
            ll = mb[:, MB_XTL + h * 64:MB_XTL + (h + 1) * 64]
            r = e.matmul(ps[2][:, cs:cs + 64], sth, lh,
                         start=True, stop=False, skip_group_check=True)
            r = e.matmul(ps[2][:, cs:cs + 64], stl, lh,
                         start=False, stop=False, skip_group_check=True)
            r = e.matmul(ps[2][:, cs:cs + 64], sth, ll,
                         start=False, stop=True, skip_group_check=True)
        return r
    p.emit("pe", f_y, reads=[B_xth[0], B_xtl[0], B_sth[0], B_stl[0]],
           writes=[PB[2]])

    t16c = mf[64:128, MF_XBS + 16 * 128:MF_XBS + 17 * 128]

    def f_y2(e, l=l, g4=g4):
        cs = (g4 % 2) * 256
        return e.matmul(
            ps[3][:, cs:cs + 256], t16c,
            state[64:128, l * DIN + g4 * 256:l * DIN + (g4 + 1) * 256],
            start=True, stop=True, skip_group_check=True)
    p.emit("pe", f_y2, reads=[B_st[l], B_xbs[0]], writes=[PB[3]])


def emit_group_tail(E, sw, b, l, ck, g8):
    p = E["p"]
    g = lambda k: E[k]
    mb, mf = g("mb"), g("mf")
    ps, PB = g("ps"), g("PB")
    state, B_st = g("state"), g("B_st")
    aeb, B_aeb = g("aeb"), g("B_aeb")
    s_ch, B_sch = g("s_ch"), g("B_sch")
    yg, B_yg = g("yg"), g("B_yg")
    zt, B_zt = g("zt"), g("B_zt")
    wring, B_wr = g("wring"), g("B_wr")
    cin, B_cin = g("cin"), g("B_cin")
    uf_hi, uf_lo, B_ufh, B_ufl = g("uf_hi"), g("uf_lo"), g("B_ufh"), g("B_ufl")
    bvcol, B_bv = g("bvcol"), g("B_bv")
    drep2, B_dr2 = g("drep2"), g("B_dr2")
    B_xtf, B_adt = g("B_xtf"), g("B_adt")
    MF_XTF, MF_ADT, MF_BW = g("MF_XTF"), g("MF_ADT"), g("MF_BW")
    sweep = g("sweep")
    S = sweep

    p.emit("dve", lambda e, l=l, g8=g8: e.tensor_tensor(
        out=s_ch[64:128, :].rearrange("p (i n) -> p i n", n=64),
        in0=state[64:128, l * DIN + g8 * 512:l * DIN + (g8 + 1) * 512]
        .rearrange("p (i n) -> p i n", n=64),
        in1=aeb[64:128, g8 * 8:(g8 + 1) * 8].unsqueeze(2)
        .broadcast_to([64, 8, 64]),
        op=Alu.mult),
        reads=[B_st[l], B_aeb], writes=[B_sch])
    p.emit("dve", lambda e, l=l, g8=g8: e.tensor_tensor(
        out=state[64:128, l * DIN + g8 * 512:l * DIN + (g8 + 1) * 512],
        in0=s_ch[64:128, :], in1=ps[5][64:128, :], op=Alu.add),
        reads=[B_sch, PB[5]], writes=[B_st[l]])

    zb = g8 % 2
    for k2 in range(KD // 2):
        slot = E.setdefault("wslot", 0)
        E["wslot"] = 1 - slot
        p.emit("sp", lambda e, l=l, g8=g8, k2=k2, slot=slot: e.dma_start(
            out=wring[:, slot * 2048:slot * 2048 + 2048],
            in_=g("cin_z")[l, g8, k2]),
            reads=[B_cin], writes=[B_wr[slot]], dma=1)

        def f_zmm(e, k2=k2, slot=slot, zb=zb, ck=ck):
            r = None
            for kl in range(2):
                k = 2 * k2 + kl
                base = slot * 2048 + kl * 1024
                wh = wring[:, base:base + 512]
                wl = wring[:, base + 512:base + 1024]
                xh_ = uf_hi[:, k * S + ck * CH:k * S + (ck + 1) * CH]
                xl_ = uf_lo[:, k * S + ck * CH:k * S + (ck + 1) * CH]
                r = e.matmul(ps[zb][:, 0:512], xh_, wh,
                             start=(k == 0), stop=False,
                             skip_group_check=True)
                r = e.matmul(ps[zb][:, 0:512], xl_, wh, start=False,
                             stop=False, skip_group_check=True)
                r = e.matmul(ps[zb][:, 0:512], xh_, wl, start=False,
                             stop=(k == KD - 1), skip_group_check=True)
            return r
        p.emit("pe", f_zmm, reads=[B_wr[slot], B_ufh, B_ufl],
               writes=[PB[zb]])
    dec_t, B_dec = g("dec"), g("B_dec")
    p.emit("dve", lambda e, g8=g8, zb=zb: e.tensor_tensor(
        out=zt[:, 0:512], in0=ps[zb][:, 0:512],
        in1=dec_t[:, g8 * 512:(g8 + 1) * 512], op=Alu.add),
        reads=[PB[zb], B_dec], writes=[B_zt])
    p.emit("act", lambda e: e.activation(
        out=zt[:, 0:512], in_=zt[:, 0:512], func=Act.Silu),
        reads=[B_zt], writes=[B_zt])

    for kt in range(4):
        ht = g8 * 4 + kt
        ygs = yg[:, ht * 128:(ht + 1) * 128]
        ygs3 = ygs.rearrange("p (a b) -> p a b", b=64)
        eab = (mf[:, MF_ADT + 96 + 2 * ht:MF_ADT + 96 + 2 * ht + 2]
               .unsqueeze(2).broadcast_to([128, 2, 64]))
        dab = (drep2[:, l * NH + 2 * ht:l * NH + 2 * ht + 2]
               .unsqueeze(2).broadcast_to([128, 2, 64]))
        p.emit("dve", lambda e, kt=kt, ygs3=ygs3, eab=eab: e.tensor_tensor(
            out=ygs3, in0=ps[3][:, kt * 128:(kt + 1) * 128]
            .rearrange("p (a b) -> p a b", b=64), in1=eab, op=Alu.mult),
            reads=[PB[3], B_adt[0]], writes=[B_yg])
        p.emit("dve", lambda e, kt=kt, ygs=ygs: e.tensor_tensor(
            out=ygs, in0=ygs, in1=ps[2][:, kt * 128:(kt + 1) * 128],
            op=Alu.add),
            reads=[PB[2], B_yg], writes=[B_yg])
        p.emit("dve", lambda e, ht=ht, dab=dab: e.tensor_tensor(
            out=s_ch[:, 0:128].rearrange("p (a b) -> p a b", b=64),
            in0=mf[:, MF_XTF + ht * 128:MF_XTF + (ht + 1) * 128]
            .rearrange("p (a b) -> p a b", b=64),
            in1=dab, op=Alu.mult),
            reads=[B_xtf[0], B_dr2], writes=[B_sch])
        p.emit("pool", lambda e, ygs=ygs: e.tensor_tensor(
            out=ygs, in0=ygs, in1=s_ch[:, 0:128], op=Alu.add),
            reads=[B_sch, B_yg], writes=[B_yg])
        p.emit("dve", lambda e, kt=kt, ygs=ygs: e.tensor_tensor(
            out=ygs, in0=ygs, in1=zt[:, kt * 128:(kt + 1) * 128],
            op=Alu.mult),
            reads=[B_zt, B_yg], writes=[B_yg])


def emit_outproj_chunk(E, sw, b, l, ck):
    p = E["p"]
    g = lambda k: E[k]
    mb, mf = g("mb"), g("mf")
    ps, PB = g("ps"), g("PB")
    wring, B_wr = g("wring"), g("B_wr")
    cout_, B_cout = g("cout_"), g("B_cout")
    s_ch, B_sch = g("s_ch"), g("B_sch")
    h_t, B_ht = g("h_t"), g("B_ht")
    ident, B_id = g("ident"), g("B_id")
    yg, B_yg = g("yg"), g("B_yg")
    stat, B_stat = g("stat"), g("B_stat")
    cst, B_cst = g("cst"), g("B_cst")
    B_ygh, B_ygl = g("B_ygh"), g("B_ygl")
    B_ygfh, B_ygfl = g("B_xth"), g("B_xtl")
    MB_YGH, MB_YGL = g("MB_YGH"), g("MB_YGL")
    MB_XTH, MB_XTL = g("MB_XTH"), g("MB_XTL")

    B_xbs = g("B_xbs")
    MF_XBS = g("MF_XBS")
    p.emit("act", lambda e: e.activation(
        out=g("mf")[:, MF_XBS:MF_XBS + DIN], in_=yg[:, 0:DIN],
        func=Act.Square, accum_out=stat[:, 520:521]),
        reads=[B_yg], writes=[B_xbs[0], B_stat])
    p.emit("act", lambda e: e.activation(
        out=stat[:, 521:522], in_=stat[:, 520:521], func=Act.Sqrt,
        scale=float(1.0 / DIN), bias=cst[:, 3:4]),
        reads=[B_stat, B_cst], writes=[B_stat])
    p.emit("dve", lambda e: e.reciprocal(
        out=stat[:, 522:523], in_=stat[:, 521:522]),
        reads=[B_stat], writes=[B_stat])
    p.emit("dve", lambda e: e.tensor_scalar(
        out=yg[:, 0:DIN], in0=yg[:, 0:DIN], scalar1=stat[:, 522:523],
        scalar2=None, op0=Alu.mult),
        reads=[B_stat, B_yg], writes=[B_yg])
    ps6_ = g("ps")
    for tq in range(4):
        def f_ty(e, tq=tq):
            r = None
            for i in range(4):
                t_ = tq * 4 + i
                r = e.transpose(ps[6 + tq % 2][:, i * 128:(i + 1) * 128],
                                yg[:, t_ * 128:(t_ + 1) * 128], ident[:])
            return r
        p.emit("pe", f_ty, reads=[B_yg, B_id], writes=[PB[6 + tq % 2]])
        p.emit("dve", lambda e, tq=tq: e.tensor_copy(
            g("mf")[:, g("MF_XTF") + tq * 512:g("MF_XTF") + (tq + 1) * 512],
            ps[6 + tq % 2][:, 0:512]),
            reads=[PB[6 + tq % 2]], writes=[g("B_xtf")[0]])
    p.emit("act", lambda e: e.copy(
        out=mb[:, MB_XTH:MB_XTH + DIN],
        in_=g("mf")[:, g("MF_XTF"):g("MF_XTF") + DIN]),
        reads=[g("B_xtf")[0]], writes=[B_ygfh[0]])
    p.emit("dve", lambda e: e.scalar_tensor_tensor(
        out=mb[:, MB_XTL:MB_XTL + DIN],
        in0=g("mf")[:, g("MF_XTF"):g("MF_XTF") + DIN],
        scalar=1.0, in1=mb[:, MB_XTH:MB_XTH + DIN],
        op0=Alu.mult, op1=Alu.subtract),
        reads=[g("B_xtf")[0], B_ygfh[0]], writes=[B_ygfl[0]])

    for slab in range(2):
        pb = slab
        for k2 in range(KDIN // 2):
            slot = E.setdefault("wslot", 0)
            E["wslot"] = 1 - slot
            p.emit("sp", lambda e, l=l, slab=slab, k2=k2, slot=slot:
                   e.dma_start(
                       out=wring[:, slot * 2048:slot * 2048 + 2048],
                       in_=g("cout_z")[l, slab, k2]),
                   reads=[B_cout], writes=[B_wr[slot]], dma=1)

            def f_mm(e, k2=k2, slot=slot, pb=pb):
                r = None
                for kl in range(2):
                    k = 2 * k2 + kl
                    base = slot * 2048 + kl * 1024
                    wh = wring[:, base:base + 512]
                    wl = wring[:, base + 512:base + 1024]
                    yh = mb[:, MB_XTH + k * 128:MB_XTH + (k + 1) * 128]
                    yl = mb[:, MB_XTL + k * 128:MB_XTL + (k + 1) * 128]
                    r = e.matmul(ps[pb][:, 0:512], yh, wh,
                                 start=(k == 0), stop=False,
                                 skip_group_check=True)
                    r = e.matmul(ps[pb][:, 0:512], yl, wh, start=False,
                                 stop=False, skip_group_check=True)
                    r = e.matmul(ps[pb][:, 0:512], yh, wl, start=False,
                                 stop=(k == KDIN - 1),
                                 skip_group_check=True)
                return r
            p.emit("pe", f_mm, reads=[B_wr[slot], B_ygfh[0], B_ygfl[0]],
                   writes=[PB[pb]])

        def f_add(e, slab=slab, pb=pb, ck=ck):
            return e.scalar_tensor_tensor(
                out=h_t[:, ck * D + slab * 512:ck * D + (slab + 1) * 512],
                in0=ps[pb][:, 0:512], scalar=1.0,
                in1=h_t[:, ck * D + slab * 512:ck * D + (slab + 1) * 512],
                op0=Alu.mult, op1=Alu.add)
        p.emit("dve", f_add, reads=[PB[pb], B_ht], writes=[B_ht])


def build_full(n_sweeps=8, sweep=512):
    E = build_program(n_sweeps, sweep)
    emit_all(E)
    E["nc"].finalize()
    return E["nc"]


def kernel(**inputs):
    xs = np.ascontiguousarray(np.asarray(inputs["L0_attn_x"], dtype=np.float32))
    in_t = np.ascontiguousarray(
        np.transpose(np.asarray(inputs["L0_attn_in_proj_w"]), (0, 2, 1)))
    out_t = np.ascontiguousarray(
        np.transpose(np.asarray(inputs["L0_attn_out_proj_w"]), (0, 2, 1)))
    common = {
        "w_in_t": in_t,
        "w_out_t": out_t,
        "conv_w": np.ascontiguousarray(inputs["L0_attn_conv_w"]),
        "conv_b": np.ascontiguousarray(inputs["L0_attn_conv_b"]),
        "dt_bias": np.ascontiguousarray(inputs["L0_attn_dt_bias"]),
        "a_log": np.ascontiguousarray(inputs["L0_attn_A_log"]),
        "d_vec": np.ascontiguousarray(inputs["L0_attn_D"]),
        "rms_w": np.ascontiguousarray(inputs["L0_attn_rms_w"]),
        "ln_w": np.ascontiguousarray(inputs["L0_attn_ln_w"]),
        "ln_b": np.ascontiguousarray(inputs["L0_attn_ln_b"]),
        "w_fc": np.ascontiguousarray(inputs["L0_ffn_w_fc"]),
        "b_fc": np.ascontiguousarray(inputs["L0_ffn_b_fc"]),
        "w_proj": np.ascontiguousarray(inputs["L0_ffn_w_proj"]),
        "b_proj": np.ascontiguousarray(inputs["L0_ffn_b_proj"]),
    }
    in_maps = []
    for c in range(N_CORES):
        m = dict(common)
        m["x"] = np.ascontiguousarray(xs[2 * c:2 * c + 2].reshape(2 * L_FULL, D))
        in_maps.append(m)
    nc = build_full()
    res = run_bass_kernel_spmd(nc, in_maps, core_ids=list(range(N_CORES)))
    out = np.concatenate(
        [np.asarray(r["y"]).reshape(2, L_FULL, D) for r in res.results], axis=0)
    return out


# version 06
# speedup vs baseline: 1.2601x; 1.1067x over previous
import os as _os
import numpy as np
import ml_dtypes

_DBG_NL = int(_os.environ.get("KDBG_NL", "2"))
_DBG_FFN = int(_os.environ.get("KDBG_FFN", "1"))

import concourse.bass as bass
import concourse.mybir as mybir
from concourse.bass_utils import run_bass_kernel_spmd

F32 = mybir.dt.float32
BF16 = mybir.dt.bfloat16
Act = mybir.ActivationFunctionType
Alu = mybir.AluOpType
AX = mybir.AxisListType

D = 1024
DIN = 2048
DST = 64
NH = 32
HD = 64
CONVDIM = DIN + 2 * DST
DPROJ = 2 * DIN + 2 * DST + NH
HID = 4096
BK = 32
EPS = 1e-5
LN2 = float(np.log(2.0))
N_CORES = 8
B_FULL, L_FULL = 16, 2048
CH = 128

KD = D // 128
NDC = (DPROJ + 127) // 128
KDIN = DIN // 128
NOC = D // 128
NXT = (CONVDIM + 127) // 128
NHT = DIN // 128



class Step:
    __slots__ = ("fn", "waits", "inc")

    def __init__(self, fn, waits, inc):
        self.fn, self.waits, self.inc = fn, waits, inc


class Buf:
    __slots__ = ("writer", "readers", "name")

    def __init__(self, name=""):
        self.writer = None
        self.readers = []
        self.name = name


class Prog:
    ENGS = ("pe", "dve", "act", "pool", "sp")

    def __init__(self, nc):
        self.nc = nc
        self.steps = {e: [] for e in self.ENGS}
        self.count = {e: 0 for e in self.ENGS}
        self.sems = {}
        self._guards = []
        self.bufsem = {}
        self.nbufsem = 0
        self.waited = {e: {} for e in self.ENGS}

    def sbuf(self, name, shape, dtype):
        g = self.nc.sbuf_tensor(name, list(shape), dtype)
        t = g.__enter__()
        self._guards.append(g)
        return t

    def psum(self, name, shape, dtype=F32):
        g = self.nc.psum_tensor(name, list(shape), dtype)
        t = g.__enter__()
        self._guards.append(g)
        return t

    def emit(self, eng, fn, reads=(), writes=(), dma=0):
        waits = {}
        wd = self.waited[eng]

        def add(dep):
            if dep is None:
                return
            de, dv = dep
            if wd.get(de, 0) >= dv:
                return
            if waits.get(de, 0) < dv:
                waits[de] = dv

        for b in reads:
            add(b.writer)
        for b in writes:
            add(b.writer)
            for r in b.readers:
                add(r)
        for de, dv in waits.items():
            wd[de] = dv
        if dma:
            assert len(writes) == 1, "dma step must write exactly one Buf"
            bkey = id(writes[0])
            sname = self.bufsem.get(bkey)
            if sname is None:
                sname = "D%d" % self.nbufsem
                self.nbufsem += 1
                self.bufsem[bkey] = sname
                self.count[sname] = 0
                self.steps.setdefault(sname, None)
            inc = 16 * dma
            self.count[sname] += inc
            tag = (sname, self.count[sname])
            for b in reads:
                b.readers.append(tag)
            writes[0].writer = tag
            writes[0].readers = []
            self.steps[eng].append(Step(fn, waits, (sname, inc)))
            return tag
        self.count[eng] += 1
        tag = (eng, self.count[eng])
        for b in reads:
            b.readers.append(tag)
        for b in writes:
            b.writer = tag
            b.readers = []
        self.steps[eng].append(Step(fn, waits, 1))
        return tag

    def barrier(self):
        snap = dict(self.count)
        for e in self.ENGS:
            waits = {d: v for d, v in snap.items() if v > 0}
            self.count[e] += 1
            self.steps[e].append(Step(lambda en: en.nop(), waits, 1))
            self.waited[e] = dict(waits)

    def build(self):
        nc = self.nc
        for e in list(self.count):
            g = nc.semaphore("S" + e)
            self.sems[e] = g.__enter__()
            self._guards.append(g)
        blk_g = nc.Block()
        blk = blk_g.__enter__()

        def runner(name):
            def run(e):
                cnt = 0
                for st in self.steps[name]:
                    if not st.waits:
                        e.wait_ge(self.sems[name], cnt)
                    for de, dv in sorted(st.waits.items()):
                        e.wait_ge(self.sems[de], dv)
                    if not isinstance(st.inc, tuple):
                        cnt += st.inc
                    last = st.fn(e)
                    if isinstance(st.inc, tuple):
                        sname, inc = st.inc
                        insts = last if isinstance(last, (list, tuple)) else [last]
                        assert len(insts) * 16 == inc
                        for i in insts:
                            i.then_inc(self.sems[sname], 16)
                    else:
                        if isinstance(last, (list, tuple)):
                            last = last[-1]
                        last.then_inc(self.sems[name], 1)
            return run

        blk.tensor(runner("pe"))
        blk.vector(runner("dve"))
        blk.scalar(runner("act"))
        blk.gpsimd(runner("pool"))
        blk.sync(runner("sp"))
        blk_g.__exit__(None, None, None)



def quant_ladder(p, eng, buf_io, src_ap, dst_ap, inv_ap, sc_ap, t_ap, b_ap,
                 nlv=7, span=32.0):
    reads, writes = buf_io
    lv = [2 ** (nlv - 1 - i) for i in range(nlv)]

    def f_mul(e):
        return e.tensor_tensor(out=t_ap, in0=src_ap, in1=inv_ap, op=Alu.mult)
    p.emit(eng, f_mul, reads=reads, writes=[writes[0]])

    def f_add(e):
        return e.tensor_scalar(out=t_ap, in0=t_ap, scalar1=span + 0.5,
                               scalar2=None, op0=Alu.add)
    p.emit(eng, f_add, reads=[writes[0]], writes=[writes[0]])
    for c in lv:
        def f_ge(e, c=c):
            return e.tensor_scalar(out=b_ap, in0=t_ap, scalar1=float(c),
                                   scalar2=None, op0=Alu.is_ge)
        p.emit(eng, f_ge, reads=[writes[0]], writes=[writes[1]])

        def f_up(e, c=c):
            return e.scalar_tensor_tensor(out=t_ap, in0=b_ap, scalar=float(-c),
                                          in1=t_ap, op0=Alu.mult, op1=Alu.add)
        p.emit(eng, f_up, reads=[writes[1], writes[0]], writes=[writes[0]])

    def f_u2(e):
        return e.tensor_tensor(out=b_ap, in0=src_ap, in1=inv_ap, op=Alu.mult)
    p.emit(eng, f_u2, reads=reads + [writes[0]], writes=[writes[1]])

    def f_d(e):
        return e.scalar_tensor_tensor(out=t_ap, in0=b_ap, scalar=0.5,
                                      in1=t_ap, op0=Alu.add, op1=Alu.subtract)
    p.emit(eng, f_d, reads=[writes[1], writes[0]], writes=[writes[0]])

    def f_q(e):
        return e.tensor_tensor(out=dst_ap, in0=t_ap, in1=sc_ap, op=Alu.mult)
    p.emit(eng, f_q, reads=[writes[0]], writes=[writes[2]])


def exp_ladder(p, eng, rw, lg_ap, e_out_ap, t_ap, b_ap, nlv=9, span=256.0):
    reads, wT, wB, wE = rw
    def f0(e):
        return e.tensor_scalar(out=t_ap, in0=lg_ap, scalar1=span, scalar2=None,
                               op0=Alu.add)
    p.emit(eng, f0, reads=reads, writes=[wT])
    for i in range(nlv):
        c = 2 ** (nlv - 1 - i)
        def f_ge(e, c=c):
            return e.tensor_scalar(out=b_ap, in0=t_ap, scalar1=float(c),
                                   scalar2=None, op0=Alu.is_ge)
        p.emit(eng, f_ge, reads=[wT], writes=[wB])

        def f_up(e, c=c):
            return e.scalar_tensor_tensor(out=t_ap, in0=b_ap, scalar=float(-c),
                                          in1=t_ap, op0=Alu.mult, op1=Alu.add)
        p.emit(eng, f_up, reads=[wB, wT], writes=[wT])
    def f_e(e):
        return e.scalar_tensor_tensor(out=e_out_ap, in0=lg_ap, scalar=1.0,
                                      in1=t_ap, op0=Alu.mult, op1=Alu.subtract)
    p.emit(eng, f_e, reads=reads + [wT], writes=[wE])



def build_program(n_sweeps=8, sweep=512):
    NT = sweep // 128
    NCH = sweep // CH
    TOK = n_sweeps * sweep
    HSL = HID // 512
    PSL = D // 512

    nc = bass.Bass("TRN2", target_bir_lowering=False)
    p = Prog(nc)
    dt_ = nc.dram_tensor

    x_in = dt_("x", [TOK, D], F32, kind="ExternalInput").ap()
    w_in_t = dt_("w_in_t", [2, D, DPROJ], F32, kind="ExternalInput").ap()
    w_out_t = dt_("w_out_t", [2, DIN, D], F32, kind="ExternalInput").ap()
    conv_w = dt_("conv_w", [2, CONVDIM, 4], F32, kind="ExternalInput").ap()
    conv_b = dt_("conv_b", [2, CONVDIM], F32, kind="ExternalInput").ap()
    dt_bias = dt_("dt_bias", [2, NH], F32, kind="ExternalInput").ap()
    a_log = dt_("a_log", [2, NH], F32, kind="ExternalInput").ap()
    d_vec = dt_("d_vec", [2, NH], F32, kind="ExternalInput").ap()
    rms_w = dt_("rms_w", [2, DIN], F32, kind="ExternalInput").ap()
    ln_w = dt_("ln_w", [2, D], F32, kind="ExternalInput").ap()
    ln_b = dt_("ln_b", [2, D], F32, kind="ExternalInput").ap()
    w_fc = dt_("w_fc", [HID, D], F32, kind="ExternalInput").ap()
    b_fc = dt_("b_fc", [HID], F32, kind="ExternalInput").ap()
    w_proj = dt_("w_proj", [D, HID], F32, kind="ExternalInput").ap()
    b_proj = dt_("b_proj", [D], F32, kind="ExternalInput").ap()
    y_out = dt_("y", [TOK, D], F32, kind="ExternalOutput").ap()

    cin = dt_("cin", [2, NDC, KD, 2, 128, 128], BF16).ap()
    cin_z = dt_("cin_z", [2, 4, KD // 2, 128, 2048], BF16).ap()
    cout_z = dt_("cout_z", [2, 2, KDIN // 2, 128, 2048], BF16).ap()
    cout_ = dt_("cout", [2, NOC, KDIN, 2, 128, 128], BF16).ap()
    cfc = dt_("cfc", [KD, HSL, 128, 512], BF16).ap()
    cproj = dt_("cproj", [HID // 128, PSL, 128, 512], BF16).ap()
    cbv = dt_("cbv", [2, NDC * 128], F32).ap()
    dr_a = dt_("dr_a", [NH, sweep], F32).ap()
    dr_ea = dt_("dr_ea", [NH, sweep], F32).ap()
    dr_rs = dt_("dr_rs", [sweep], F32).ap()
    B_cin, B_cout, B_cfc, B_cproj, B_cbv = (Buf(), Buf(), Buf(), Buf(), Buf())
    B_dra, B_drea, B_drrs, B_y = Buf(), Buf(), Buf(), Buf()

    sb = p.sbuf
    h_t = sb("h_t", [128, NT * D], F32); B_ht = Buf()
    state = sb("state", [128, 2 * DIN], F32); B_st = [Buf(), Buf()]
    stk = sb("stk", [96, sweep], F32); B_stk = Buf()
    dts = sb("dts", [32, 3 * sweep], F32)
    B_dt0 = Buf(); B_dt1 = Buf(); B_dt2 = Buf()
    ident = sb("ident", [128, 128], F32); B_id = Buf()
    identb = sb("identb", [128, 128], BF16); B_idb = Buf()
    mask01 = sb("mask01", [128, CH], F32); B_mk = Buf()
    imask = sb("imask", [128, CH], mybir.dt.int32); B_imk = Buf()
    ccol = sb("ccol", [128, 16], F32); B_cc = Buf()
    drep = sb("drep", [128, 2 * NHT], F32); B_dr = Buf()
    cbcol = sb("cbcol", [128, 2 * NXT], F32); B_cb = Buf()
    bvcol = sb("bvcol", [128, 2 * NDC], F32); B_bv = Buf()
    convc = sb("convc", [128, 2 * 2 * NXT * 4], F32)
    B_cvc = [[Buf(), Buf()], [Buf(), Buf()]]
    cw = sb("cw", [128, 2 * NXT * 4], F32); B_cw = Buf()
    B_exp = Buf("exp")
    misc = sb("misc", [128, 1024], F32); B_misc = Buf()
    B_bt = Buf()
    stat = sb("stat", [128, 768], F32); B_stat = Buf()
    aeb = sb("aeb", [128, 64], F32); B_aeb = Buf()
    wTt = sb("wTt", [128, 32], F32); B_wT = Buf()
    abg = sb("abg", [128, 512], F32); B_abg = [Buf()]
    uf_hi = sb("uf_hi", [128, KD * sweep], BF16); B_ufh = Buf()
    uf_lo = sb("uf_lo", [128, KD * sweep], BF16); B_ufl = Buf()
    if KD * sweep >= HID:
        hqs = uf_lo[:, 0:HID]
    else:
        hqs = sb("hqs", [128, HID], BF16)
    wring = sb("wring", [128, 4096], BF16)
    B_wr = [Buf(), Buf()]
    wf = wring[:, 0:2048]
    B_wf = [Buf() for _ in range(4)]
    B_hqs = Buf()
    cst = sb("cst", [128, 4], F32)
    B_cst = Buf()
    drep2 = sb("drep2", [128, 2 * NH], F32)
    B_dr2 = Buf()
    mb = sb("mb", [128, 16384], BF16); B_mb_hqT = Buf()
    uh = mb[:, 0:NT * D]; B_uh = Buf()
    ul = mb[:, NT * D:2 * NT * D]; B_ul = Buf()
    MB_XHH, MB_XHL = 0, 2048
    MB_XTH, MB_XTL = 4096, 6144
    MB_YGH, MB_YGL = 8192, 10240
    MB_BPK, MB_CPA, MB_CPB = 12288, 12416, 12544
    MB_STH, MB_STL = 12672, 13184
    B_xhh = [Buf(), Buf()]; B_xhl = [Buf(), Buf()]
    B_xth = [Buf(), Buf()]; B_xtl = [Buf(), Buf()]
    B_ygh = Buf(); B_ygl = Buf()
    B_bpk = Buf(); B_cpa = Buf(); B_cpb = Buf()
    B_sth = [Buf() for _ in range(4)]; B_stl = [Buf() for _ in range(4)]
    mf = sb("mf", [128, 8192], F32)
    MF_XBS, MF_XTF, MF_BW = 0, 2176, 4224
    MF_STF, MF_ST0, MF_BCT, MF_ADT, MF_WRK = 4736, 5248, 5376, 5440, 5568
    B_xbs = [Buf(), Buf()]; B_xtf = [Buf(), Buf()]; B_ea = Buf(); B_bw = Buf()
    B_stf = [Buf() for _ in range(4)]; B_st0 = Buf(); B_bcT = Buf()
    B_adt = [Buf(), Buf()]; B_wrk = Buf()
    xbc_r = sb("xbc_r", [128, NXT * (sweep + 4)], F32); B_xbr = Buf()
    dec = sb("dec", [128, DIN], F32); B_dec = Buf()
    s_ch = sb("s_ch", [128, 512], F32); B_sch = Buf()
    yg = sb("yg", [128, NHT * CH], F32); B_yg = Buf()
    zt = sb("zt", [128, 512], F32); B_zt = Buf()
    btile = zt
    sg = sb("sg", [128, 1024], F32); B_sg = Buf()
    xr = sb("xr", [128, 1024], F32); B_xr = Buf()

    ps = [p.psum(f"ps{i}", [128, 512]) for i in range(8)]
    PB = [Buf(f"ps{i}") for i in range(8)]

    E = dict(locals())
    E["p"] = p
    E["n_sweeps"] = n_sweeps
    E["sweep"] = sweep
    E["NT"] = NT
    E["NCH"] = NCH
    E["HSL"] = HSL
    E["PSL"] = PSL
    return E


def emit_all(E):
    emit_consts(E)
    emit_transform(E)
    E["p"].barrier()
    n_sweeps = E["n_sweeps"]
    for sw in range(n_sweeps):
        b = 0 if sw < n_sweeps // 2 else 1
        emit_sweep(E, sw, b)
    p = E["p"]
    p.emit("sp", lambda e: e.nop(), reads=[E["B_y"]])
    p.build()


def emit_consts(E):
    p = E["p"]
    g = lambda k: E[k]
    imask, mask01, ident = g("imask"), g("mask01"), g("ident")
    B_imk, B_mk, B_id = g("B_imk"), g("B_mk"), g("B_id")
    ccol, B_cc = g("ccol"), g("B_cc")
    drep, B_dr = g("drep"), g("B_dr")
    cbcol, B_cb = g("cbcol"), g("B_cb")
    misc, B_misc = g("misc"), g("B_misc")
    convc = g("convc")
    state, B_st = g("state"), g("B_st")
    stat, B_stat = g("stat"), g("B_stat")
    a_log, dt_bias, d_vec, conv_b, b_proj = (
        g("a_log"), g("dt_bias"), g("d_vec"), g("conv_b"), g("b_proj"))

    cst, B_cst = g("cst"), g("B_cst")
    p.emit("dve", lambda e: e.memset(cst[:, 0:1], 1.0), writes=[B_cst])
    p.emit("dve", lambda e: e.memset(cst[:, 1:2], float(3.0 * LN2)),
           writes=[B_cst])
    p.emit("dve", lambda e: e.memset(cst[:, 2:3], float(-3.0 * LN2)),
           writes=[B_cst])
    p.emit("dve", lambda e: e.memset(cst[:, 3:4], float(EPS)), writes=[B_cst])
    p.emit("pool", lambda e: e.iota(
        imask[:], pattern=[[1, CH]], base=0, channel_multiplier=-1),
        writes=[B_imk])
    p.emit("dve", lambda e: e.tensor_scalar(
        out=mask01[:], in0=imask[:], scalar1=0.0, scalar2=None, op0=Alu.is_ge),
        reads=[B_imk], writes=[B_mk])
    p.emit("dve", lambda e: e.tensor_scalar(
        out=ident[:], in0=imask[:], scalar1=0.0, scalar2=None, op0=Alu.is_equal),
        reads=[B_imk], writes=[B_id])
    p.emit("act", lambda e: e.copy(out=g("identb")[:], in_=ident[:]),
           reads=[B_id], writes=[g("B_idb")])
    p.emit("dve", lambda e: e.memset(state[:], 0.0), writes=[B_st[0], B_st[1]])
    p.emit("dve", lambda e: e.memset(convc[:], 0.0),
           writes=[b for lb in E["B_cvc"] for b in lb])
    for l in range(2):
        p.emit("sp", lambda e, l=l: [
            e.dma_start(out=ccol[0:NH, 4 + l:5 + l],
                        in_=a_log[l, :].unsqueeze(1)),
            e.dma_start(out=ccol[0:NH, 6 + l:7 + l],
                        in_=dt_bias[l, :].unsqueeze(1)),
        ], writes=[E["B_cc"]], dma=2)
        p.emit("act", lambda e, l=l: e.activation(
            out=ccol[0:NH, 2 + l:3 + l], in_=ccol[0:NH, 4 + l:5 + l],
            func=Act.Exp), reads=[B_cc], writes=[B_cc])
        p.emit("dve", lambda e, l=l: e.tensor_scalar(
            out=ccol[0:NH, l:l + 1], in0=ccol[0:NH, 2 + l:3 + l],
            scalar1=-1.0, scalar2=None, op0=Alu.mult),
            reads=[B_cc], writes=[B_cc])
        p.emit("sp", lambda e, l=l: [
            e.dma_start(
                out=drep[0:64, l * NHT:(l + 1) * NHT],
                in_=d_vec[l, :].rearrange("(t h) -> h t", h=2)[0]
                .unsqueeze(0).broadcast_to([64, NHT]),
                allow_slow_non_contiguous=True),
            e.dma_start(
                out=drep[64:128, l * NHT:(l + 1) * NHT],
                in_=d_vec[l, :].rearrange("(t h) -> h t", h=2)[1]
                .unsqueeze(0).broadcast_to([64, NHT]),
                allow_slow_non_contiguous=True)],
            writes=[B_dr], dma=2)
        p.emit("sp", lambda e, l=l: e.dma_start(
            out=g("drep2")[:, l * NH:(l + 1) * NH],
            in_=d_vec[l, :].unsqueeze(0).broadcast_to([128, NH])),
            writes=[g("B_dr2")], dma=1)
        p.emit("sp", lambda e, l=l: e.dma_start(
            out=cbcol[:, l * NXT:(l + 1) * NXT],
            in_=conv_b[l, :].rearrange("(a b) -> b a", b=128),
            allow_slow_non_contiguous=True),
            writes=[B_cb], dma=1)
        p.emit("sp", lambda e, l=l: e.dma_start(
            out=g("cw")[:, l * NXT * 4:(l + 1) * NXT * 4]
            .rearrange("p (t k) -> p t k", k=4),
            in_=g("conv_w")[l].rearrange("(t p) k -> p t k", p=128)),
            writes=[g("B_cw")], dma=1)
    p.emit("sp", lambda e: e.dma_start(
        out=misc[:, 0:D],
        in_=b_proj[:].unsqueeze(0).broadcast_to([128, D])),
        writes=[B_misc], dma=1)


def _wsplit_store(p, E, src_f32_ap, ncols, scol_ap, hi_ap, lo_ap,
                  rbufs, wbufs):
    mf, B = E["mf"], None

    def f_sc(e):
        return e.tensor_scalar(out=src_f32_ap, in0=src_f32_ap,
                               scalar1=scol_ap, scalar2=None, op0=Alu.mult)
    p.emit("dve", f_sc, reads=rbufs, writes=[wbufs[0]])

    def f_hi(e):
        return e.copy(out=hi_ap, in_=src_f32_ap)
    p.emit("act", f_hi, reads=[wbufs[0]], writes=[wbufs[1]])

    def f_lo(e):
        return e.scalar_tensor_tensor(out=lo_ap, in0=src_f32_ap, scalar=1.0,
                                      in1=hi_ap, op0=Alu.mult, op1=Alu.subtract)
    p.emit("dve", f_lo, reads=[wbufs[0], wbufs[1]], writes=[wbufs[2]])


def emit_transform(E):
    p = E["p"]
    g = lambda k: E[k]
    nc = g("nc")
    mf, mb, xbc_r, stat = g("mf"), g("mb"), g("xbc_r"), g("stat")
    B_wrk, B_xbr, B_stat = g("B_wrk"), g("B_xbr"), g("B_stat")
    B_mbs = Buf("mb_scr")
    ps, PB = g("ps"), g("PB")
    w_in_t, ln_w, ln_b = g("w_in_t"), g("ln_w"), g("ln_b")
    w_out_t, rms_w = g("w_out_t"), g("rms_w")
    w_fc, w_proj = g("w_fc"), g("w_proj")
    cin, cout_, cfc, cproj, cbv = g("cin"), g("cout_"), g("cfc"), g("cproj"), g("cbv")
    cin_z, cout_z = g("cin_z"), g("cout_z")
    B_cin, B_cout, B_cfc, B_cproj, B_cbv = (
        g("B_cin"), g("B_cout"), g("B_cfc"), g("B_cproj"), g("B_cbv"))
    bvcol, B_bv = g("bvcol"), g("B_bv")
    ccol, B_cc = g("ccol"), g("B_cc")
    s_ch, B_sch = g("s_ch"), g("B_sch")
    dec, B_dec = g("dec"), g("B_dec")

    p.emit("dve", lambda e: e.memset(mb[:, 4256:4352], 0.0), writes=[B_mbs])
    p.emit("dve", lambda e: e.memset(mb[:, 4352 + 4256:4352 + 4352], 0.0),
           writes=[B_mbs])

    for l in range(2):
        for k in range(KD):
            p.emit("sp", lambda e, l=l, k=k: e.dma_start(
                out=mf[:, 0:DPROJ], in_=w_in_t[l, k * 128:(k + 1) * 128, :]),
                writes=[B_wrk], dma=1)
            p.emit("sp", lambda e, l=l, k=k: [
                e.dma_start(out=stat[:, 0:1],
                            in_=ln_w[l, k * 128:(k + 1) * 128].unsqueeze(1)),
                e.dma_start(out=stat[:, 1:2],
                            in_=ln_b[l, k * 128:(k + 1) * 128].unsqueeze(1))],
                writes=[B_stat], dma=2)

            def f_bv(e, l=l, k=k):
                r = None
                for nt in range(8):
                    r = e.matmul(ps[nt][0:1, :], stat[:, 1:2],
                                 mf[:, nt * 512:(nt + 1) * 512],
                                 start=(k == 0), stop=(k == KD - 1))
                return r
            p.emit("pe", f_bv, reads=[B_wrk, B_stat],
                   writes=[PB[nt] for nt in range(8)])

            _wsplit_store(p, E, mf[:, 0:DPROJ], DPROJ, stat[:, 0:1],
                          mb[:, 0:DPROJ], mb[:, 4352:4352 + DPROJ],
                          [B_stat], [B_wrk, B_mbs, B_mbs])

            def f_st(e, l=l, k=k):
                ops = []
                for h in range(2):
                    hoff = 4352 * h
                    a0 = (k % 2) * 2 + h
                    ops.append(e.dma_start(
                        out=cin_z[l, :, k // 2, :, a0 * 512:(a0 + 1) * 512]
                        .rearrange("g p c -> p g c"),
                        in_=mb[:, hoff:hoff + 2048]
                        .rearrange("p (g c) -> p g c", c=512)))
                    ops.append(e.dma_start(
                        out=cin[l, 16:NDC, k, h].rearrange("c p n -> p c n"),
                        in_=mb[:, hoff + 2048:hoff + 4352]
                        .rearrange("p (c n) -> p c n", n=128)))
                return ops
            p.emit("sp", f_st, reads=[B_mbs], writes=[B_cin], dma=4)
        for nt in range(8):
            def f_ev(e, nt=nt):
                return e.tensor_copy(s_ch[0:1, :], ps[nt][0:1, :])
            p.emit("dve", f_ev, reads=[PB[nt]], writes=[B_sch])

            def f_dst(e, l=l, nt=nt):
                return e.dma_start(out=cbv[l, nt * 512:(nt + 1) * 512]
                                   .unsqueeze(0), in_=s_ch[0:1, :])
            p.emit("act", f_dst, reads=[B_sch], writes=[B_cbv], dma=1)
        for k in range(KD):
            p.emit("sp", lambda e, l=l, k=k: e.dma_start(
                out=dec[:, 0:160],
                in_=w_in_t[l, k * 128:(k + 1) * 128, 4096:4256]),
                writes=[B_dec], dma=1)
            p.emit("sp", lambda e, l=l, k=k: e.dma_start(
                out=stat[:, 1:2],
                in_=ln_b[l, k * 128:(k + 1) * 128].unsqueeze(1)),
                writes=[B_stat], dma=1)

            def f_bv2(e, l=l, k=k):
                return e.matmul(ps[0][0:1, 0:160], stat[:, 1:2], dec[:, 0:160],
                                start=(k == 0), stop=(k == KD - 1))
            p.emit("pe", f_bv2, reads=[B_dec, B_stat], writes=[PB[0]])
        p.emit("dve", lambda e: e.tensor_copy(s_ch[0:1, 0:160],
                                              ps[0][0:1, 0:160]),
               reads=[PB[0]], writes=[B_sch])
        p.emit("act", lambda e, l=l: e.dma_start(
            out=cbv[l, 4096:4256].unsqueeze(0), in_=s_ch[0:1, 0:160]),
            reads=[B_sch], writes=[B_cbv], dma=1)
        p.emit("dve", lambda e: e.memset(s_ch[0:1, 160:256], 0.0),
               writes=[B_sch])
        p.emit("act", lambda e, l=l: e.dma_start(
            out=cbv[l, 4256:4352].unsqueeze(0), in_=s_ch[0:1, 160:256]),
            reads=[B_sch], writes=[B_cbv], dma=1)
        p.emit("sp", lambda e, l=l: e.dma_start(
            out=bvcol[:, l * NDC:(l + 1) * NDC],
            in_=cbv[l, :].rearrange("(a b) -> b a", b=128),
            allow_slow_non_contiguous=True),
            reads=[B_cbv], writes=[B_bv], dma=1)
        p.emit("dve", lambda e, l=l: e.tensor_scalar(
            out=ccol[0:NH, 8 + l:9 + l], in0=bvcol[0:NH, l * NDC + 33:l * NDC + 34],
            scalar1=ccol[0:NH, 6 + l:7 + l], scalar2=None, op0=Alu.add),
            reads=[B_bv, B_cc], writes=[B_cc])

        for k in range(KDIN):
            p.emit("sp", lambda e, l=l, k=k: e.dma_start(
                out=mf[:, 0:D], in_=w_out_t[l, k * 128:(k + 1) * 128, :]),
                writes=[B_wrk], dma=1)
            p.emit("sp", lambda e, l=l, k=k: e.dma_start(
                out=stat[:, 0:1],
                in_=rms_w[l, k * 128:(k + 1) * 128].unsqueeze(1)),
                writes=[B_stat], dma=1)
            _wsplit_store(p, E, mf[:, 0:D], D, stat[:, 0:1],
                          mb[:, 0:D], mb[:, 4352:4352 + D],
                          [B_stat], [B_wrk, B_mbs, B_mbs])

            def f_so(e, l=l, k=k):
                ops = []
                for h in range(2):
                    hoff = 4352 * h
                    a0 = (k % 2) * 2 + h
                    ops.append(e.dma_start(
                        out=cout_z[l, :, k // 2, :, a0 * 512:(a0 + 1) * 512]
                        .rearrange("s p c -> p s c"),
                        in_=mb[:, hoff:hoff + D]
                        .rearrange("p (s c) -> p s c", c=512)))
                return ops
            p.emit("sp", f_so, reads=[B_mbs], writes=[B_cout], dma=2)

    for which, wap, rows, blocks, cdst, B_c in (
            ("fc", w_fc, HID // 128, D // BK, cfc, B_cfc),
            ("proj", w_proj, D // 128, HID // BK, cproj, B_cproj)):
        ncols = D if which == "fc" else HID
        nkt = ncols // 128
        for r in range(rows):
            p.emit("sp", lambda e, wap=wap, r=r, ncols=ncols: e.dma_start(
                out=mf[:, 0:ncols], in_=wap[r * 128:(r + 1) * 128, :]),
                writes=[B_wrk], dma=1)
            for q in range(ncols // 1024):
                emit_quant(E, "dve", mf[:, q * 1024:(q + 1) * 1024],
                           mb[:, q * 1024:(q + 1) * 1024], 1024,
                           [B_wrk], B_mbs,
                           stat, B_stat, mf_scr(E), B_wrk2(E))
            for kt in range(nkt):
                eng = "sp" if kt % 2 == 0 else "act"
                p.emit(eng, lambda e, kt=kt, ncols=ncols: e.dma_start_transpose(
                    mb[:, 8704 + kt * 128:8704 + (kt + 1) * 128],
                    mb[:, kt * 128:(kt + 1) * 128]),
                    reads=[B_mbs], writes=[B_mbs], dma=1)
            def f_str(e, r=r, cdst=cdst, nkt=nkt):
                return e.dma_start(
                    out=cdst[:, r // 4, :, (r % 4) * 128:(r % 4) * 128 + 128]
                    .rearrange("c p n -> p c n"),
                    in_=mb[:, 8704:8704 + nkt * 128]
                    .rearrange("p (c n) -> p c n", n=128))
            p.emit("sp", f_str, reads=[B_mbs], writes=[B_c], dma=1)


def mf_scr(E):
    return E["mf"][:, 4608:6656]


def B_wrk2(E):
    return E["B_wrk"]


def emit_quant(E, eng, src_ap, dst_ap, ncols, rbufs, B_dst, stat, B_stat,
               scr, B_scr):
    p = E["p"]
    nb = ncols // BK
    am = stat[:, 0:nb]
    am2 = stat[:, nb:2 * nb]
    lgc = stat[:, 2 * nb:3 * nb]
    ev = stat[:, 3 * nb:4 * nb]
    tt = stat[:, 4 * nb:5 * nb]
    bb = stat[:, 5 * nb:6 * nb]
    inv = stat[:, 6 * nb:7 * nb]
    sc = stat[:, 7 * nb:8 * nb]

    def f_am(e):
        return e.tensor_reduce(out=am, in_=src_ap.rearrange(
            "p (b k) -> p b k", k=BK), axis=AX.X, op=Alu.max,
            apply_absolute_value=True)
    p.emit(eng, f_am, reads=rbufs, writes=[B_stat])

    def f_am2(e):
        return e.tensor_scalar(out=am2, in0=am, scalar1=float(2.0 ** -126),
                               scalar2=None, op0=Alu.max)
    p.emit(eng, f_am2, reads=[B_stat], writes=[B_stat])

    def f_lg(e):
        return e.activation(out=lgc, in_=am2, func=Act.Ln)
    p.emit("act", f_lg, reads=[B_stat], writes=[B_stat])

    def f_lgc(e):
        return e.tensor_scalar(out=lgc, in0=lgc, scalar1=float(1.0 / LN2),
                               scalar2=None, op0=Alu.mult)
    p.emit(eng, f_lgc, reads=[B_stat], writes=[B_stat])

    exp_ladder(p, eng, ([B_stat], B_stat, B_stat, B_stat), lgc, ev, tt, bb)

    cst, B_cst = E["cst"], E["B_cst"]

    def f_inv(e):
        return e.activation(out=inv, in_=ev, func=Act.Exp, scale=-LN2,
                            bias=cst[:, 1:2])
    p.emit("act", f_inv, reads=[B_stat, B_cst], writes=[B_stat])

    def f_sc(e):
        return e.activation(out=sc, in_=ev, func=Act.Exp, scale=LN2,
                            bias=cst[:, 2:3])
    p.emit("act", f_sc, reads=[B_stat, B_cst], writes=[B_stat])

    inv_b = inv.unsqueeze(2).broadcast_to([128, nb, BK])
    sc_b = sc.unsqueeze(2).broadcast_to([128, nb, BK])
    src3 = src_ap.rearrange("p (b k) -> p b k", k=BK)
    dst3 = dst_ap.rearrange("p (b k) -> p b k", k=BK)
    quant_ladder(p, eng, (rbufs + [B_stat], [B_scr, B_scr, B_dst]),
                 src3, dst3, inv_b, sc_b,
                 scr[:, 0:ncols].rearrange("p (b k) -> p b k", k=BK),
                 scr[:, ncols:2 * ncols].rearrange("p (b k) -> p b k", k=BK))


def emit_qstats(E, units):
    p = E["p"]
    dec, B_dec = E["dec"], E["B_dec"]
    yg, B_yg = E["yg"], E["B_yg"]
    cst, B_cst = E["cst"], E["B_cst"]
    SC = len(units) * 32
    for u, (src, rbufs) in enumerate(units):
        p.emit("dve", lambda e, src=src, u=u: e.tensor_reduce(
            out=dec[:, u * 32:(u + 1) * 32],
            in_=src.rearrange("p (b k) -> p b k", k=BK), axis=AX.X,
            op=Alu.max, apply_absolute_value=True),
            reads=rbufs, writes=[B_dec])
    p.emit("dve", lambda e: e.tensor_scalar(
        out=dec[:, 0:SC], in0=dec[:, 0:SC], scalar1=float(2.0 ** -126),
        scalar2=None, op0=Alu.max), reads=[B_dec], writes=[B_dec])
    p.emit("act", lambda e: e.activation(
        out=dec[:, SC:2 * SC], in_=dec[:, 0:SC], func=Act.Ln),
        reads=[B_dec], writes=[B_dec])
    p.emit("dve", lambda e: e.tensor_scalar(
        out=dec[:, SC:2 * SC], in0=dec[:, SC:2 * SC],
        scalar1=float(1.0 / LN2), scalar2=None, op0=Alu.mult),
        reads=[B_dec], writes=[B_dec])
    exp_ladder(p, "dve", ([B_dec], B_yg, B_yg, B_dec),
               dec[:, SC:2 * SC], dec[:, SC:2 * SC],
               yg[:, 0:SC], yg[:, SC:2 * SC])
    p.emit("act", lambda e: e.activation(
        out=dec[:, 2 * SC:3 * SC], in_=dec[:, SC:2 * SC], func=Act.Exp,
        scale=-LN2, bias=cst[:, 1:2]),
        reads=[B_dec, B_cst], writes=[B_dec])
    p.emit("act", lambda e: e.activation(
        out=dec[:, 3 * SC:4 * SC], in_=dec[:, SC:2 * SC], func=Act.Exp,
        scale=LN2, bias=cst[:, 2:3]),
        reads=[B_dec, B_cst], writes=[B_dec])
    return SC


def emit_vquant(E, src_ap, rbufs, dst_ap, B_dst, u, SC):
    p = E["p"]
    dec, B_dec = E["dec"], E["B_dec"]
    yg, B_yg = E["yg"], E["B_yg"]
    inv_b = dec[:, 2 * SC + u * 32:2 * SC + (u + 1) * 32] \
        .unsqueeze(2).broadcast_to([128, 32, BK])
    sc_b = dec[:, 3 * SC + u * 32:3 * SC + (u + 1) * 32] \
        .unsqueeze(2).broadcast_to([128, 32, BK])
    quant_ladder(p, "dve", (rbufs + [B_dec], [B_yg, B_yg, B_dst]),
                 src_ap.rearrange("p (b k) -> p b k", k=BK),
                 dst_ap.rearrange("p (b k) -> p b k", k=BK),
                 inv_b, sc_b,
                 yg[:, 0:1024].rearrange("p (b k) -> p b k", k=BK),
                 yg[:, 1024:2048].rearrange("p (b k) -> p b k", k=BK),
                 nlv=6)


def emit_ln_stats(E, src_ap, rbufs, nt_col, u_out_ap, wbufs, x2_scr_ap, B_x2):
    p = E["p"]
    stat, B_stat = E["stat"], E["B_stat"]
    c0 = 512 + nt_col * 8
    sx = stat[:, c0:c0 + 1]
    sxx = stat[:, c0 + 1:c0 + 2]
    m = stat[:, c0 + 2:c0 + 3]
    b4 = stat[:, c0 + 3:c0 + 4]
    rstd = stat[:, c0 + 4:c0 + 5]

    p.emit("dve", lambda e: e.tensor_reduce(
        out=sx, in_=src_ap, axis=AX.X, op=Alu.add),
        reads=rbufs, writes=[B_stat])
    p.emit("act", lambda e: e.activation(
        out=x2_scr_ap, in_=src_ap, func=Act.Square, accum_out=sxx),
        reads=rbufs, writes=[B_x2, B_stat])
    p.emit("dve", lambda e: e.tensor_scalar(
        out=m, in0=sx, scalar1=float(1.0 / D), scalar2=None, op0=Alu.mult),
        reads=[B_stat], writes=[B_stat])
    p.emit("dve", lambda e: e.tensor_scalar(
        out=b4, in0=m, scalar1=m, scalar2=-1.0, op0=Alu.mult, op1=Alu.mult),
        reads=[B_stat], writes=[B_stat])
    p.emit("dve", lambda e: e.tensor_scalar(
        out=b4, in0=b4, scalar1=EPS, scalar2=None, op0=Alu.add),
        reads=[B_stat], writes=[B_stat])
    p.emit("act", lambda e: e.activation(
        out=rstd, in_=sxx, func=Act.Sqrt, scale=float(1.0 / D), bias=b4),
        reads=[B_stat], writes=[B_stat])
    p.emit("dve", lambda e: e.reciprocal(out=rstd, in_=rstd),
        reads=[B_stat], writes=[B_stat])
    p.emit("dve", lambda e: e.tensor_scalar(
        out=u_out_ap, in0=src_ap, scalar1=m, scalar2=rstd,
        op0=Alu.subtract, op1=Alu.mult),
        reads=rbufs + [B_stat], writes=wbufs)
    return rstd


def emit_sweep(E, sw, b):
    p = E["p"]
    g = lambda k: E[k]
    sweep, NT, NCH = g("sweep"), g("NT"), g("NCH")
    n_sweeps = g("n_sweeps")
    x_in, y_out = g("x_in"), g("y_out")
    B_y = g("B_y")
    h_t, B_ht = g("h_t"), g("B_ht")
    uh, ul, B_uh, B_ul = g("uh"), g("ul"), g("B_uh"), g("B_ul")
    uf_hi, uf_lo, B_ufh, B_ufl = g("uf_hi"), g("uf_lo"), g("B_ufh"), g("B_ufl")
    sg, B_sg = g("sg"), g("B_sg")
    yg, B_yg = g("yg"), g("B_yg")
    stat, B_stat = g("stat"), g("B_stat")
    wring, B_wr = g("wring"), g("B_wr")
    cin, cout_, B_cin, B_cout = g("cin"), g("cout_"), g("B_cin"), g("B_cout")
    ps, PB = g("ps"), g("PB")
    xbc_r, B_xbr = g("xbc_r"), g("B_xbr")
    bvcol, B_bv = g("bvcol"), g("B_bv")
    cbcol, B_cb = g("cbcol"), g("B_cb")
    ccol, B_cc = g("ccol"), g("B_cc")
    drep, B_dr = g("drep"), g("B_dr")
    dts = g("dts")
    B_dt0, B_dt1, B_dt2 = g("B_dt0"), g("B_dt1"), g("B_dt2")
    stk, B_stk = g("stk"), g("B_stk")
    dr_a, dr_ea, dr_rs = g("dr_a"), g("dr_ea"), g("dr_rs")
    B_dra, B_drea, B_drrs = g("B_dra"), g("B_drea"), g("B_drrs")
    mb, mf = g("mb"), g("mf")
    convc = g("convc")
    B_cvc = g("B_cvc")
    state, B_st = g("state"), g("B_st")
    ident, B_id = g("ident"), g("B_id")
    mask01, B_mk = g("mask01"), g("B_mk")
    aeb, B_aeb = g("aeb"), g("B_aeb")
    wTt, B_wT = g("wTt"), g("B_wT")
    abg, B_abg = g("abg"), g("B_abg")
    dec, B_dec = g("dec"), g("B_dec")
    yg, B_yg = g("yg"), g("B_yg")
    zt, B_zt = g("zt"), g("B_zt")
    s_ch, B_sch = g("s_ch"), g("B_sch")
    misc, B_misc = g("misc"), g("B_misc")
    btile, B_bt = g("btile"), g("B_bt")
    xr, B_xr = g("xr"), g("B_xr")
    xq_t, B_xq = g("uh"), g("B_uh")
    hq_t, B_hq = g("hqs"), g("B_hqs")
    xqT, B_xqT = g("uf_hi"), g("B_ufh")
    wf, B_wf = g("wf"), g("B_wf")
    hqT, B_hqT = g("mb"), g("B_mb_hqT")
    cfc, cproj, B_cfc, B_cproj = g("cfc"), g("cproj"), g("B_cfc"), g("B_cproj")
    b_fc = g("b_fc")
    B_xbs, B_xtf, B_ea, B_bw = g("B_xbs"), g("B_xtf"), g("B_ea"), g("B_bw")
    B_xhh, B_xhl, B_xth, B_xtl = g("B_xhh"), g("B_xhl"), g("B_xth"), g("B_xtl")
    B_ygh, B_ygl = g("B_ygh"), g("B_ygl")
    B_bpk, B_cpa, B_cpb = g("B_bpk"), g("B_cpa"), g("B_cpb")
    B_sth, B_stl, B_stf = g("B_sth"), g("B_stl"), g("B_stf")
    B_st0, B_bcT, B_adt, B_wrk = g("B_st0"), g("B_bcT"), g("B_adt"), g("B_wrk")
    MB_XHH, MB_XHL = g("MB_XHH"), g("MB_XHL")
    MB_XTH, MB_XTL = g("MB_XTH"), g("MB_XTL")
    MB_YGH, MB_YGL = g("MB_YGH"), g("MB_YGL")
    MB_BPK, MB_CPA, MB_CPB = g("MB_BPK"), g("MB_CPA"), g("MB_CPB")
    MB_SCR = 12672
    MB_STH, MB_STL = 12800, 13312
    MF_XBS, MF_XTF, MF_BW = g("MF_XBS"), g("MF_XTF"), g("MF_BW")
    MF_STF, MF_ST0, MF_BCT, MF_ADT, MF_WRK = (
        g("MF_STF"), g("MF_ST0"), g("MF_BCT"), g("MF_ADT"), g("MF_WRK"))
    MF_EXP = 6080
    HSL, PSL = g("HSL"), g("PSL")
    row0 = sw * sweep

    for nt in range(NT):
        r0 = row0 + nt * 128
        p.emit("sp", lambda e, r0=r0: e.dma_start(
            out=sg[:], in_=x_in[r0:r0 + 128, :]), writes=[B_sg], dma=1)
        emit_ln_stats(E, sg[:], [B_sg], nt,
                      h_t[:, nt * D:(nt + 1) * D], [B_ht], yg[:, 0:D], B_yg)

    if sw == n_sweeps // 2:
        state, B_st = g("state"), g("B_st")
        p.emit("dve", lambda e: e.memset(state[:], 0.0),
               writes=[B_st[0], B_st[1]])

    for l in range(_DBG_NL):
        emit_mamba_layer(E, sw, b, l)

    p.barrier()
    if not _DBG_FFN:
        for nt in range(NT):
            r0 = row0 + nt * 128
            p.emit("dve", lambda e, nt=nt: e.tensor_copy(
                sg[:], h_t[:, nt * D:(nt + 1) * D]),
                reads=[B_ht], writes=[B_sg])
            p.emit("act", lambda e, r0=r0: e.dma_start(
                out=y_out[r0:r0 + 128, :], in_=sg[:]),
                reads=[B_sg], writes=[B_y], dma=1)
        p.barrier()
        return
    ident, B_id = g("ident"), g("B_id")
    for nt in range(NT):
        r0 = row0 + nt * 128
        p.emit("sp", lambda e, r0=r0: e.dma_start(
            out=xr[:], in_=x_in[r0:r0 + 128, :]), writes=[B_xr], dma=1)
        emit_ln_stats(E, xr[:], [B_xr], nt, mf[:, nt * D:(nt + 1) * D],
                      [B_wrk], yg[:, 0:D], B_yg)
    SCx = emit_qstats(E, [(mf[:, nt * D:(nt + 1) * D], [B_wrk])
                          for nt in range(NT)])
    for nt in range(NT):
        emit_vquant(E, mf[:, nt * D:(nt + 1) * D], [B_wrk],
                    mf[:, nt * D:(nt + 1) * D], B_wrk, nt, SCx)
    xqT3 = xqT[:, 0:KD * sweep].rearrange("p (k t) -> p k t", t=sweep)
    for nt in range(NT):
        for tg in range(KD // 4):
            def f_txq(e, nt=nt, tg=tg):
                r = None
                for i in range(4):
                    k = tg * 4 + i
                    r = e.transpose(
                        ps[4 + tg % 2][:, i * 128:(i + 1) * 128],
                        mf[:, nt * D + k * 128:nt * D + (k + 1) * 128],
                        ident[:])
                return r
            p.emit("pe", f_txq, reads=[B_wrk, B_id],
                   writes=[PB[4 + tg % 2]])
            p.emit("act", lambda e, nt=nt, tg=tg: e.copy(
                out=xqT3[:, tg * 4:(tg + 1) * 4, nt * 128:nt * 128 + 128],
                in_=ps[4 + tg % 2][:, 0:512]
                .rearrange("p (k t) -> p k t", t=128)),
                reads=[PB[4 + tg % 2]], writes=[B_xqT])
    harea = [(mf, B_wrk, 0), (mf, B_wrk, HID),
             (xbc_r, B_xbr, 0), (xbc_r, B_xbr, HID)]
    for sl in range(HSL):
        p.emit("sp", lambda e, sl=sl: e.dma_start(
            out=btile[:], in_=b_fc[sl * 512:(sl + 1) * 512]
            .unsqueeze(0).broadcast_to([128, 512])),
            writes=[B_bt], dma=1)
        for k in range(KD):
            s4 = (sl * KD + k) % 4
            p.emit("sp", lambda e, sl=sl, k=k, s4=s4: e.dma_start(
                out=wf[:, s4 * 512:(s4 + 1) * 512], in_=cfc[k, sl]),
                reads=[B_cfc], writes=[B_wf[s4]], dma=1)

            def f_mm(e, sl=sl, k=k, s4=s4):
                r = None
                for nt in range(NT):
                    r = e.matmul(
                        ps[nt][:, :],
                        xqT[:, k * sweep + nt * 128:k * sweep + (nt + 1) * 128],
                        wf[:, s4 * 512:(s4 + 1) * 512],
                        start=(k == 0), stop=(k == KD - 1))
                return r
            p.emit("pe", f_mm, reads=[B_xqT, B_wf[s4]],
                   writes=[PB[nt] for nt in range(NT)])
        for nt in range(NT):
            ar, abuf, aoff = harea[nt]

            def f_bias(e, nt=nt, ar=ar, aoff=aoff, sl=sl):
                return e.tensor_tensor(
                    out=ar[:, aoff + sl * 512:aoff + (sl + 1) * 512],
                    in0=ps[nt][:, :], in1=btile[:], op=Alu.add)
            p.emit("dve", f_bias, reads=[PB[nt], B_bt], writes=[abuf])

            def f_gelu(e, nt=nt, ar=ar, aoff=aoff, sl=sl):
                return e.activation(
                    out=ar[:, aoff + sl * 512:aoff + (sl + 1) * 512],
                    in_=ar[:, aoff + sl * 512:aoff + (sl + 1) * 512],
                    func=Act.Gelu)
            p.emit("act", f_gelu, reads=[abuf], writes=[abuf])
    SCh = emit_qstats(E, [
        (harea[u // 4][0][:, harea[u // 4][2] + (u % 4) * 1024:
                          harea[u // 4][2] + (u % 4 + 1) * 1024],
         [harea[u // 4][1]]) for u in range(NT * 4)])
    hqT3 = hqT[:, 0:(HID // 128) * sweep].rearrange(
        "p (k t) -> p k t", t=sweep)
    for nt in range(NT):
        ar, abuf, aoff = harea[nt]
        for q in range(4):
            emit_vquant(E, ar[:, aoff + q * 1024:aoff + (q + 1) * 1024],
                        [abuf], ar[:, aoff + q * 1024:aoff + (q + 1) * 1024],
                        abuf, nt * 4 + q, SCh)
        for tg in range(8):
            def f_thq(e, nt=nt, tg=tg, ar=ar, aoff=aoff):
                r = None
                for i in range(4):
                    kt = tg * 4 + i
                    r = e.transpose(
                        ps[4 + tg % 2][:, i * 128:(i + 1) * 128],
                        ar[:, aoff + kt * 128:aoff + (kt + 1) * 128],
                        ident[:])
                return r
            p.emit("pe", f_thq, reads=[abuf, B_id],
                   writes=[PB[4 + tg % 2]])
            p.emit("act", lambda e, nt=nt, tg=tg: e.copy(
                out=hqT3[:, tg * 4:(tg + 1) * 4, nt * 128:nt * 128 + 128],
                in_=ps[4 + tg % 2][:, 0:512]
                .rearrange("p (k t) -> p k t", t=128)),
                reads=[PB[4 + tg % 2]], writes=[B_hqT])
    for psl in range(PSL):
        for kt in range(HID // 128):
            s4 = (psl * (HID // 128) + kt) % 4
            p.emit("sp", lambda e, psl=psl, kt=kt, s4=s4: e.dma_start(
                out=wf[:, s4 * 512:(s4 + 1) * 512], in_=cproj[kt, psl]),
                reads=[B_cproj], writes=[B_wf[s4]], dma=1)

            def f_pm(e, psl=psl, kt=kt, s4=s4):
                r = None
                for nt in range(NT):
                    bank = (4 + nt) if psl == 0 else nt
                    r = e.matmul(
                        ps[bank][:, :],
                        hqT[:, kt * sweep + nt * 128:kt * sweep + (nt + 1) * 128],
                        wf[:, s4 * 512:(s4 + 1) * 512],
                        start=(kt == 0), stop=(kt == HID // 128 - 1))
                return r
            p.emit("pe", f_pm, reads=[B_hqT, B_wf[s4]],
                   writes=[PB[(4 + nt) if psl == 0 else nt] for nt in range(NT)])
    for nt in range(NT):
        r0 = row0 + nt * 128
        p.emit("sp", lambda e, r0=r0: e.dma_start(
            out=xr[:], in_=x_in[r0:r0 + 128, :]), writes=[B_xr], dma=1)
        for psl in range(PSL):
            bank = (4 + nt) if psl == 0 else nt

            def f_gb(e, nt=nt, psl=psl, bank=bank):
                return e.tensor_tensor(
                    out=sg[:, psl * 512:(psl + 1) * 512], in0=ps[bank][:, :],
                    in1=misc[:, psl * 512:(psl + 1) * 512],
                    op=Alu.add)
            p.emit("dve", f_gb, reads=[PB[bank], B_misc], writes=[B_sg])
        p.emit("act", lambda e: e.activation(out=sg[:], in_=sg[:],
                                             func=Act.Silu),
               reads=[B_sg], writes=[B_sg])
        p.emit("dve", lambda e, nt=nt: e.tensor_tensor(
            out=sg[:], in0=sg[:], in1=h_t[:, nt * D:(nt + 1) * D],
            op=Alu.mult), reads=[B_ht, B_sg], writes=[B_sg])
        p.emit("dve", lambda e: e.tensor_tensor(
            out=sg[:], in0=sg[:], in1=xr[:], op=Alu.add),
            reads=[B_xr, B_sg], writes=[B_sg])
        p.emit("act", lambda e, r0=r0: e.dma_start(
            out=y_out[r0:r0 + 128, :], in_=sg[:]),
            reads=[B_sg], writes=[B_y], dma=1)
    p.barrier()


def emit_mamba_layer(E, sw, b, l):
    p = E["p"]
    g = lambda k: E[k]
    sweep, NT, NCH = g("sweep"), g("NT"), g("NCH")
    h_t, B_ht = g("h_t"), g("B_ht")
    uh, ul, B_uh, B_ul = g("uh"), g("ul"), g("B_uh"), g("B_ul")
    uf_hi, uf_lo, B_ufh, B_ufl = g("uf_hi"), g("uf_lo"), g("B_ufh"), g("B_ufl")
    sg, B_sg = g("sg"), g("B_sg")
    stat, B_stat = g("stat"), g("B_stat")
    wring, B_wr = g("wring"), g("B_wr")
    cin, B_cin = g("cin"), g("B_cin")
    ps, PB = g("ps"), g("PB")
    xbc_r, B_xbr = g("xbc_r"), g("B_xbr")
    bvcol, B_bv = g("bvcol"), g("B_bv")
    ccol, B_cc = g("ccol"), g("B_cc")
    dts = g("dts")
    B_dt0, B_dt1, B_dt2 = g("B_dt0"), g("B_dt1"), g("B_dt2")
    stk, B_stk = g("stk"), g("B_stk")
    dr_a, dr_ea, dr_rs = g("dr_a"), g("dr_ea"), g("dr_rs")
    B_dra, B_drea, B_drrs = g("B_dra"), g("B_drea"), g("B_drrs")
    convc, B_cvc = g("convc"), g("B_cvc")
    misc, B_misc = g("misc"), g("B_misc")
    cw, B_cw = g("cw"), g("B_cw")
    S = sweep
    SP4 = S + 4

    ps, PB = g("ps"), g("PB")
    ident, B_id = g("ident"), g("B_id")
    mf = g("mf")
    MF_XBS = g("MF_XBS")
    B_uff = g("B_xbs")[0]
    for nt in range(NT):
        src_ap, src_bufs = h_t[:, nt * D:(nt + 1) * D], [g("B_ht")]
        emit_ln_stats(E, src_ap, src_bufs, nt, sg[:], [B_sg],
                      g("yg")[:, 0:D], g("B_yg"))
        for tq in range(2):
            def f_tu(e, tq=tq):
                r = None
                for i in range(4):
                    k = tq * 4 + i
                    r = e.transpose(ps[6 + tq][:, i * 128:(i + 1) * 128],
                                    sg[:, k * 128:(k + 1) * 128], ident[:])
                return r
            p.emit("pe", f_tu, reads=[B_sg, B_id], writes=[PB[6 + tq]])
            p.emit("dve", lambda e, tq=tq: e.tensor_copy(
                mf[:, MF_XBS + tq * 512:MF_XBS + (tq + 1) * 512],
                ps[6 + tq][:, 0:512]),
                reads=[PB[6 + tq]], writes=[B_uff])
        ufh3 = uf_hi[:, 0:KD * S].rearrange("p (k t) -> p k t", t=S)[
            :, :, nt * 128:nt * 128 + 128]
        ufl3 = uf_lo[:, 0:KD * S].rearrange("p (k t) -> p k t", t=S)[
            :, :, nt * 128:nt * 128 + 128]
        uff3 = mf[:, MF_XBS:MF_XBS + 1024].rearrange(
            "p (k t) -> p k t", t=128)
        p.emit("act", lambda e, ufh3=ufh3, uff3=uff3: e.copy(
            out=ufh3, in_=uff3), reads=[B_uff], writes=[B_ufh])
        p.emit("dve", lambda e, ufh3=ufh3, ufl3=ufl3, uff3=uff3:
               e.scalar_tensor_tensor(out=ufl3, in0=uff3, scalar=1.0,
                                      in1=ufh3, op0=Alu.mult,
                                      op1=Alu.subtract),
               reads=[B_uff, B_ufh], writes=[B_ufl])

    cslot = (l * 2 + b) * NXT * 4
    p.emit("dve", lambda e, cslot=cslot: e.tensor_copy(
        xbc_r[:].rearrange("p (t c) -> p t c", c=SP4)[:, :, 1:4],
        convc[:, cslot:cslot + NXT * 4]
        .rearrange("p (t c) -> p t c", c=4)[:, :, 1:4]),
        reads=[B_cvc[l][b]], writes=[B_xbr])

    for i, dc in enumerate(range(16, NDC)):
        slot = i % 2
        p.emit("sp", lambda e, dc=dc, slot=slot: e.dma_start(
            out=wring[:, slot * 2048:slot * 2048 + 2048]
            .rearrange("p (c n) -> p c n", n=128),
            in_=cin[l, dc].rearrange("k h p n -> p (k h) n")),
            reads=[B_cin], writes=[B_wr[slot]], dma=1)

        def f_mm(e, dc=dc, slot=slot):
            r = None
            for k in range(KD):
                wh = wring[:, slot * 2048 + (k * 2) * 128:slot * 2048 + (k * 2 + 1) * 128]
                wl = wring[:, slot * 2048 + (k * 2 + 1) * 128:slot * 2048 + (k * 2 + 2) * 128]
                xh_ = uf_hi[:, k * S:(k + 1) * S]
                xl_ = uf_lo[:, k * S:(k + 1) * S]
                r = e.matmul(ps[slot][:, 0:S], wh, xh_, start=(k == 0), stop=False)
                r = e.matmul(ps[slot][:, 0:S], wh, xl_, start=False, stop=False)
                r = e.matmul(ps[slot][:, 0:S], wl, xh_, start=False,
                             stop=(k == KD - 1))
            return r
        p.emit("pe", f_mm, reads=[B_wr[slot], B_ufh, B_ufl],
               writes=[PB[slot]])
        if dc < 33:
            xt = dc - 16

            def f_ev(e, dc=dc, slot=slot, xt=xt):
                return e.activation(
                    out=xbc_r[:, xt * SP4 + 4:xt * SP4 + 4 + S],
                    in_=ps[slot][:, 0:S], func=Act.Identity,
                    bias=bvcol[:, l * NDC + dc:l * NDC + dc + 1])
            p.emit("act", f_ev, reads=[PB[slot], B_bv], writes=[B_xbr])
        else:
            def f_ev(e, slot=slot):
                return e.activation(
                    out=dts[:, 0:S], in_=ps[slot][0:NH, 0:S], func=Act.Identity,
                    bias=ccol[0:NH, 8 + l:9 + l])
            p.emit("act", f_ev, reads=[PB[slot], B_cc], writes=[B_dt0])

    q0, q1, q2 = dts[:, 0:S], dts[:, S:2 * S], dts[:, 2 * S:3 * S]
    p.emit("act", lambda e: e.activation(out=q2, in_=q0, func=Act.Abs),
           reads=[B_dt0], writes=[B_dt2])
    p.emit("act", lambda e: e.activation(out=q2, in_=q2, func=Act.Exp,
                                         scale=-1.0),
           reads=[B_dt2], writes=[B_dt2])
    p.emit("act", lambda e: e.activation(out=q2, in_=q2, func=Act.Ln,
                                         bias=g("cst")[0:NH, 0:1]),
           reads=[B_dt2, g("B_cst")], writes=[B_dt2])
    p.emit("dve", lambda e: e.scalar_tensor_tensor(
        out=q1, in0=q0, scalar=0.0, in1=q2, op0=Alu.max, op1=Alu.add),
        reads=[B_dt0, B_dt2], writes=[B_dt1])
    p.emit("act", lambda e: e.activation(out=q2, in_=q1, func=Act.Ln),
           reads=[B_dt1], writes=[B_dt2])
    p.emit("act", lambda e: [
        e.dma_start(out=stk[32:64, :], in_=q1),
        e.dma_start(out=stk[64:96, :], in_=q2)],
        reads=[B_dt1, B_dt2], writes=[B_stk], dma=2)
    p.emit("dve", lambda e, l=l: e.tensor_scalar(
        out=q0, in0=q1, scalar1=ccol[0:NH, l:l + 1], scalar2=None,
        op0=Alu.mult), reads=[B_dt1, g("B_cc")], writes=[B_dt0])
    for ck in range(NCH):
        p.emit("dve", lambda e, ck=ck: e.tensor_tensor_scan(
            out=stk[0:32, ck * CH:(ck + 1) * CH],
            data0=q0[:, ck * CH:(ck + 1) * CH],
            data1=q0[:, ck * CH:(ck + 1) * CH],
            initial=0.0, op0=Alu.add, op1=Alu.bypass),
            reads=[B_dt0], writes=[B_stk])
    p.emit("act", lambda e: e.activation(out=q0, in_=stk[0:32, :],
                                         func=Act.Exp),
           reads=[B_stk], writes=[B_dt0])
    p.emit("act", lambda e: e.dma_start(out=dr_a[:, :], in_=stk[0:32, :]),
           reads=[B_stk], writes=[B_dra], dma=1)
    p.emit("act", lambda e: e.dma_start(out=dr_ea[:, :], in_=q0),
           reads=[B_dt0], writes=[B_drea], dma=1)

    p.emit("sp", lambda e, l=l: e.dma_start(
        out=g("dec")[:, 0:DIN],
        in_=g("cbv")[l, 0:DIN].unsqueeze(0).broadcast_to([128, DIN])),
        reads=[g("B_cbv")], writes=[g("B_dec")], dma=1)

    for ck in range(NCH):
        emit_chunk(E, sw, b, l, ck)

    p.emit("dve", lambda e, cslot=cslot: e.tensor_copy(
        convc[:, cslot:cslot + NXT * 4]
        .rearrange("p (t c) -> p t c", c=4)[:, :, 1:4],
        xbc_r[:].rearrange("p (t c) -> p t c", c=SP4)[:, :, S + 1:S + 4]),
        reads=[B_xbr], writes=[B_cvc[l][b]])


def emit_chunk(E, sw, b, l, ck):
    p = E["p"]
    g = lambda k: E[k]
    sweep, NCH = g("sweep"), g("NCH")
    S, SP4 = sweep, sweep + 4
    xbc_r, B_xbr = g("xbc_r"), g("B_xbr")
    mb, mf = g("mb"), g("mf")
    stk, B_stk = g("stk"), g("B_stk")
    dr_a, B_dra = g("dr_a"), g("B_dra")
    ps, PB = g("ps"), g("PB")
    ident, B_id = g("ident"), g("B_id")
    mask01, B_mk = g("mask01"), g("B_mk")
    aeb, B_aeb = g("aeb"), g("B_aeb")
    wTt, B_wT = g("wTt"), g("B_wT")
    abg, B_abg = g("abg"), g("B_abg")
    s_ch, B_sch = g("s_ch"), g("B_sch")
    cbcol, B_cb = g("cbcol"), g("B_cb")
    cw, B_cw = g("cw"), g("B_cw")
    B_xbs, B_xtf, B_bw = g("B_xbs"), g("B_xtf"), g("B_bw")
    B_xhh, B_xhl, B_xth, B_xtl = g("B_xhh"), g("B_xhl"), g("B_xth"), g("B_xtl")
    B_bpk, B_cpa, B_cpb = g("B_bpk"), g("B_cpa"), g("B_cpb")
    B_st0, B_bcT, B_adt, B_wrk = g("B_st0"), g("B_bcT"), g("B_adt"), g("B_wrk")
    MB_XHH, MB_XHL = g("MB_XHH"), g("MB_XHL")
    MB_XTH, MB_XTL = g("MB_XTH"), g("MB_XTL")
    MB_BPK, MB_CPA, MB_CPB = g("MB_BPK"), g("MB_CPA"), g("MB_CPB")
    MB_SCR = 12672
    MF_XBS, MF_XTF, MF_BW = g("MF_XBS"), g("MF_XTF"), g("MF_BW")
    MF_STF, MF_ST0, MF_BCT, MF_ADT, MF_WRK = (
        g("MF_STF"), g("MF_ST0"), g("MF_BCT"), g("MF_ADT"), g("MF_WRK"))
    MF_EXP = 6080

    xbs = lambda t: mf[:, MF_XBS + t * 128:MF_XBS + (t + 1) * 128]
    t16 = lambda rs, re_: mf[rs:re_, MF_XBS + 16 * 128:MF_XBS + 17 * 128]
    acc3 = mf[:, MF_XBS:MF_XBS + DIN].rearrange("p (t n) -> p t n", n=128)
    tmp3 = mf[:, MF_XTF:MF_XTF + DIN].rearrange("p (t n) -> p t n", n=128)

    def win3(kk, ck=ck):
        return bass.AP(tensor=xbc_r.tensor, offset=1 + kk + ck * CH,
                       ap=[[NXT * SP4, 128], [SP4, 16], [1, CH]])             if False else             xbc_r[:, 0:16 * SP4].rearrange("p (t n) -> p t n", n=SP4)[
                :, :, 1 + kk + ck * CH:1 + kk + ck * CH + CH]

    def cw3(kk):
        return cw[:, l * NXT * 4 + kk:l * NXT * 4 + kk + 16 * 4:1]             .rearrange("p (t o) -> p t o", o=4)[:, :, 0:1]             .broadcast_to([128, 16, CH])

    def cb3():
        return cbcol[:, l * NXT:l * NXT + 16].unsqueeze(2)             .broadcast_to([128, 16, CH])

    def f_cb0(e):
        return e.tensor_tensor(out=acc3, in0=win3(0), in1=cw3(0), op=Alu.mult)
    p.emit("dve", f_cb0, reads=[B_xbr, B_cw], writes=[B_xbs[0]])
    for kk in range(1, 4):
        def f_cm(e, kk=kk):
            return e.tensor_tensor(out=tmp3, in0=win3(kk), in1=cw3(kk),
                                   op=Alu.mult)
        p.emit("dve", f_cm, reads=[B_xbr, B_cw], writes=[B_xtf[0]])

        def f_ca(e):
            return e.tensor_tensor(out=acc3, in0=acc3, in1=tmp3, op=Alu.add)
        p.emit("dve", f_ca, reads=[B_xtf[0], B_xbs[0]], writes=[B_xbs[0]])

    def f_cba(e):
        return e.tensor_tensor(out=acc3, in0=acc3, in1=cb3(), op=Alu.add)
    p.emit("dve", f_cba, reads=[B_cb, B_xbs[0]], writes=[B_xbs[0]])
    p.emit("act", lambda e: e.activation(
        out=mf[:, MF_XBS:MF_XBS + DIN], in_=mf[:, MF_XBS:MF_XBS + DIN],
        func=Act.Silu), reads=[B_xbs[0]], writes=[B_xbs[0]])
    xt = 16
    def f_c0(e, xt=xt):
        return e.tensor_scalar(
            out=xbs(xt), in0=xbc_r[:, xt * SP4 + 1 + ck * CH:
                                   xt * SP4 + 1 + ck * CH + CH],
            scalar1=cw[:, (l * NXT + xt) * 4:(l * NXT + xt) * 4 + 1],
            scalar2=None, op0=Alu.mult)
    p.emit("dve", f_c0, reads=[B_xbr, B_cw], writes=[B_xbs[0]])
    for kk in range(1, 4):
        def f_cr(e, xt=xt, kk=kk):
            return e.scalar_tensor_tensor(
                out=xbs(xt),
                in0=xbc_r[:, xt * SP4 + 1 + kk + ck * CH:
                          xt * SP4 + 1 + kk + ck * CH + CH],
                scalar=cw[:, (l * NXT + xt) * 4 + kk:
                          (l * NXT + xt) * 4 + kk + 1],
                in1=xbs(xt), op0=Alu.mult, op1=Alu.add)
        p.emit("dve", f_cr, reads=[B_xbr, B_cw, B_xbs[0]], writes=[B_xbs[0]])
    def f_si(e, xt=xt):
        return e.activation(out=xbs(xt), in_=xbs(xt), func=Act.Silu,
                            bias=cbcol[:, l * NXT + xt:l * NXT + xt + 1])
    p.emit("act", f_si, reads=[B_xbs[0], B_cb], writes=[B_xbs[0]])

    p.emit("act", lambda e: e.copy(out=mb[0:64, MB_BPK:MB_BPK + 128],
                                   in_=t16(0, 64)),
           reads=[B_xbs[0]], writes=[B_bpk])
    p.emit("dve", lambda e: e.scalar_tensor_tensor(
        out=mb[0:64, MB_SCR:MB_SCR + 128], in0=t16(0, 64), scalar=1.0,
        in1=mb[0:64, MB_BPK:MB_BPK + 128], op0=Alu.mult, op1=Alu.subtract),
        reads=[B_xbs[0], B_bpk], writes=[B_wrk])
    p.emit("act", lambda e: e.dma_start(
        out=mb[64:128, MB_BPK:MB_BPK + 128],
        in_=mb[0:64, MB_SCR:MB_SCR + 128]),
        reads=[B_wrk], writes=[B_bpk], dma=1)
    p.emit("act", lambda e: e.copy(out=mb[64:128, MB_CPB:MB_CPB + 128],
                                   in_=t16(64, 128)),
           reads=[B_xbs[0]], writes=[B_cpb])
    p.emit("dve", lambda e: e.scalar_tensor_tensor(
        out=mb[64:128, MB_CPA:MB_CPA + 128], in0=t16(64, 128), scalar=1.0,
        in1=mb[64:128, MB_CPB:MB_CPB + 128], op0=Alu.mult, op1=Alu.subtract),
        reads=[B_xbs[0], B_cpb], writes=[B_cpa])
    p.emit("act", lambda e: e.dma_start(
        out=mb[0:64, MB_CPA:MB_CPA + 128],
        in_=mb[64:128, MB_CPB:MB_CPB + 128]),
        reads=[B_cpb], writes=[B_cpa], dma=1)
    p.emit("act", lambda e: e.dma_start(
        out=mb[0:64, MB_CPB:MB_CPB + 128],
        in_=mb[64:128, MB_CPA:MB_CPA + 128]),
        reads=[B_cpa], writes=[B_cpb], dma=1)

    for tq in range(4):
        def f_tx(e, tq=tq):
            r = None
            for i in range(4):
                t_ = tq * 4 + i
                r = e.transpose(
                    ps[6 + tq % 2][:, i * 128:(i + 1) * 128],
                    mf[:, MF_XBS + t_ * 128:MF_XBS + (t_ + 1) * 128],
                    ident[:])
            return r
        p.emit("pe", f_tx, reads=[B_xbs[0], B_id], writes=[PB[6 + tq % 2]])
        p.emit("dve", lambda e, tq=tq: e.tensor_copy(
            mf[:, MF_XTF + tq * 512:MF_XTF + (tq + 1) * 512],
            ps[6 + tq % 2][:, 0:512]),
            reads=[PB[6 + tq % 2]], writes=[B_xtf[0]])
    p.emit("act", lambda e: e.copy(out=mb[:, MB_XTH:MB_XTH + DIN],
                                   in_=mf[:, MF_XTF:MF_XTF + DIN]),
           reads=[B_xtf[0]], writes=[B_xth[0]])
    p.emit("dve", lambda e: e.scalar_tensor_tensor(
        out=mb[:, MB_XTL:MB_XTL + DIN], in0=mf[:, MF_XTF:MF_XTF + DIN],
        scalar=1.0, in1=mb[:, MB_XTH:MB_XTH + DIN],
        op0=Alu.mult, op1=Alu.subtract),
        reads=[B_xtf[0], B_xth[0]], writes=[B_xtl[0]])

    p.emit("pe", lambda e: e.transpose(
        ps[6][:, 128:192], t16(0, 64), ident[0:64, 0:64]),
        reads=[B_xbs[0], B_id], writes=[PB[6]])
    p.emit("dve", lambda e: e.tensor_copy(
        mf[:, MF_BCT:MF_BCT + 64], ps[6][:, 128:192]),
        reads=[PB[6]], writes=[B_bcT])
    p.emit("pe", lambda e, ck=ck: e.transpose(
        ps[6][:, 192:288], stk[:, ck * CH:(ck + 1) * CH],
        ident[0:96, 0:96]),
        reads=[B_stk, B_id], writes=[PB[6]])
    p.emit("dve", lambda e: e.tensor_copy(
        mf[:, MF_ADT:MF_ADT + 96], ps[6][:, 192:288]),
        reads=[PB[6]], writes=[B_adt[0]])

    p.emit("sp", lambda e, ck=ck: e.dma_start(
        out=aeb[:, 0:32],
        in_=dr_a[:, ck * CH + CH - 1].unsqueeze(0).broadcast_to([128, 32]),
        allow_slow_non_contiguous=True),
        reads=[B_dra], writes=[B_aeb], dma=1)
    p.emit("dve", lambda e: e.tensor_tensor(
        out=aeb[:, 32:64], in0=aeb[:, 0:32],
        in1=mf[:, MF_ADT:MF_ADT + 32], op=Alu.subtract),
        reads=[B_aeb, B_adt[0]], writes=[B_aeb])
    p.emit("dve", lambda e: e.tensor_tensor(
        out=aeb[:, 32:64], in0=aeb[:, 32:64],
        in1=mf[:, MF_ADT + 64:MF_ADT + 96], op=Alu.add),
        reads=[B_aeb, B_adt[0]], writes=[B_aeb])
    p.emit("act", lambda e: e.activation(out=wTt[:], in_=aeb[:, 32:64],
                                         func=Act.Exp),
           reads=[B_aeb], writes=[B_wT])
    p.emit("act", lambda e: e.activation(
        out=mf[:, MF_ADT + 96:MF_ADT + 128], in_=mf[:, MF_ADT:MF_ADT + 32],
        func=Act.Exp), reads=[B_adt[0]], writes=[B_adt[0]])
    p.emit("act", lambda e: e.activation(
        out=aeb[:, 0:32], in_=aeb[:, 0:32], func=Act.Exp),
        reads=[B_aeb], writes=[B_aeb])

    def f_st0(e):
        r = e.matmul(ps[7][:, 0:128], mb[:, MB_BPK:MB_BPK + 128],
                     mb[:, MB_CPA:MB_CPA + 128], start=True, stop=False,
                     skip_group_check=True)
        r = e.matmul(ps[7][:, 0:128], mb[:, MB_BPK:MB_BPK + 128],
                     mb[:, MB_CPB:MB_CPB + 128], start=False, stop=True,
                     skip_group_check=True)
        return r
    p.emit("pe", f_st0, reads=[B_bpk, B_cpa, B_cpb], writes=[PB[7]])
    p.emit("dve", lambda e: e.tensor_tensor(
        out=mf[:, MF_ST0:MF_ST0 + 128], in0=ps[7][:, 0:128], in1=mask01[:],
        op=Alu.mult), reads=[PB[7], B_mk], writes=[B_st0])

    for g4 in range(8):
        g8 = g4 // 2
        p.emit("sp", lambda e, g4=g4, ck=ck: e.dma_start(
            out=abg[:].rearrange("p (i n) -> p i n", n=128),
            in_=dr_a[4 * g4:4 * g4 + 4, ck * CH:(ck + 1) * CH]
            .unsqueeze(0).broadcast_to([128, 4, CH])),
            reads=[B_dra], writes=[B_abg[0]], dma=1)

        def f_ts4(e, g4=g4):
            r = None
            for i4 in range(4):
                h = 4 * g4 + i4
                r = e.tensor_scalar(
                    out=mf[:, MF_WRK + i4 * 128:MF_WRK + (i4 + 1) * 128],
                    in0=abg[:, i4 * 128:(i4 + 1) * 128],
                    scalar1=mf[:, MF_ADT + h:MF_ADT + h + 1], scalar2=0.0,
                    op0=Alu.subtract, op1=Alu.min)
            return r
        p.emit("dve", f_ts4, reads=[B_abg[0], B_adt[0]], writes=[B_wrk])
        p.emit("act", lambda e: e.activation(
            out=mf[:, MF_EXP:MF_EXP + 512], in_=mf[:, MF_WRK:MF_WRK + 512],
            func=Act.Exp), reads=[B_wrk], writes=[g("B_exp")])
        if g4 % 2 == 0:
            p.emit("dve", lambda e, g8=g8: e.tensor_tensor(
                out=mf[:, MF_BW:MF_BW + 512].rearrange(
                    "p (i n) -> p i n", n=64),
                in0=mf[:, MF_XTF + g8 * 512:MF_XTF + (g8 + 1) * 512]
                .rearrange("p (i n) -> p i n", n=64),
                in1=wTt[:, g8 * 8:(g8 + 1) * 8].unsqueeze(2)
                .broadcast_to([128, 8, 64]),
                op=Alu.mult),
                reads=[B_xtf[0], B_wT], writes=[B_bw])
            p.emit("pe", lambda e: e.matmul(
                ps[5][64:128, 0:512], mf[:, MF_BCT:MF_BCT + 64],
                mf[:, MF_BW:MF_BW + 512], start=True, stop=True,
                skip_group_check=True),
                reads=[B_bcT, B_bw], writes=[PB[5]])
        emit_head_group(E, sw, b, l, ck, g4)
        if g4 % 2 == 1:
            emit_group_tail(E, sw, b, l, ck, g8)

    emit_outproj_chunk(E, sw, b, l, ck)


def emit_head_group(E, sw, b, l, ck, g4):
    p = E["p"]
    g = lambda k: E[k]
    mb, mf = g("mb"), g("mf")
    ps, PB = g("ps"), g("PB")
    state, B_st = g("state"), g("B_st")
    s_ch, B_sch = g("s_ch"), g("B_sch")
    B_sth, B_stl, B_stf = g("B_sth"), g("B_stl"), g("B_stf")
    B_st0, B_adt, B_wrk = g("B_st0"), g("B_adt"), g("B_wrk")
    B_exp, B_bw = g("B_exp"), g("B_bw")
    B_xth, B_xtl, B_xtf, B_xbs = g("B_xth"), g("B_xtl"), g("B_xtf"), g("B_xbs")
    MB_XTH, MB_XTL = g("MB_XTH"), g("MB_XTL")
    MB_STH, MB_STL = 12800, 13312
    MF_STF, MF_ST0, MF_ADT, MF_BW = (
        g("MF_STF"), g("MF_ST0"), g("MF_ADT"), g("MF_BW"))
    MF_XTF, MF_XBS = g("MF_XTF"), g("MF_XBS")
    MF_EXP = 6080
    hs = [4 * g4 + i for i in range(4)]

    def f_st4(e):
        r = None
        for i4, h in enumerate(hs):
            r = e.scalar_tensor_tensor(
                out=mf[:, MF_STF + i4 * 128:MF_STF + (i4 + 1) * 128],
                in0=mf[:, MF_EXP + i4 * 128:MF_EXP + (i4 + 1) * 128],
                scalar=mf[:, MF_ADT + 32 + h:MF_ADT + 33 + h],
                in1=mf[:, MF_ST0:MF_ST0 + 128], op0=Alu.mult, op1=Alu.mult)
        return r
    p.emit("dve", f_st4, reads=[B_exp, B_adt[0], B_st0],
           writes=[B_stf[0]])

    def f_hi4(e):
        return e.copy(out=mb[:, MB_STH:MB_STH + 512],
                      in_=mf[:, MF_STF:MF_STF + 512])
    p.emit("act", f_hi4, reads=[B_stf[0]], writes=[B_sth[0]])

    def f_lo4(e):
        return e.scalar_tensor_tensor(
            out=mb[:, MB_STL:MB_STL + 512], in0=mf[:, MF_STF:MF_STF + 512],
            scalar=1.0, in1=mb[:, MB_STH:MB_STH + 512],
            op0=Alu.mult, op1=Alu.subtract)
    p.emit("dve", f_lo4, reads=[B_stf[0], B_sth[0]], writes=[B_stl[0]])

    def f_y(e, g4=g4):
        r = None
        for i4, h in enumerate(hs):
            cs = (h % 8) * 64
            sth = mb[:, MB_STH + i4 * 128:MB_STH + (i4 + 1) * 128]
            stl = mb[:, MB_STL + i4 * 128:MB_STL + (i4 + 1) * 128]
            lh = mb[:, MB_XTH + h * 64:MB_XTH + (h + 1) * 64]
            ll = mb[:, MB_XTL + h * 64:MB_XTL + (h + 1) * 64]
            r = e.matmul(ps[2][:, cs:cs + 64], sth, lh,
                         start=True, stop=False, skip_group_check=True)
            r = e.matmul(ps[2][:, cs:cs + 64], stl, lh,
                         start=False, stop=False, skip_group_check=True)
            r = e.matmul(ps[2][:, cs:cs + 64], sth, ll,
                         start=False, stop=True, skip_group_check=True)
        return r
    p.emit("pe", f_y, reads=[B_xth[0], B_xtl[0], B_sth[0], B_stl[0]],
           writes=[PB[2]])

    t16c = mf[64:128, MF_XBS + 16 * 128:MF_XBS + 17 * 128]

    def f_y2(e, l=l, g4=g4):
        cs = (g4 % 2) * 256
        return e.matmul(
            ps[3][:, cs:cs + 256], t16c,
            state[64:128, l * DIN + g4 * 256:l * DIN + (g4 + 1) * 256],
            start=True, stop=True, skip_group_check=True)
    p.emit("pe", f_y2, reads=[B_st[l], B_xbs[0]], writes=[PB[3]])


def emit_group_tail(E, sw, b, l, ck, g8):
    p = E["p"]
    g = lambda k: E[k]
    mb, mf = g("mb"), g("mf")
    ps, PB = g("ps"), g("PB")
    state, B_st = g("state"), g("B_st")
    aeb, B_aeb = g("aeb"), g("B_aeb")
    s_ch, B_sch = g("s_ch"), g("B_sch")
    yg, B_yg = g("yg"), g("B_yg")
    zt, B_zt = g("zt"), g("B_zt")
    wring, B_wr = g("wring"), g("B_wr")
    cin, B_cin = g("cin"), g("B_cin")
    uf_hi, uf_lo, B_ufh, B_ufl = g("uf_hi"), g("uf_lo"), g("B_ufh"), g("B_ufl")
    bvcol, B_bv = g("bvcol"), g("B_bv")
    drep2, B_dr2 = g("drep2"), g("B_dr2")
    B_xtf, B_adt = g("B_xtf"), g("B_adt")
    MF_XTF, MF_ADT, MF_BW = g("MF_XTF"), g("MF_ADT"), g("MF_BW")
    sweep = g("sweep")
    S = sweep

    p.emit("dve", lambda e, l=l, g8=g8: e.tensor_tensor(
        out=s_ch[64:128, :].rearrange("p (i n) -> p i n", n=64),
        in0=state[64:128, l * DIN + g8 * 512:l * DIN + (g8 + 1) * 512]
        .rearrange("p (i n) -> p i n", n=64),
        in1=aeb[64:128, g8 * 8:(g8 + 1) * 8].unsqueeze(2)
        .broadcast_to([64, 8, 64]),
        op=Alu.mult),
        reads=[B_st[l], B_aeb], writes=[B_sch])
    p.emit("dve", lambda e, l=l, g8=g8: e.tensor_tensor(
        out=state[64:128, l * DIN + g8 * 512:l * DIN + (g8 + 1) * 512],
        in0=s_ch[64:128, :], in1=ps[5][64:128, :], op=Alu.add),
        reads=[B_sch, PB[5]], writes=[B_st[l]])

    zb = g8 % 2
    for k2 in range(KD // 2):
        slot = E.setdefault("wslot", 0)
        E["wslot"] = 1 - slot
        p.emit("sp", lambda e, l=l, g8=g8, k2=k2, slot=slot: e.dma_start(
            out=wring[:, slot * 2048:slot * 2048 + 2048],
            in_=g("cin_z")[l, g8, k2]),
            reads=[B_cin], writes=[B_wr[slot]], dma=1)

        def f_zmm(e, k2=k2, slot=slot, zb=zb, ck=ck):
            r = None
            for kl in range(2):
                k = 2 * k2 + kl
                base = slot * 2048 + kl * 1024
                wh = wring[:, base:base + 512]
                wl = wring[:, base + 512:base + 1024]
                xh_ = uf_hi[:, k * S + ck * CH:k * S + (ck + 1) * CH]
                xl_ = uf_lo[:, k * S + ck * CH:k * S + (ck + 1) * CH]
                r = e.matmul(ps[zb][:, 0:512], xh_, wh,
                             start=(k == 0), stop=False,
                             skip_group_check=True)
                r = e.matmul(ps[zb][:, 0:512], xl_, wh, start=False,
                             stop=False, skip_group_check=True)
                r = e.matmul(ps[zb][:, 0:512], xh_, wl, start=False,
                             stop=(k == KD - 1), skip_group_check=True)
            return r
        p.emit("pe", f_zmm, reads=[B_wr[slot], B_ufh, B_ufl],
               writes=[PB[zb]])
    dec_t, B_dec = g("dec"), g("B_dec")
    p.emit("dve", lambda e, g8=g8, zb=zb: e.tensor_tensor(
        out=zt[:, 0:512], in0=ps[zb][:, 0:512],
        in1=dec_t[:, g8 * 512:(g8 + 1) * 512], op=Alu.add),
        reads=[PB[zb], B_dec], writes=[B_zt])
    p.emit("act", lambda e: e.activation(
        out=zt[:, 0:512], in_=zt[:, 0:512], func=Act.Silu),
        reads=[B_zt], writes=[B_zt])

    for kt in range(4):
        ht = g8 * 4 + kt
        ygs = yg[:, ht * 128:(ht + 1) * 128]
        ygs3 = ygs.rearrange("p (a b) -> p a b", b=64)
        eab = (mf[:, MF_ADT + 96 + 2 * ht:MF_ADT + 96 + 2 * ht + 2]
               .unsqueeze(2).broadcast_to([128, 2, 64]))
        dab = (drep2[:, l * NH + 2 * ht:l * NH + 2 * ht + 2]
               .unsqueeze(2).broadcast_to([128, 2, 64]))
        p.emit("dve", lambda e, kt=kt, ygs3=ygs3, eab=eab: e.tensor_tensor(
            out=ygs3, in0=ps[3][:, kt * 128:(kt + 1) * 128]
            .rearrange("p (a b) -> p a b", b=64), in1=eab, op=Alu.mult),
            reads=[PB[3], B_adt[0]], writes=[B_yg])
        p.emit("dve", lambda e, kt=kt, ygs=ygs: e.tensor_tensor(
            out=ygs, in0=ygs, in1=ps[2][:, kt * 128:(kt + 1) * 128],
            op=Alu.add),
            reads=[PB[2], B_yg], writes=[B_yg])
        p.emit("dve", lambda e, ht=ht, dab=dab: e.tensor_tensor(
            out=s_ch[:, 0:128].rearrange("p (a b) -> p a b", b=64),
            in0=mf[:, MF_XTF + ht * 128:MF_XTF + (ht + 1) * 128]
            .rearrange("p (a b) -> p a b", b=64),
            in1=dab, op=Alu.mult),
            reads=[B_xtf[0], B_dr2], writes=[B_sch])
        p.emit("pool", lambda e, ygs=ygs: e.tensor_tensor(
            out=ygs, in0=ygs, in1=s_ch[:, 0:128], op=Alu.add),
            reads=[B_sch, B_yg], writes=[B_yg])
        p.emit("dve", lambda e, kt=kt, ygs=ygs: e.tensor_tensor(
            out=ygs, in0=ygs, in1=zt[:, kt * 128:(kt + 1) * 128],
            op=Alu.mult),
            reads=[B_zt, B_yg], writes=[B_yg])


def emit_outproj_chunk(E, sw, b, l, ck):
    p = E["p"]
    g = lambda k: E[k]
    mb, mf = g("mb"), g("mf")
    ps, PB = g("ps"), g("PB")
    wring, B_wr = g("wring"), g("B_wr")
    cout_, B_cout = g("cout_"), g("B_cout")
    s_ch, B_sch = g("s_ch"), g("B_sch")
    h_t, B_ht = g("h_t"), g("B_ht")
    ident, B_id = g("ident"), g("B_id")
    yg, B_yg = g("yg"), g("B_yg")
    stat, B_stat = g("stat"), g("B_stat")
    cst, B_cst = g("cst"), g("B_cst")
    B_ygh, B_ygl = g("B_ygh"), g("B_ygl")
    B_ygfh, B_ygfl = g("B_xth"), g("B_xtl")
    MB_YGH, MB_YGL = g("MB_YGH"), g("MB_YGL")
    MB_XTH, MB_XTL = g("MB_XTH"), g("MB_XTL")

    B_xbs = g("B_xbs")
    MF_XBS = g("MF_XBS")
    p.emit("act", lambda e: e.activation(
        out=g("mf")[:, MF_XBS:MF_XBS + DIN], in_=yg[:, 0:DIN],
        func=Act.Square, accum_out=stat[:, 520:521]),
        reads=[B_yg], writes=[B_xbs[0], B_stat])
    p.emit("act", lambda e: e.activation(
        out=stat[:, 521:522], in_=stat[:, 520:521], func=Act.Sqrt,
        scale=float(1.0 / DIN), bias=cst[:, 3:4]),
        reads=[B_stat, B_cst], writes=[B_stat])
    p.emit("dve", lambda e: e.reciprocal(
        out=stat[:, 522:523], in_=stat[:, 521:522]),
        reads=[B_stat], writes=[B_stat])
    p.emit("dve", lambda e: e.tensor_scalar(
        out=yg[:, 0:DIN], in0=yg[:, 0:DIN], scalar1=stat[:, 522:523],
        scalar2=None, op0=Alu.mult),
        reads=[B_stat, B_yg], writes=[B_yg])
    ps6_ = g("ps")
    for tq in range(4):
        def f_ty(e, tq=tq):
            r = None
            for i in range(4):
                t_ = tq * 4 + i
                r = e.transpose(ps[6 + tq % 2][:, i * 128:(i + 1) * 128],
                                yg[:, t_ * 128:(t_ + 1) * 128], ident[:])
            return r
        p.emit("pe", f_ty, reads=[B_yg, B_id], writes=[PB[6 + tq % 2]])
        p.emit("dve", lambda e, tq=tq: e.tensor_copy(
            g("mf")[:, g("MF_XTF") + tq * 512:g("MF_XTF") + (tq + 1) * 512],
            ps[6 + tq % 2][:, 0:512]),
            reads=[PB[6 + tq % 2]], writes=[g("B_xtf")[0]])
    p.emit("act", lambda e: e.copy(
        out=mb[:, MB_XTH:MB_XTH + DIN],
        in_=g("mf")[:, g("MF_XTF"):g("MF_XTF") + DIN]),
        reads=[g("B_xtf")[0]], writes=[B_ygfh[0]])
    p.emit("dve", lambda e: e.scalar_tensor_tensor(
        out=mb[:, MB_XTL:MB_XTL + DIN],
        in0=g("mf")[:, g("MF_XTF"):g("MF_XTF") + DIN],
        scalar=1.0, in1=mb[:, MB_XTH:MB_XTH + DIN],
        op0=Alu.mult, op1=Alu.subtract),
        reads=[g("B_xtf")[0], B_ygfh[0]], writes=[B_ygfl[0]])

    for slab in range(2):
        pb = slab
        for k2 in range(KDIN // 2):
            slot = E.setdefault("wslot", 0)
            E["wslot"] = 1 - slot
            p.emit("sp", lambda e, l=l, slab=slab, k2=k2, slot=slot:
                   e.dma_start(
                       out=wring[:, slot * 2048:slot * 2048 + 2048],
                       in_=g("cout_z")[l, slab, k2]),
                   reads=[B_cout], writes=[B_wr[slot]], dma=1)

            def f_mm(e, k2=k2, slot=slot, pb=pb):
                r = None
                for kl in range(2):
                    k = 2 * k2 + kl
                    base = slot * 2048 + kl * 1024
                    wh = wring[:, base:base + 512]
                    wl = wring[:, base + 512:base + 1024]
                    yh = mb[:, MB_XTH + k * 128:MB_XTH + (k + 1) * 128]
                    yl = mb[:, MB_XTL + k * 128:MB_XTL + (k + 1) * 128]
                    r = e.matmul(ps[pb][:, 0:512], yh, wh,
                                 start=(k == 0), stop=False,
                                 skip_group_check=True)
                    r = e.matmul(ps[pb][:, 0:512], yl, wh, start=False,
                                 stop=False, skip_group_check=True)
                    r = e.matmul(ps[pb][:, 0:512], yh, wl, start=False,
                                 stop=(k == KDIN - 1),
                                 skip_group_check=True)
                return r
            p.emit("pe", f_mm, reads=[B_wr[slot], B_ygfh[0], B_ygfl[0]],
                   writes=[PB[pb]])

        def f_add(e, slab=slab, pb=pb, ck=ck):
            return e.scalar_tensor_tensor(
                out=h_t[:, ck * D + slab * 512:ck * D + (slab + 1) * 512],
                in0=ps[pb][:, 0:512], scalar=1.0,
                in1=h_t[:, ck * D + slab * 512:ck * D + (slab + 1) * 512],
                op0=Alu.mult, op1=Alu.add)
        p.emit("dve", f_add, reads=[PB[pb], B_ht], writes=[B_ht])


def build_full(n_sweeps=8, sweep=512):
    E = build_program(n_sweeps, sweep)
    emit_all(E)
    E["nc"].finalize()
    return E["nc"]


def kernel(**inputs):
    xs = np.ascontiguousarray(np.asarray(inputs["L0_attn_x"], dtype=np.float32))
    in_t = np.ascontiguousarray(
        np.transpose(np.asarray(inputs["L0_attn_in_proj_w"]), (0, 2, 1)))
    out_t = np.ascontiguousarray(
        np.transpose(np.asarray(inputs["L0_attn_out_proj_w"]), (0, 2, 1)))
    common = {
        "w_in_t": in_t,
        "w_out_t": out_t,
        "conv_w": np.ascontiguousarray(inputs["L0_attn_conv_w"]),
        "conv_b": np.ascontiguousarray(inputs["L0_attn_conv_b"]),
        "dt_bias": np.ascontiguousarray(inputs["L0_attn_dt_bias"]),
        "a_log": np.ascontiguousarray(inputs["L0_attn_A_log"]),
        "d_vec": np.ascontiguousarray(inputs["L0_attn_D"]),
        "rms_w": np.ascontiguousarray(inputs["L0_attn_rms_w"]),
        "ln_w": np.ascontiguousarray(inputs["L0_attn_ln_w"]),
        "ln_b": np.ascontiguousarray(inputs["L0_attn_ln_b"]),
        "w_fc": np.ascontiguousarray(inputs["L0_ffn_w_fc"]),
        "b_fc": np.ascontiguousarray(inputs["L0_ffn_b_fc"]),
        "w_proj": np.ascontiguousarray(inputs["L0_ffn_w_proj"]),
        "b_proj": np.ascontiguousarray(inputs["L0_ffn_b_proj"]),
    }
    in_maps = []
    for c in range(N_CORES):
        m = dict(common)
        m["x"] = np.ascontiguousarray(xs[2 * c:2 * c + 2].reshape(2 * L_FULL, D))
        in_maps.append(m)
    nc = build_full()
    res = run_bass_kernel_spmd(nc, in_maps, core_ids=list(range(N_CORES)))
    out = np.concatenate(
        [np.asarray(r["y"]).reshape(2, L_FULL, D) for r in res.results], axis=0)
    return out


# version 07
# speedup vs baseline: 1.3155x; 1.0440x over previous
import os as _os
import numpy as np
import ml_dtypes

_DBG_NL = int(_os.environ.get("KDBG_NL", "2"))
_DBG_FFN = int(_os.environ.get("KDBG_FFN", "1"))

import concourse.bass as bass
import concourse.mybir as mybir
from concourse.bass_utils import run_bass_kernel_spmd

F32 = mybir.dt.float32
BF16 = mybir.dt.bfloat16
Act = mybir.ActivationFunctionType
Alu = mybir.AluOpType
AX = mybir.AxisListType

D = 1024
DIN = 2048
DST = 64
NH = 32
HD = 64
CONVDIM = DIN + 2 * DST
DPROJ = 2 * DIN + 2 * DST + NH
HID = 4096
BK = 32
EPS = 1e-5
LN2 = float(np.log(2.0))
N_CORES = 8
B_FULL, L_FULL = 16, 2048
CH = 128

KD = D // 128
NDC = (DPROJ + 127) // 128
KDIN = DIN // 128
NOC = D // 128
NXT = (CONVDIM + 127) // 128
NHT = DIN // 128



class Step:
    __slots__ = ("fn", "waits", "inc")

    def __init__(self, fn, waits, inc):
        self.fn, self.waits, self.inc = fn, waits, inc


class Buf:
    __slots__ = ("writer", "readers", "name")

    def __init__(self, name=""):
        self.writer = None
        self.readers = []
        self.name = name


class Prog:
    ENGS = ("pe", "dve", "act", "pool", "sp")

    def __init__(self, nc):
        self.nc = nc
        self.steps = {e: [] for e in self.ENGS}
        self.count = {e: 0 for e in self.ENGS}
        self.sems = {}
        self._guards = []
        self.bufsem = {}
        self.nbufsem = 0
        self.waited = {e: {} for e in self.ENGS}

    def sbuf(self, name, shape, dtype):
        g = self.nc.sbuf_tensor(name, list(shape), dtype)
        t = g.__enter__()
        self._guards.append(g)
        return t

    def psum(self, name, shape, dtype=F32):
        g = self.nc.psum_tensor(name, list(shape), dtype)
        t = g.__enter__()
        self._guards.append(g)
        return t

    def emit(self, eng, fn, reads=(), writes=(), dma=0):
        waits = {}
        wd = self.waited[eng]

        def add(dep):
            if dep is None:
                return
            de, dv = dep
            if wd.get(de, 0) >= dv:
                return
            if waits.get(de, 0) < dv:
                waits[de] = dv

        for b in reads:
            add(b.writer)
        for b in writes:
            add(b.writer)
            for r in b.readers:
                add(r)
        for de, dv in waits.items():
            wd[de] = dv
        if dma:
            assert len(writes) == 1, "dma step must write exactly one Buf"
            bkey = id(writes[0])
            sname = self.bufsem.get(bkey)
            if sname is None:
                sname = "D%d" % self.nbufsem
                self.nbufsem += 1
                self.bufsem[bkey] = sname
                self.count[sname] = 0
                self.steps.setdefault(sname, None)
            inc = 16 * dma
            self.count[sname] += inc
            tag = (sname, self.count[sname])
            for b in reads:
                b.readers.append(tag)
            writes[0].writer = tag
            writes[0].readers = []
            self.steps[eng].append(Step(fn, waits, (sname, inc)))
            return tag
        self.count[eng] += 1
        tag = (eng, self.count[eng])
        for b in reads:
            b.readers.append(tag)
        for b in writes:
            b.writer = tag
            b.readers = []
        self.steps[eng].append(Step(fn, waits, 1))
        return tag

    def barrier(self):
        snap = dict(self.count)
        for e in self.ENGS:
            waits = {d: v for d, v in snap.items() if v > 0}
            self.count[e] += 1
            self.steps[e].append(Step(lambda en: en.nop(), waits, 1))
            self.waited[e] = dict(waits)

    def build(self):
        nc = self.nc
        for e in list(self.count):
            g = nc.semaphore("S" + e)
            self.sems[e] = g.__enter__()
            self._guards.append(g)
        blk_g = nc.Block()
        blk = blk_g.__enter__()

        def runner(name):
            def run(e):
                cnt = 0
                for st in self.steps[name]:
                    if not st.waits:
                        e.wait_ge(self.sems[name], cnt)
                    for de, dv in sorted(st.waits.items()):
                        e.wait_ge(self.sems[de], dv)
                    if not isinstance(st.inc, tuple):
                        cnt += st.inc
                    last = st.fn(e)
                    if isinstance(st.inc, tuple):
                        sname, inc = st.inc
                        insts = last if isinstance(last, (list, tuple)) else [last]
                        assert len(insts) * 16 == inc
                        for i in insts:
                            i.then_inc(self.sems[sname], 16)
                    else:
                        if isinstance(last, (list, tuple)):
                            last = last[-1]
                        last.then_inc(self.sems[name], 1)
            return run

        blk.tensor(runner("pe"))
        blk.vector(runner("dve"))
        blk.scalar(runner("act"))
        blk.gpsimd(runner("pool"))
        blk.sync(runner("sp"))
        blk_g.__exit__(None, None, None)



def quant_ladder(p, eng, buf_io, src_ap, dst_ap, inv_ap, sc_ap, t_ap, b_ap,
                 nlv=7, span=32.0):
    reads, writes = buf_io
    lv = [2 ** (nlv - 1 - i) for i in range(nlv)]

    def f_mul(e):
        return e.tensor_tensor(out=t_ap, in0=src_ap, in1=inv_ap, op=Alu.mult)
    p.emit(eng, f_mul, reads=reads, writes=[writes[0]])

    def f_add(e):
        return e.tensor_scalar(out=t_ap, in0=t_ap, scalar1=span + 0.5,
                               scalar2=None, op0=Alu.add)
    p.emit(eng, f_add, reads=[writes[0]], writes=[writes[0]])
    for c in lv:
        def f_ge(e, c=c):
            return e.tensor_scalar(out=b_ap, in0=t_ap, scalar1=float(c),
                                   scalar2=None, op0=Alu.is_ge)
        p.emit(eng, f_ge, reads=[writes[0]], writes=[writes[1]])

        def f_up(e, c=c):
            return e.scalar_tensor_tensor(out=t_ap, in0=b_ap, scalar=float(-c),
                                          in1=t_ap, op0=Alu.mult, op1=Alu.add)
        p.emit(eng, f_up, reads=[writes[1], writes[0]], writes=[writes[0]])

    def f_u2(e):
        return e.tensor_tensor(out=b_ap, in0=src_ap, in1=inv_ap, op=Alu.mult)
    p.emit(eng, f_u2, reads=reads + [writes[0]], writes=[writes[1]])

    def f_d(e):
        return e.scalar_tensor_tensor(out=t_ap, in0=b_ap, scalar=0.5,
                                      in1=t_ap, op0=Alu.add, op1=Alu.subtract)
    p.emit(eng, f_d, reads=[writes[1], writes[0]], writes=[writes[0]])

    def f_q(e):
        return e.tensor_tensor(out=dst_ap, in0=t_ap, in1=sc_ap, op=Alu.mult)
    p.emit(eng, f_q, reads=[writes[0]], writes=[writes[2]])


def exp_ladder(p, eng, rw, lg_ap, e_out_ap, t_ap, b_ap, nlv=9, span=256.0):
    reads, wT, wB, wE = rw
    def f0(e):
        return e.tensor_scalar(out=t_ap, in0=lg_ap, scalar1=span, scalar2=None,
                               op0=Alu.add)
    p.emit(eng, f0, reads=reads, writes=[wT])
    for i in range(nlv):
        c = 2 ** (nlv - 1 - i)
        def f_ge(e, c=c):
            return e.tensor_scalar(out=b_ap, in0=t_ap, scalar1=float(c),
                                   scalar2=None, op0=Alu.is_ge)
        p.emit(eng, f_ge, reads=[wT], writes=[wB])

        def f_up(e, c=c):
            return e.scalar_tensor_tensor(out=t_ap, in0=b_ap, scalar=float(-c),
                                          in1=t_ap, op0=Alu.mult, op1=Alu.add)
        p.emit(eng, f_up, reads=[wB, wT], writes=[wT])
    def f_e(e):
        return e.scalar_tensor_tensor(out=e_out_ap, in0=lg_ap, scalar=1.0,
                                      in1=t_ap, op0=Alu.mult, op1=Alu.subtract)
    p.emit(eng, f_e, reads=reads + [wT], writes=[wE])



def build_program(n_sweeps=8, sweep=512):
    NT = sweep // 128
    NCH = sweep // CH
    TOK = n_sweeps * sweep
    HSL = HID // 512
    PSL = D // 512

    nc = bass.Bass("TRN2", target_bir_lowering=False)
    p = Prog(nc)
    dt_ = nc.dram_tensor

    x_in = dt_("x", [TOK, D], F32, kind="ExternalInput").ap()
    w_in_t = dt_("w_in_t", [2, D, DPROJ], F32, kind="ExternalInput").ap()
    w_out_t = dt_("w_out_t", [2, DIN, D], F32, kind="ExternalInput").ap()
    conv_w = dt_("conv_w", [2, CONVDIM, 4], F32, kind="ExternalInput").ap()
    conv_b = dt_("conv_b", [2, CONVDIM], F32, kind="ExternalInput").ap()
    dt_bias = dt_("dt_bias", [2, NH], F32, kind="ExternalInput").ap()
    a_log = dt_("a_log", [2, NH], F32, kind="ExternalInput").ap()
    d_vec = dt_("d_vec", [2, NH], F32, kind="ExternalInput").ap()
    rms_w = dt_("rms_w", [2, DIN], F32, kind="ExternalInput").ap()
    ln_w = dt_("ln_w", [2, D], F32, kind="ExternalInput").ap()
    ln_b = dt_("ln_b", [2, D], F32, kind="ExternalInput").ap()
    w_fc = dt_("w_fc", [HID, D], F32, kind="ExternalInput").ap()
    b_fc = dt_("b_fc", [HID], F32, kind="ExternalInput").ap()
    w_proj = dt_("w_proj", [D, HID], F32, kind="ExternalInput").ap()
    b_proj = dt_("b_proj", [D], F32, kind="ExternalInput").ap()
    y_out = dt_("y", [TOK, D], F32, kind="ExternalOutput").ap()

    cin = dt_("cin", [2, NDC, KD, 2, 128, 128], BF16).ap()
    cin_z = dt_("cin_z", [2, 4, KD // 2, 128, 2048], BF16).ap()
    cout_z = dt_("cout_z", [2, 2, KDIN // 2, 128, 2048], BF16).ap()
    cout_ = dt_("cout", [2, NOC, KDIN, 2, 128, 128], BF16).ap()
    cfc = dt_("cfc", [KD, HSL, 128, 512], BF16).ap()
    cproj = dt_("cproj", [HID // 128, PSL, 128, 512], BF16).ap()
    cbv = dt_("cbv", [2, NDC * 128], F32).ap()
    dr_a = dt_("dr_a", [NH, sweep], F32).ap()
    dr_ea = dt_("dr_ea", [NH, sweep], F32).ap()
    dr_rs = dt_("dr_rs", [sweep], F32).ap()
    B_cin, B_cout, B_cfc, B_cproj, B_cbv = (Buf(), Buf(), Buf(), Buf(), Buf())
    B_dra, B_drea, B_drrs, B_y = Buf(), Buf(), Buf(), Buf()

    sb = p.sbuf
    h_t = sb("h_t", [128, NT * D], F32); B_ht = Buf()
    state = sb("state", [128, 2 * DIN], F32); B_st = [Buf(), Buf()]
    stk = sb("stk", [96, sweep], F32); B_stk = Buf()
    dts = sb("dts", [32, 3 * sweep], F32)
    B_dt0 = Buf(); B_dt1 = Buf(); B_dt2 = Buf()
    ident = sb("ident", [128, 128], F32); B_id = Buf()
    identb = sb("identb", [128, 128], BF16); B_idb = Buf()
    mask01 = sb("mask01", [128, CH], F32); B_mk = Buf()
    imask = sb("imask", [128, CH], mybir.dt.int32); B_imk = Buf()
    ccol = sb("ccol", [128, 16], F32); B_cc = Buf()
    drep = sb("drep", [128, 2 * NHT], F32); B_dr = Buf()
    cbcol = sb("cbcol", [128, 2 * NXT], F32); B_cb = Buf()
    bvcol = sb("bvcol", [128, 2 * NDC], F32); B_bv = Buf()
    convc = sb("convc", [128, 2 * 2 * NXT * 4], F32)
    B_cvc = [[Buf(), Buf()], [Buf(), Buf()]]
    cw = sb("cw", [128, 2 * NXT * 4], F32); B_cw = Buf()
    B_exp = Buf("exp")
    misc = sb("misc", [128, 1024], F32); B_misc = Buf()
    B_bt = Buf()
    stat = sb("stat", [128, 768], F32); B_stat = Buf()
    aeb = sb("aeb", [128, 64], F32); B_aeb = Buf()
    wTt = sb("wTt", [128, 32], F32); B_wT = Buf()
    abg = sb("abg", [128, 512], F32); B_abg = [Buf()]
    uf_hi = sb("uf_hi", [128, KD * sweep], BF16); B_ufh = Buf()
    uf_lo = sb("uf_lo", [128, KD * sweep], BF16); B_ufl = Buf()
    if KD * sweep >= HID:
        hqs = uf_lo[:, 0:HID]
    else:
        hqs = sb("hqs", [128, HID], BF16)
    wring = sb("wring", [128, 4096], BF16)
    B_wr = [Buf(), Buf()]
    wf = wring[:, 0:2048]
    B_wf = [Buf() for _ in range(4)]
    B_hqs = Buf()
    cst = sb("cst", [128, 4], F32)
    B_cst = Buf()
    drep2 = sb("drep2", [128, 2 * NH], F32)
    B_dr2 = Buf()
    mb = sb("mb", [128, 16384], BF16); B_mb_hqT = Buf()
    uh = mb[:, 0:NT * D]; B_uh = Buf()
    ul = mb[:, NT * D:2 * NT * D]; B_ul = Buf()
    MB_XHH, MB_XHL = 0, 2048
    MB_XTH, MB_XTL = 4096, 6144
    MB_YGH, MB_YGL = 8192, 10240
    MB_BPK, MB_CPA, MB_CPB = 12288, 12416, 12544
    MB_STH, MB_STL = 12672, 13184
    B_xhh = [Buf(), Buf()]; B_xhl = [Buf(), Buf()]
    B_xth = [Buf(), Buf()]; B_xtl = [Buf(), Buf()]
    B_ygh = Buf(); B_ygl = Buf()
    B_bpk = Buf(); B_cpa = Buf(); B_cpb = Buf()
    B_sth = [Buf() for _ in range(4)]; B_stl = [Buf() for _ in range(4)]
    mf = sb("mf", [128, 8192], F32)
    MF_XBS, MF_XTF, MF_BW = 0, 2176, 4224
    MF_STF, MF_ST0, MF_BCT, MF_ADT, MF_WRK = 4736, 5248, 5376, 5440, 5568
    B_xbs = [Buf(), Buf()]; B_xtf = [Buf(), Buf()]; B_ea = Buf(); B_bw = Buf()
    B_stf = [Buf() for _ in range(4)]; B_st0 = Buf(); B_bcT = Buf()
    B_adt = [Buf(), Buf()]; B_wrk = Buf()
    xbc_r = sb("xbc_r", [128, NXT * (sweep + 4)], F32); B_xbr = Buf()
    dec = sb("dec", [128, DIN], F32); B_dec = Buf()
    s_ch = sb("s_ch", [128, 512], F32); B_sch = Buf()
    yg = sb("yg", [128, NHT * CH], F32); B_yg = Buf()
    zt = sb("zt", [128, 512], F32); B_zt = Buf()
    btile = zt
    sg = sb("sg", [128, 1024], F32); B_sg = Buf()
    xr = sb("xr", [128, 1024], F32); B_xr = Buf()

    ps = [p.psum(f"ps{i}", [128, 512]) for i in range(8)]
    PB = [Buf(f"ps{i}") for i in range(8)]

    E = dict(locals())
    E["p"] = p
    E["n_sweeps"] = n_sweeps
    E["sweep"] = sweep
    E["NT"] = NT
    E["NCH"] = NCH
    E["HSL"] = HSL
    E["PSL"] = PSL
    return E


def emit_all(E):
    emit_consts(E)
    emit_transform(E)
    E["p"].barrier()
    n_sweeps = E["n_sweeps"]
    for sw in range(n_sweeps):
        b = 0 if sw < n_sweeps // 2 else 1
        emit_sweep(E, sw, b)
    p = E["p"]
    p.emit("sp", lambda e: e.nop(), reads=[E["B_y"]])
    p.build()


def emit_consts(E):
    p = E["p"]
    g = lambda k: E[k]
    imask, mask01, ident = g("imask"), g("mask01"), g("ident")
    B_imk, B_mk, B_id = g("B_imk"), g("B_mk"), g("B_id")
    ccol, B_cc = g("ccol"), g("B_cc")
    drep, B_dr = g("drep"), g("B_dr")
    cbcol, B_cb = g("cbcol"), g("B_cb")
    misc, B_misc = g("misc"), g("B_misc")
    convc = g("convc")
    state, B_st = g("state"), g("B_st")
    stat, B_stat = g("stat"), g("B_stat")
    a_log, dt_bias, d_vec, conv_b, b_proj = (
        g("a_log"), g("dt_bias"), g("d_vec"), g("conv_b"), g("b_proj"))

    cst, B_cst = g("cst"), g("B_cst")
    p.emit("dve", lambda e: e.memset(cst[:, 0:1], 1.0), writes=[B_cst])
    p.emit("dve", lambda e: e.memset(cst[:, 1:2], float(3.0 * LN2)),
           writes=[B_cst])
    p.emit("dve", lambda e: e.memset(cst[:, 2:3], float(-3.0 * LN2)),
           writes=[B_cst])
    p.emit("dve", lambda e: e.memset(cst[:, 3:4], float(EPS)), writes=[B_cst])
    p.emit("pool", lambda e: e.iota(
        imask[:], pattern=[[1, CH]], base=0, channel_multiplier=-1),
        writes=[B_imk])
    p.emit("dve", lambda e: e.tensor_scalar(
        out=mask01[:], in0=imask[:], scalar1=0.0, scalar2=None, op0=Alu.is_ge),
        reads=[B_imk], writes=[B_mk])
    p.emit("dve", lambda e: e.tensor_scalar(
        out=ident[:], in0=imask[:], scalar1=0.0, scalar2=None, op0=Alu.is_equal),
        reads=[B_imk], writes=[B_id])
    p.emit("act", lambda e: e.copy(out=g("identb")[:], in_=ident[:]),
           reads=[B_id], writes=[g("B_idb")])
    p.emit("dve", lambda e: e.memset(state[:], 0.0), writes=[B_st[0], B_st[1]])
    p.emit("dve", lambda e: e.memset(convc[:], 0.0),
           writes=[b for lb in E["B_cvc"] for b in lb])
    for l in range(2):
        p.emit("sp", lambda e, l=l: [
            e.dma_start(out=ccol[0:NH, 4 + l:5 + l],
                        in_=a_log[l, :].unsqueeze(1)),
            e.dma_start(out=ccol[0:NH, 6 + l:7 + l],
                        in_=dt_bias[l, :].unsqueeze(1)),
        ], writes=[E["B_cc"]], dma=2)
        p.emit("act", lambda e, l=l: e.activation(
            out=ccol[0:NH, 2 + l:3 + l], in_=ccol[0:NH, 4 + l:5 + l],
            func=Act.Exp), reads=[B_cc], writes=[B_cc])
        p.emit("dve", lambda e, l=l: e.tensor_scalar(
            out=ccol[0:NH, l:l + 1], in0=ccol[0:NH, 2 + l:3 + l],
            scalar1=-1.0, scalar2=None, op0=Alu.mult),
            reads=[B_cc], writes=[B_cc])
        p.emit("sp", lambda e, l=l: [
            e.dma_start(
                out=drep[0:64, l * NHT:(l + 1) * NHT],
                in_=d_vec[l, :].rearrange("(t h) -> h t", h=2)[0]
                .unsqueeze(0).broadcast_to([64, NHT]),
                allow_slow_non_contiguous=True),
            e.dma_start(
                out=drep[64:128, l * NHT:(l + 1) * NHT],
                in_=d_vec[l, :].rearrange("(t h) -> h t", h=2)[1]
                .unsqueeze(0).broadcast_to([64, NHT]),
                allow_slow_non_contiguous=True)],
            writes=[B_dr], dma=2)
        p.emit("sp", lambda e, l=l: e.dma_start(
            out=g("drep2")[:, l * NH:(l + 1) * NH],
            in_=d_vec[l, :].unsqueeze(0).broadcast_to([128, NH])),
            writes=[g("B_dr2")], dma=1)
        p.emit("sp", lambda e, l=l: e.dma_start(
            out=cbcol[:, l * NXT:(l + 1) * NXT],
            in_=conv_b[l, :].rearrange("(a b) -> b a", b=128),
            allow_slow_non_contiguous=True),
            writes=[B_cb], dma=1)
        p.emit("sp", lambda e, l=l: e.dma_start(
            out=g("cw")[:, l * NXT * 4:(l + 1) * NXT * 4]
            .rearrange("p (t k) -> p t k", k=4),
            in_=g("conv_w")[l].rearrange("(t p) k -> p t k", p=128)),
            writes=[g("B_cw")], dma=1)
    p.emit("sp", lambda e: e.dma_start(
        out=misc[:, 0:D],
        in_=b_proj[:].unsqueeze(0).broadcast_to([128, D])),
        writes=[B_misc], dma=1)


def _wsplit_store(p, E, src_f32_ap, ncols, scol_ap, hi_ap, lo_ap,
                  rbufs, wbufs):
    mf, B = E["mf"], None

    def f_sc(e):
        return e.tensor_scalar(out=src_f32_ap, in0=src_f32_ap,
                               scalar1=scol_ap, scalar2=None, op0=Alu.mult)
    p.emit("dve", f_sc, reads=rbufs, writes=[wbufs[0]])

    def f_hi(e):
        return e.copy(out=hi_ap, in_=src_f32_ap)
    p.emit("act", f_hi, reads=[wbufs[0]], writes=[wbufs[1]])

    def f_lo(e):
        return e.scalar_tensor_tensor(out=lo_ap, in0=src_f32_ap, scalar=1.0,
                                      in1=hi_ap, op0=Alu.mult, op1=Alu.subtract)
    p.emit("dve", f_lo, reads=[wbufs[0], wbufs[1]], writes=[wbufs[2]])


def emit_transform(E):
    p = E["p"]
    g = lambda k: E[k]
    nc = g("nc")
    mf, mb, xbc_r, stat = g("mf"), g("mb"), g("xbc_r"), g("stat")
    B_wrk, B_xbr, B_stat = g("B_wrk"), g("B_xbr"), g("B_stat")
    B_mbs = Buf("mb_scr")
    ps, PB = g("ps"), g("PB")
    w_in_t, ln_w, ln_b = g("w_in_t"), g("ln_w"), g("ln_b")
    w_out_t, rms_w = g("w_out_t"), g("rms_w")
    w_fc, w_proj = g("w_fc"), g("w_proj")
    cin, cout_, cfc, cproj, cbv = g("cin"), g("cout_"), g("cfc"), g("cproj"), g("cbv")
    cin_z, cout_z = g("cin_z"), g("cout_z")
    B_cin, B_cout, B_cfc, B_cproj, B_cbv = (
        g("B_cin"), g("B_cout"), g("B_cfc"), g("B_cproj"), g("B_cbv"))
    bvcol, B_bv = g("bvcol"), g("B_bv")
    ccol, B_cc = g("ccol"), g("B_cc")
    s_ch, B_sch = g("s_ch"), g("B_sch")
    dec, B_dec = g("dec"), g("B_dec")

    p.emit("dve", lambda e: e.memset(mb[:, 4256:4352], 0.0), writes=[B_mbs])
    p.emit("dve", lambda e: e.memset(mb[:, 4352 + 4256:4352 + 4352], 0.0),
           writes=[B_mbs])

    for l in range(2):
        for k in range(KD):
            p.emit("sp", lambda e, l=l, k=k: e.dma_start(
                out=mf[:, 0:DPROJ], in_=w_in_t[l, k * 128:(k + 1) * 128, :]),
                writes=[B_wrk], dma=1)
            p.emit("sp", lambda e, l=l, k=k: [
                e.dma_start(out=stat[:, 0:1],
                            in_=ln_w[l, k * 128:(k + 1) * 128].unsqueeze(1)),
                e.dma_start(out=stat[:, 1:2],
                            in_=ln_b[l, k * 128:(k + 1) * 128].unsqueeze(1))],
                writes=[B_stat], dma=2)

            def f_bv(e, l=l, k=k):
                r = None
                for nt in range(8):
                    r = e.matmul(ps[nt][0:1, :], stat[:, 1:2],
                                 mf[:, nt * 512:(nt + 1) * 512],
                                 start=(k == 0), stop=(k == KD - 1))
                return r
            p.emit("pe", f_bv, reads=[B_wrk, B_stat],
                   writes=[PB[nt] for nt in range(8)])

            _wsplit_store(p, E, mf[:, 0:DPROJ], DPROJ, stat[:, 0:1],
                          mb[:, 0:DPROJ], mb[:, 4352:4352 + DPROJ],
                          [B_stat], [B_wrk, B_mbs, B_mbs])

            def f_st(e, l=l, k=k):
                ops = []
                for h in range(2):
                    hoff = 4352 * h
                    a0 = (k % 2) * 2 + h
                    ops.append(e.dma_start(
                        out=cin_z[l, :, k // 2, :, a0 * 512:(a0 + 1) * 512]
                        .rearrange("g p c -> p g c"),
                        in_=mb[:, hoff:hoff + 2048]
                        .rearrange("p (g c) -> p g c", c=512)))
                    ops.append(e.dma_start(
                        out=cin[l, 16:NDC, k, h].rearrange("c p n -> p c n"),
                        in_=mb[:, hoff + 2048:hoff + 4352]
                        .rearrange("p (c n) -> p c n", n=128)))
                return ops
            p.emit("sp", f_st, reads=[B_mbs], writes=[B_cin], dma=4)
        for nt in range(8):
            def f_ev(e, nt=nt):
                return e.tensor_copy(s_ch[0:1, :], ps[nt][0:1, :])
            p.emit("dve", f_ev, reads=[PB[nt]], writes=[B_sch])

            def f_dst(e, l=l, nt=nt):
                return e.dma_start(out=cbv[l, nt * 512:(nt + 1) * 512]
                                   .unsqueeze(0), in_=s_ch[0:1, :])
            p.emit("act", f_dst, reads=[B_sch], writes=[B_cbv], dma=1)
        for k in range(KD):
            p.emit("sp", lambda e, l=l, k=k: e.dma_start(
                out=dec[:, 0:160],
                in_=w_in_t[l, k * 128:(k + 1) * 128, 4096:4256]),
                writes=[B_dec], dma=1)
            p.emit("sp", lambda e, l=l, k=k: e.dma_start(
                out=stat[:, 1:2],
                in_=ln_b[l, k * 128:(k + 1) * 128].unsqueeze(1)),
                writes=[B_stat], dma=1)

            def f_bv2(e, l=l, k=k):
                return e.matmul(ps[0][0:1, 0:160], stat[:, 1:2], dec[:, 0:160],
                                start=(k == 0), stop=(k == KD - 1))
            p.emit("pe", f_bv2, reads=[B_dec, B_stat], writes=[PB[0]])
        p.emit("dve", lambda e: e.tensor_copy(s_ch[0:1, 0:160],
                                              ps[0][0:1, 0:160]),
               reads=[PB[0]], writes=[B_sch])
        p.emit("act", lambda e, l=l: e.dma_start(
            out=cbv[l, 4096:4256].unsqueeze(0), in_=s_ch[0:1, 0:160]),
            reads=[B_sch], writes=[B_cbv], dma=1)
        p.emit("dve", lambda e: e.memset(s_ch[0:1, 160:256], 0.0),
               writes=[B_sch])
        p.emit("act", lambda e, l=l: e.dma_start(
            out=cbv[l, 4256:4352].unsqueeze(0), in_=s_ch[0:1, 160:256]),
            reads=[B_sch], writes=[B_cbv], dma=1)
        p.emit("sp", lambda e, l=l: e.dma_start(
            out=bvcol[:, l * NDC:(l + 1) * NDC],
            in_=cbv[l, :].rearrange("(a b) -> b a", b=128),
            allow_slow_non_contiguous=True),
            reads=[B_cbv], writes=[B_bv], dma=1)
        p.emit("dve", lambda e, l=l: e.tensor_scalar(
            out=ccol[0:NH, 8 + l:9 + l], in0=bvcol[0:NH, l * NDC + 33:l * NDC + 34],
            scalar1=ccol[0:NH, 6 + l:7 + l], scalar2=None, op0=Alu.add),
            reads=[B_bv, B_cc], writes=[B_cc])

        for k in range(KDIN):
            p.emit("sp", lambda e, l=l, k=k: e.dma_start(
                out=mf[:, 0:D], in_=w_out_t[l, k * 128:(k + 1) * 128, :]),
                writes=[B_wrk], dma=1)
            p.emit("sp", lambda e, l=l, k=k: e.dma_start(
                out=stat[:, 0:1],
                in_=rms_w[l, k * 128:(k + 1) * 128].unsqueeze(1)),
                writes=[B_stat], dma=1)
            _wsplit_store(p, E, mf[:, 0:D], D, stat[:, 0:1],
                          mb[:, 0:D], mb[:, 4352:4352 + D],
                          [B_stat], [B_wrk, B_mbs, B_mbs])

            def f_so(e, l=l, k=k):
                ops = []
                for h in range(2):
                    hoff = 4352 * h
                    a0 = (k % 2) * 2 + h
                    ops.append(e.dma_start(
                        out=cout_z[l, :, k // 2, :, a0 * 512:(a0 + 1) * 512]
                        .rearrange("s p c -> p s c"),
                        in_=mb[:, hoff:hoff + D]
                        .rearrange("p (s c) -> p s c", c=512)))
                return ops
            p.emit("sp", f_so, reads=[B_mbs], writes=[B_cout], dma=2)

    for which, wap, rows, blocks, cdst, B_c in (
            ("fc", w_fc, HID // 128, D // BK, cfc, B_cfc),
            ("proj", w_proj, D // 128, HID // BK, cproj, B_cproj)):
        ncols = D if which == "fc" else HID
        nkt = ncols // 128
        for r in range(rows):
            p.emit("sp", lambda e, wap=wap, r=r, ncols=ncols: e.dma_start(
                out=mf[:, 0:ncols], in_=wap[r * 128:(r + 1) * 128, :]),
                writes=[B_wrk], dma=1)
            for q in range(ncols // 1024):
                emit_quant(E, "dve", mf[:, q * 1024:(q + 1) * 1024],
                           mb[:, q * 1024:(q + 1) * 1024], 1024,
                           [B_wrk], B_mbs,
                           stat, B_stat, mf_scr(E), B_wrk2(E))
            for kt in range(nkt):
                eng = "sp" if kt % 2 == 0 else "act"
                p.emit(eng, lambda e, kt=kt, ncols=ncols: e.dma_start_transpose(
                    mb[:, 8704 + kt * 128:8704 + (kt + 1) * 128],
                    mb[:, kt * 128:(kt + 1) * 128]),
                    reads=[B_mbs], writes=[B_mbs], dma=1)
            def f_str(e, r=r, cdst=cdst, nkt=nkt):
                return e.dma_start(
                    out=cdst[:, r // 4, :, (r % 4) * 128:(r % 4) * 128 + 128]
                    .rearrange("c p n -> p c n"),
                    in_=mb[:, 8704:8704 + nkt * 128]
                    .rearrange("p (c n) -> p c n", n=128))
            p.emit("sp", f_str, reads=[B_mbs], writes=[B_c], dma=1)


def mf_scr(E):
    return E["mf"][:, 4608:6656]


def B_wrk2(E):
    return E["B_wrk"]


def emit_quant(E, eng, src_ap, dst_ap, ncols, rbufs, B_dst, stat, B_stat,
               scr, B_scr):
    p = E["p"]
    nb = ncols // BK
    am = stat[:, 0:nb]
    am2 = stat[:, nb:2 * nb]
    lgc = stat[:, 2 * nb:3 * nb]
    ev = stat[:, 3 * nb:4 * nb]
    tt = stat[:, 4 * nb:5 * nb]
    bb = stat[:, 5 * nb:6 * nb]
    inv = stat[:, 6 * nb:7 * nb]
    sc = stat[:, 7 * nb:8 * nb]

    def f_am(e):
        return e.tensor_reduce(out=am, in_=src_ap.rearrange(
            "p (b k) -> p b k", k=BK), axis=AX.X, op=Alu.max,
            apply_absolute_value=True)
    p.emit(eng, f_am, reads=rbufs, writes=[B_stat])

    def f_am2(e):
        return e.tensor_scalar(out=am2, in0=am, scalar1=float(2.0 ** -126),
                               scalar2=None, op0=Alu.max)
    p.emit(eng, f_am2, reads=[B_stat], writes=[B_stat])

    def f_lg(e):
        return e.activation(out=lgc, in_=am2, func=Act.Ln)
    p.emit("act", f_lg, reads=[B_stat], writes=[B_stat])

    def f_lgc(e):
        return e.tensor_scalar(out=lgc, in0=lgc, scalar1=float(1.0 / LN2),
                               scalar2=None, op0=Alu.mult)
    p.emit(eng, f_lgc, reads=[B_stat], writes=[B_stat])

    exp_ladder(p, eng, ([B_stat], B_stat, B_stat, B_stat), lgc, ev, tt, bb)

    cst, B_cst = E["cst"], E["B_cst"]

    def f_inv(e):
        return e.activation(out=inv, in_=ev, func=Act.Exp, scale=-LN2,
                            bias=cst[:, 1:2])
    p.emit("act", f_inv, reads=[B_stat, B_cst], writes=[B_stat])

    def f_sc(e):
        return e.activation(out=sc, in_=ev, func=Act.Exp, scale=LN2,
                            bias=cst[:, 2:3])
    p.emit("act", f_sc, reads=[B_stat, B_cst], writes=[B_stat])

    inv_b = inv.unsqueeze(2).broadcast_to([128, nb, BK])
    sc_b = sc.unsqueeze(2).broadcast_to([128, nb, BK])
    src3 = src_ap.rearrange("p (b k) -> p b k", k=BK)
    dst3 = dst_ap.rearrange("p (b k) -> p b k", k=BK)
    quant_ladder(p, eng, (rbufs + [B_stat], [B_scr, B_scr, B_dst]),
                 src3, dst3, inv_b, sc_b,
                 scr[:, 0:ncols].rearrange("p (b k) -> p b k", k=BK),
                 scr[:, ncols:2 * ncols].rearrange("p (b k) -> p b k", k=BK))


def emit_qstats(E, units):
    p = E["p"]
    dec, B_dec = E["dec"], E["B_dec"]
    yg, B_yg = E["yg"], E["B_yg"]
    cst, B_cst = E["cst"], E["B_cst"]
    SC = len(units) * 32
    for u, (src, rbufs) in enumerate(units):
        p.emit("dve", lambda e, src=src, u=u: e.tensor_reduce(
            out=dec[:, u * 32:(u + 1) * 32],
            in_=src.rearrange("p (b k) -> p b k", k=BK), axis=AX.X,
            op=Alu.max, apply_absolute_value=True),
            reads=rbufs, writes=[B_dec])
    p.emit("dve", lambda e: e.tensor_scalar(
        out=dec[:, 0:SC], in0=dec[:, 0:SC], scalar1=float(2.0 ** -126),
        scalar2=None, op0=Alu.max), reads=[B_dec], writes=[B_dec])
    p.emit("act", lambda e: e.activation(
        out=dec[:, SC:2 * SC], in_=dec[:, 0:SC], func=Act.Ln),
        reads=[B_dec], writes=[B_dec])
    p.emit("dve", lambda e: e.tensor_scalar(
        out=dec[:, SC:2 * SC], in0=dec[:, SC:2 * SC],
        scalar1=float(1.0 / LN2), scalar2=None, op0=Alu.mult),
        reads=[B_dec], writes=[B_dec])
    exp_ladder(p, "dve", ([B_dec], B_yg, B_yg, B_dec),
               dec[:, SC:2 * SC], dec[:, SC:2 * SC],
               yg[:, 0:SC], yg[:, SC:2 * SC])
    p.emit("act", lambda e: e.activation(
        out=dec[:, 2 * SC:3 * SC], in_=dec[:, SC:2 * SC], func=Act.Exp,
        scale=-LN2, bias=cst[:, 1:2]),
        reads=[B_dec, B_cst], writes=[B_dec])
    p.emit("act", lambda e: e.activation(
        out=dec[:, 3 * SC:4 * SC], in_=dec[:, SC:2 * SC], func=Act.Exp,
        scale=LN2, bias=cst[:, 2:3]),
        reads=[B_dec, B_cst], writes=[B_dec])
    return SC


def emit_vquant(E, src_ap, rbufs, dst_ap, B_dst, u, SC):
    p = E["p"]
    dec, B_dec = E["dec"], E["B_dec"]
    yg, B_yg = E["yg"], E["B_yg"]
    inv_b = dec[:, 2 * SC + u * 32:2 * SC + (u + 1) * 32] \
        .unsqueeze(2).broadcast_to([128, 32, BK])
    sc_b = dec[:, 3 * SC + u * 32:3 * SC + (u + 1) * 32] \
        .unsqueeze(2).broadcast_to([128, 32, BK])
    quant_ladder(p, "dve", (rbufs + [B_dec], [B_yg, B_yg, B_dst]),
                 src_ap.rearrange("p (b k) -> p b k", k=BK),
                 dst_ap.rearrange("p (b k) -> p b k", k=BK),
                 inv_b, sc_b,
                 yg[:, 0:1024].rearrange("p (b k) -> p b k", k=BK),
                 yg[:, 1024:2048].rearrange("p (b k) -> p b k", k=BK),
                 nlv=6)


def emit_ln_stats(E, src_ap, rbufs, nt_col, u_out_ap, wbufs, x2_scr_ap, B_x2):
    p = E["p"]
    stat, B_stat = E["stat"], E["B_stat"]
    c0 = 512 + nt_col * 8
    sx = stat[:, c0:c0 + 1]
    sxx = stat[:, c0 + 1:c0 + 2]
    m = stat[:, c0 + 2:c0 + 3]
    b4 = stat[:, c0 + 3:c0 + 4]
    rstd = stat[:, c0 + 4:c0 + 5]

    p.emit("dve", lambda e: e.tensor_reduce(
        out=sx, in_=src_ap, axis=AX.X, op=Alu.add),
        reads=rbufs, writes=[B_stat])
    p.emit("act", lambda e: e.activation(
        out=x2_scr_ap, in_=src_ap, func=Act.Square, accum_out=sxx),
        reads=rbufs, writes=[B_x2, B_stat])
    p.emit("dve", lambda e: e.tensor_scalar(
        out=m, in0=sx, scalar1=float(1.0 / D), scalar2=None, op0=Alu.mult),
        reads=[B_stat], writes=[B_stat])
    p.emit("dve", lambda e: e.tensor_scalar(
        out=b4, in0=m, scalar1=m, scalar2=-1.0, op0=Alu.mult, op1=Alu.mult),
        reads=[B_stat], writes=[B_stat])
    p.emit("dve", lambda e: e.tensor_scalar(
        out=b4, in0=b4, scalar1=EPS, scalar2=None, op0=Alu.add),
        reads=[B_stat], writes=[B_stat])
    p.emit("act", lambda e: e.activation(
        out=rstd, in_=sxx, func=Act.Sqrt, scale=float(1.0 / D), bias=b4),
        reads=[B_stat], writes=[B_stat])
    p.emit("dve", lambda e: e.reciprocal(out=rstd, in_=rstd),
        reads=[B_stat], writes=[B_stat])
    p.emit("dve", lambda e: e.tensor_scalar(
        out=u_out_ap, in0=src_ap, scalar1=m, scalar2=rstd,
        op0=Alu.subtract, op1=Alu.mult),
        reads=rbufs + [B_stat], writes=wbufs)
    return rstd


def emit_sweep(E, sw, b):
    p = E["p"]
    g = lambda k: E[k]
    sweep, NT, NCH = g("sweep"), g("NT"), g("NCH")
    n_sweeps = g("n_sweeps")
    x_in, y_out = g("x_in"), g("y_out")
    B_y = g("B_y")
    h_t, B_ht = g("h_t"), g("B_ht")
    uh, ul, B_uh, B_ul = g("uh"), g("ul"), g("B_uh"), g("B_ul")
    uf_hi, uf_lo, B_ufh, B_ufl = g("uf_hi"), g("uf_lo"), g("B_ufh"), g("B_ufl")
    sg, B_sg = g("sg"), g("B_sg")
    yg, B_yg = g("yg"), g("B_yg")
    stat, B_stat = g("stat"), g("B_stat")
    wring, B_wr = g("wring"), g("B_wr")
    cin, cout_, B_cin, B_cout = g("cin"), g("cout_"), g("B_cin"), g("B_cout")
    ps, PB = g("ps"), g("PB")
    xbc_r, B_xbr = g("xbc_r"), g("B_xbr")
    bvcol, B_bv = g("bvcol"), g("B_bv")
    cbcol, B_cb = g("cbcol"), g("B_cb")
    ccol, B_cc = g("ccol"), g("B_cc")
    drep, B_dr = g("drep"), g("B_dr")
    dts = g("dts")
    B_dt0, B_dt1, B_dt2 = g("B_dt0"), g("B_dt1"), g("B_dt2")
    stk, B_stk = g("stk"), g("B_stk")
    dr_a, dr_ea, dr_rs = g("dr_a"), g("dr_ea"), g("dr_rs")
    B_dra, B_drea, B_drrs = g("B_dra"), g("B_drea"), g("B_drrs")
    mb, mf = g("mb"), g("mf")
    convc = g("convc")
    B_cvc = g("B_cvc")
    state, B_st = g("state"), g("B_st")
    ident, B_id = g("ident"), g("B_id")
    mask01, B_mk = g("mask01"), g("B_mk")
    aeb, B_aeb = g("aeb"), g("B_aeb")
    wTt, B_wT = g("wTt"), g("B_wT")
    abg, B_abg = g("abg"), g("B_abg")
    dec, B_dec = g("dec"), g("B_dec")
    yg, B_yg = g("yg"), g("B_yg")
    zt, B_zt = g("zt"), g("B_zt")
    s_ch, B_sch = g("s_ch"), g("B_sch")
    misc, B_misc = g("misc"), g("B_misc")
    btile, B_bt = g("btile"), g("B_bt")
    xr, B_xr = g("xr"), g("B_xr")
    xq_t, B_xq = g("uh"), g("B_uh")
    hq_t, B_hq = g("hqs"), g("B_hqs")
    xqT, B_xqT = g("uf_hi"), g("B_ufh")
    wf, B_wf = g("wf"), g("B_wf")
    hqT, B_hqT = g("mb"), g("B_mb_hqT")
    cfc, cproj, B_cfc, B_cproj = g("cfc"), g("cproj"), g("B_cfc"), g("B_cproj")
    b_fc = g("b_fc")
    B_xbs, B_xtf, B_ea, B_bw = g("B_xbs"), g("B_xtf"), g("B_ea"), g("B_bw")
    B_xhh, B_xhl, B_xth, B_xtl = g("B_xhh"), g("B_xhl"), g("B_xth"), g("B_xtl")
    B_ygh, B_ygl = g("B_ygh"), g("B_ygl")
    B_bpk, B_cpa, B_cpb = g("B_bpk"), g("B_cpa"), g("B_cpb")
    B_sth, B_stl, B_stf = g("B_sth"), g("B_stl"), g("B_stf")
    B_st0, B_bcT, B_adt, B_wrk = g("B_st0"), g("B_bcT"), g("B_adt"), g("B_wrk")
    MB_XHH, MB_XHL = g("MB_XHH"), g("MB_XHL")
    MB_XTH, MB_XTL = g("MB_XTH"), g("MB_XTL")
    MB_YGH, MB_YGL = g("MB_YGH"), g("MB_YGL")
    MB_BPK, MB_CPA, MB_CPB = g("MB_BPK"), g("MB_CPA"), g("MB_CPB")
    MB_SCR = 12672
    MB_STH, MB_STL = 12800, 13312
    MF_XBS, MF_XTF, MF_BW = g("MF_XBS"), g("MF_XTF"), g("MF_BW")
    MF_STF, MF_ST0, MF_BCT, MF_ADT, MF_WRK = (
        g("MF_STF"), g("MF_ST0"), g("MF_BCT"), g("MF_ADT"), g("MF_WRK"))
    MF_EXP = 6080
    HSL, PSL = g("HSL"), g("PSL")
    row0 = sw * sweep

    for nt in range(NT):
        r0 = row0 + nt * 128
        p.emit("sp", lambda e, r0=r0: e.dma_start(
            out=sg[:], in_=x_in[r0:r0 + 128, :]), writes=[B_sg], dma=1)
        emit_ln_stats(E, sg[:], [B_sg], nt,
                      h_t[:, nt * D:(nt + 1) * D], [B_ht], yg[:, 0:D], B_yg)

    if sw == n_sweeps // 2:
        state, B_st = g("state"), g("B_st")
        p.emit("dve", lambda e: e.memset(state[:], 0.0),
               writes=[B_st[0], B_st[1]])

    for l in range(_DBG_NL):
        emit_mamba_layer(E, sw, b, l)

    p.barrier()
    if not _DBG_FFN:
        for nt in range(NT):
            r0 = row0 + nt * 128
            p.emit("dve", lambda e, nt=nt: e.tensor_copy(
                sg[:], h_t[:, nt * D:(nt + 1) * D]),
                reads=[B_ht], writes=[B_sg])
            p.emit("act", lambda e, r0=r0: e.dma_start(
                out=y_out[r0:r0 + 128, :], in_=sg[:]),
                reads=[B_sg], writes=[B_y], dma=1)
        p.barrier()
        return
    ident, B_id = g("ident"), g("B_id")
    for nt in range(NT):
        r0 = row0 + nt * 128
        p.emit("sp", lambda e, r0=r0: e.dma_start(
            out=xr[:], in_=x_in[r0:r0 + 128, :]), writes=[B_xr], dma=1)
        emit_ln_stats(E, xr[:], [B_xr], nt, mf[:, nt * D:(nt + 1) * D],
                      [B_wrk], yg[:, 0:D], B_yg)
    SCx = emit_qstats(E, [(mf[:, nt * D:(nt + 1) * D], [B_wrk])
                          for nt in range(NT)])
    for nt in range(NT):
        emit_vquant(E, mf[:, nt * D:(nt + 1) * D], [B_wrk],
                    mf[:, nt * D:(nt + 1) * D], B_wrk, nt, SCx)
    xqT3 = xqT[:, 0:KD * sweep].rearrange("p (k t) -> p k t", t=sweep)
    for nt in range(NT):
        for tg in range(KD // 4):
            def f_txq(e, nt=nt, tg=tg):
                r = None
                for i in range(4):
                    k = tg * 4 + i
                    r = e.transpose(
                        ps[4 + tg % 2][:, i * 128:(i + 1) * 128],
                        mf[:, nt * D + k * 128:nt * D + (k + 1) * 128],
                        ident[:])
                return r
            p.emit("pe", f_txq, reads=[B_wrk, B_id],
                   writes=[PB[4 + tg % 2]])
            p.emit("act", lambda e, nt=nt, tg=tg: e.copy(
                out=xqT3[:, tg * 4:(tg + 1) * 4, nt * 128:nt * 128 + 128],
                in_=ps[4 + tg % 2][:, 0:512]
                .rearrange("p (k t) -> p k t", t=128)),
                reads=[PB[4 + tg % 2]], writes=[B_xqT])
    harea = [(mf, B_wrk, 0), (mf, B_wrk, HID),
             (xbc_r, B_xbr, 0), (xbc_r, B_xbr, HID)]
    for sl in range(HSL):
        p.emit("sp", lambda e, sl=sl: e.dma_start(
            out=btile[:], in_=b_fc[sl * 512:(sl + 1) * 512]
            .unsqueeze(0).broadcast_to([128, 512])),
            writes=[B_bt], dma=1)
        for k in range(KD):
            s4 = (sl * KD + k) % 4
            p.emit("sp", lambda e, sl=sl, k=k, s4=s4: e.dma_start(
                out=wf[:, s4 * 512:(s4 + 1) * 512], in_=cfc[k, sl]),
                reads=[B_cfc], writes=[B_wf[s4]], dma=1)

            def f_mm(e, sl=sl, k=k, s4=s4):
                r = None
                for nt in range(NT):
                    r = e.matmul(
                        ps[nt][:, :],
                        xqT[:, k * sweep + nt * 128:k * sweep + (nt + 1) * 128],
                        wf[:, s4 * 512:(s4 + 1) * 512],
                        start=(k == 0), stop=(k == KD - 1))
                return r
            p.emit("pe", f_mm, reads=[B_xqT, B_wf[s4]],
                   writes=[PB[nt] for nt in range(NT)])
        for nt in range(NT):
            ar, abuf, aoff = harea[nt]

            def f_bias(e, nt=nt, ar=ar, aoff=aoff, sl=sl):
                return e.tensor_tensor(
                    out=ar[:, aoff + sl * 512:aoff + (sl + 1) * 512],
                    in0=ps[nt][:, :], in1=btile[:], op=Alu.add)
            p.emit("dve", f_bias, reads=[PB[nt], B_bt], writes=[abuf])

            def f_gelu(e, nt=nt, ar=ar, aoff=aoff, sl=sl):
                return e.activation(
                    out=ar[:, aoff + sl * 512:aoff + (sl + 1) * 512],
                    in_=ar[:, aoff + sl * 512:aoff + (sl + 1) * 512],
                    func=Act.Gelu)
            p.emit("act", f_gelu, reads=[abuf], writes=[abuf])
    SCh = emit_qstats(E, [
        (harea[u // 4][0][:, harea[u // 4][2] + (u % 4) * 1024:
                          harea[u // 4][2] + (u % 4 + 1) * 1024],
         [harea[u // 4][1]]) for u in range(NT * 4)])
    hqT3 = hqT[:, 0:(HID // 128) * sweep].rearrange(
        "p (k t) -> p k t", t=sweep)
    for nt in range(NT):
        ar, abuf, aoff = harea[nt]
        for q in range(4):
            emit_vquant(E, ar[:, aoff + q * 1024:aoff + (q + 1) * 1024],
                        [abuf], ar[:, aoff + q * 1024:aoff + (q + 1) * 1024],
                        abuf, nt * 4 + q, SCh)
        for tg in range(8):
            def f_thq(e, nt=nt, tg=tg, ar=ar, aoff=aoff):
                r = None
                for i in range(4):
                    kt = tg * 4 + i
                    r = e.transpose(
                        ps[4 + tg % 2][:, i * 128:(i + 1) * 128],
                        ar[:, aoff + kt * 128:aoff + (kt + 1) * 128],
                        ident[:])
                return r
            p.emit("pe", f_thq, reads=[abuf, B_id],
                   writes=[PB[4 + tg % 2]])
            p.emit("act", lambda e, nt=nt, tg=tg: e.copy(
                out=hqT3[:, tg * 4:(tg + 1) * 4, nt * 128:nt * 128 + 128],
                in_=ps[4 + tg % 2][:, 0:512]
                .rearrange("p (k t) -> p k t", t=128)),
                reads=[PB[4 + tg % 2]], writes=[B_hqT])
    for psl in range(PSL):
        for kt in range(HID // 128):
            s4 = (psl * (HID // 128) + kt) % 4
            p.emit("sp", lambda e, psl=psl, kt=kt, s4=s4: e.dma_start(
                out=wf[:, s4 * 512:(s4 + 1) * 512], in_=cproj[kt, psl]),
                reads=[B_cproj], writes=[B_wf[s4]], dma=1)

            def f_pm(e, psl=psl, kt=kt, s4=s4):
                r = None
                for nt in range(NT):
                    bank = (4 + nt) if psl == 0 else nt
                    r = e.matmul(
                        ps[bank][:, :],
                        hqT[:, kt * sweep + nt * 128:kt * sweep + (nt + 1) * 128],
                        wf[:, s4 * 512:(s4 + 1) * 512],
                        start=(kt == 0), stop=(kt == HID // 128 - 1))
                return r
            p.emit("pe", f_pm, reads=[B_hqT, B_wf[s4]],
                   writes=[PB[(4 + nt) if psl == 0 else nt] for nt in range(NT)])
    for nt in range(NT):
        r0 = row0 + nt * 128
        p.emit("sp", lambda e, r0=r0: e.dma_start(
            out=xr[:], in_=x_in[r0:r0 + 128, :]), writes=[B_xr], dma=1)
        for psl in range(PSL):
            bank = (4 + nt) if psl == 0 else nt

            def f_gb(e, nt=nt, psl=psl, bank=bank):
                return e.tensor_tensor(
                    out=sg[:, psl * 512:(psl + 1) * 512], in0=ps[bank][:, :],
                    in1=misc[:, psl * 512:(psl + 1) * 512],
                    op=Alu.add)
            p.emit("dve", f_gb, reads=[PB[bank], B_misc], writes=[B_sg])
        p.emit("act", lambda e: e.activation(out=sg[:], in_=sg[:],
                                             func=Act.Silu),
               reads=[B_sg], writes=[B_sg])
        p.emit("dve", lambda e, nt=nt: e.tensor_tensor(
            out=sg[:], in0=sg[:], in1=h_t[:, nt * D:(nt + 1) * D],
            op=Alu.mult), reads=[B_ht, B_sg], writes=[B_sg])
        p.emit("dve", lambda e: e.tensor_tensor(
            out=sg[:], in0=sg[:], in1=xr[:], op=Alu.add),
            reads=[B_xr, B_sg], writes=[B_sg])
        p.emit("act", lambda e, r0=r0: e.dma_start(
            out=y_out[r0:r0 + 128, :], in_=sg[:]),
            reads=[B_sg], writes=[B_y], dma=1)
    p.barrier()


def emit_mamba_layer(E, sw, b, l):
    p = E["p"]
    g = lambda k: E[k]
    sweep, NT, NCH = g("sweep"), g("NT"), g("NCH")
    h_t, B_ht = g("h_t"), g("B_ht")
    uh, ul, B_uh, B_ul = g("uh"), g("ul"), g("B_uh"), g("B_ul")
    uf_hi, uf_lo, B_ufh, B_ufl = g("uf_hi"), g("uf_lo"), g("B_ufh"), g("B_ufl")
    sg, B_sg = g("sg"), g("B_sg")
    stat, B_stat = g("stat"), g("B_stat")
    wring, B_wr = g("wring"), g("B_wr")
    cin, B_cin = g("cin"), g("B_cin")
    ps, PB = g("ps"), g("PB")
    xbc_r, B_xbr = g("xbc_r"), g("B_xbr")
    bvcol, B_bv = g("bvcol"), g("B_bv")
    ccol, B_cc = g("ccol"), g("B_cc")
    dts = g("dts")
    B_dt0, B_dt1, B_dt2 = g("B_dt0"), g("B_dt1"), g("B_dt2")
    stk, B_stk = g("stk"), g("B_stk")
    dr_a, dr_ea, dr_rs = g("dr_a"), g("dr_ea"), g("dr_rs")
    B_dra, B_drea, B_drrs = g("B_dra"), g("B_drea"), g("B_drrs")
    convc, B_cvc = g("convc"), g("B_cvc")
    misc, B_misc = g("misc"), g("B_misc")
    cw, B_cw = g("cw"), g("B_cw")
    S = sweep
    SP4 = S + 4

    ps, PB = g("ps"), g("PB")
    ident, B_id = g("ident"), g("B_id")
    mf = g("mf")
    MF_XBS = g("MF_XBS")
    B_uff = g("B_xbs")[0]
    for nt in range(NT):
        src_ap, src_bufs = h_t[:, nt * D:(nt + 1) * D], [g("B_ht")]
        emit_ln_stats(E, src_ap, src_bufs, nt, sg[:], [B_sg],
                      g("yg")[:, 0:D], g("B_yg"))
        ufh4 = uf_hi[:, 0:KD * S].rearrange("p (k t) -> p k t", t=S)
        ufl4 = uf_lo[:, 0:KD * S].rearrange("p (k t) -> p k t", t=S)
        for tq in range(2):
            def f_tu(e, tq=tq):
                r = None
                for i in range(4):
                    k = tq * 4 + i
                    r = e.transpose(ps[6 + tq][:, i * 128:(i + 1) * 128],
                                    sg[:, k * 128:(k + 1) * 128], ident[:])
                return r
            p.emit("pe", f_tu, reads=[B_sg, B_id], writes=[PB[6 + tq]])
            ufh3 = ufh4[:, tq * 4:(tq + 1) * 4, nt * 128:nt * 128 + 128]
            ufl3 = ufl4[:, tq * 4:(tq + 1) * 4, nt * 128:nt * 128 + 128]
            ps3 = ps[6 + tq][:, 0:512].rearrange("p (k t) -> p k t", t=128)
            p.emit("act", lambda e, ufh3=ufh3, ps3=ps3: e.copy(
                out=ufh3, in_=ps3), reads=[PB[6 + tq]], writes=[B_ufh])
            p.emit("dve", lambda e, ufh3=ufh3, ufl3=ufl3, ps3=ps3:
                   e.scalar_tensor_tensor(out=ufl3, in0=ps3, scalar=1.0,
                                          in1=ufh3, op0=Alu.mult,
                                          op1=Alu.subtract),
                   reads=[PB[6 + tq], B_ufh], writes=[B_ufl])

    cslot = (l * 2 + b) * NXT * 4
    p.emit("dve", lambda e, cslot=cslot: e.tensor_copy(
        xbc_r[:].rearrange("p (t c) -> p t c", c=SP4)[:, :, 1:4],
        convc[:, cslot:cslot + NXT * 4]
        .rearrange("p (t c) -> p t c", c=4)[:, :, 1:4]),
        reads=[B_cvc[l][b]], writes=[B_xbr])

    for i, dc in enumerate(range(16, NDC)):
        slot = i % 2
        p.emit("sp", lambda e, dc=dc, slot=slot: e.dma_start(
            out=wring[:, slot * 2048:slot * 2048 + 2048]
            .rearrange("p (c n) -> p c n", n=128),
            in_=cin[l, dc].rearrange("k h p n -> p (k h) n")),
            reads=[B_cin], writes=[B_wr[slot]], dma=1)

        def f_mm(e, dc=dc, slot=slot):
            r = None
            for k in range(KD):
                wh = wring[:, slot * 2048 + (k * 2) * 128:slot * 2048 + (k * 2 + 1) * 128]
                wl = wring[:, slot * 2048 + (k * 2 + 1) * 128:slot * 2048 + (k * 2 + 2) * 128]
                xh_ = uf_hi[:, k * S:(k + 1) * S]
                xl_ = uf_lo[:, k * S:(k + 1) * S]
                r = e.matmul(ps[slot][:, 0:S], wh, xh_, start=(k == 0), stop=False)
                r = e.matmul(ps[slot][:, 0:S], wh, xl_, start=False, stop=False)
                r = e.matmul(ps[slot][:, 0:S], wl, xh_, start=False,
                             stop=(k == KD - 1))
            return r
        p.emit("pe", f_mm, reads=[B_wr[slot], B_ufh, B_ufl],
               writes=[PB[slot]])
        if dc < 33:
            xt = dc - 16

            def f_ev(e, dc=dc, slot=slot, xt=xt):
                return e.activation(
                    out=xbc_r[:, xt * SP4 + 4:xt * SP4 + 4 + S],
                    in_=ps[slot][:, 0:S], func=Act.Identity,
                    bias=bvcol[:, l * NDC + dc:l * NDC + dc + 1])
            p.emit("act", f_ev, reads=[PB[slot], B_bv], writes=[B_xbr])
        else:
            def f_ev(e, slot=slot):
                return e.activation(
                    out=dts[:, 0:S], in_=ps[slot][0:NH, 0:S], func=Act.Identity,
                    bias=ccol[0:NH, 8 + l:9 + l])
            p.emit("act", f_ev, reads=[PB[slot], B_cc], writes=[B_dt0])

    q0, q1, q2 = dts[:, 0:S], dts[:, S:2 * S], dts[:, 2 * S:3 * S]
    p.emit("act", lambda e: e.activation(out=q2, in_=q0, func=Act.Abs),
           reads=[B_dt0], writes=[B_dt2])
    p.emit("act", lambda e: e.activation(out=q2, in_=q2, func=Act.Exp,
                                         scale=-1.0),
           reads=[B_dt2], writes=[B_dt2])
    p.emit("act", lambda e: e.activation(out=q2, in_=q2, func=Act.Ln,
                                         bias=g("cst")[0:NH, 0:1]),
           reads=[B_dt2, g("B_cst")], writes=[B_dt2])
    p.emit("dve", lambda e: e.scalar_tensor_tensor(
        out=q1, in0=q0, scalar=0.0, in1=q2, op0=Alu.max, op1=Alu.add),
        reads=[B_dt0, B_dt2], writes=[B_dt1])
    p.emit("act", lambda e: e.activation(out=q2, in_=q1, func=Act.Ln),
           reads=[B_dt1], writes=[B_dt2])
    p.emit("act", lambda e: [
        e.dma_start(out=stk[32:64, :], in_=q1),
        e.dma_start(out=stk[64:96, :], in_=q2)],
        reads=[B_dt1, B_dt2], writes=[B_stk], dma=2)
    p.emit("dve", lambda e, l=l: e.tensor_scalar(
        out=q0, in0=q1, scalar1=ccol[0:NH, l:l + 1], scalar2=None,
        op0=Alu.mult), reads=[B_dt1, g("B_cc")], writes=[B_dt0])
    for ck in range(NCH):
        p.emit("dve", lambda e, ck=ck: e.tensor_tensor_scan(
            out=stk[0:32, ck * CH:(ck + 1) * CH],
            data0=q0[:, ck * CH:(ck + 1) * CH],
            data1=q0[:, ck * CH:(ck + 1) * CH],
            initial=0.0, op0=Alu.add, op1=Alu.bypass),
            reads=[B_dt0], writes=[B_stk])
    p.emit("act", lambda e: e.activation(out=q0, in_=stk[0:32, :],
                                         func=Act.Exp),
           reads=[B_stk], writes=[B_dt0])
    p.emit("act", lambda e: e.dma_start(out=dr_a[:, :], in_=stk[0:32, :]),
           reads=[B_stk], writes=[B_dra], dma=1)
    p.emit("act", lambda e: e.dma_start(out=dr_ea[:, :], in_=q0),
           reads=[B_dt0], writes=[B_drea], dma=1)

    p.emit("sp", lambda e, l=l: e.dma_start(
        out=g("dec")[:, 0:DIN],
        in_=g("cbv")[l, 0:DIN].unsqueeze(0).broadcast_to([128, DIN])),
        reads=[g("B_cbv")], writes=[g("B_dec")], dma=1)

    emit_chunk_conv(E, sw, b, l, 0)
    for ck in range(NCH):
        hook = None
        if ck + 1 < NCH:
            hook = (lambda ck=ck: emit_chunk_conv(E, sw, b, l, ck + 1))
        emit_chunk(E, sw, b, l, ck, hook)

    p.emit("dve", lambda e, cslot=cslot: e.tensor_copy(
        convc[:, cslot:cslot + NXT * 4]
        .rearrange("p (t c) -> p t c", c=4)[:, :, 1:4],
        xbc_r[:].rearrange("p (t c) -> p t c", c=SP4)[:, :, S + 1:S + 4]),
        reads=[B_xbr], writes=[B_cvc[l][b]])


def emit_chunk_conv(E, sw, b, l, ck):
    p = E["p"]
    g = lambda k: E[k]
    sweep = g("sweep")
    S, SP4 = sweep, sweep + 4
    xbc_r, B_xbr = g("xbc_r"), g("B_xbr")
    mf = g("mf")
    cbcol, B_cb = g("cbcol"), g("B_cb")
    cw, B_cw = g("cw"), g("B_cw")
    B_xbs, B_xtf = g("B_xbs"), g("B_xtf")
    MF_XBS, MF_XTF = g("MF_XBS"), g("MF_XTF")

    xbs = lambda t: mf[:, MF_XBS + t * 128:MF_XBS + (t + 1) * 128]
    acc3 = mf[:, MF_XBS:MF_XBS + DIN].rearrange("p (t n) -> p t n", n=128)
    tmp3 = mf[:, MF_XTF:MF_XTF + DIN].rearrange("p (t n) -> p t n", n=128)

    def win3(kk, ck=ck):
        return xbc_r[:, 0:16 * SP4].rearrange("p (t n) -> p t n", n=SP4)[
            :, :, 1 + kk + ck * CH:1 + kk + ck * CH + CH]

    def cw3(kk):
        return cw[:, l * NXT * 4 + kk:l * NXT * 4 + kk + 16 * 4:1] \
            .rearrange("p (t o) -> p t o", o=4)[:, :, 0:1] \
            .broadcast_to([128, 16, CH])

    def cb3():
        return cbcol[:, l * NXT:l * NXT + 16].unsqueeze(2) \
            .broadcast_to([128, 16, CH])

    def f_cb0(e):
        return e.tensor_tensor(out=acc3, in0=win3(0), in1=cw3(0), op=Alu.mult)
    p.emit("dve", f_cb0, reads=[B_xbr, B_cw], writes=[B_xbs[0]])
    for kk in range(1, 4):
        def f_cm(e, kk=kk):
            return e.tensor_tensor(out=tmp3, in0=win3(kk), in1=cw3(kk),
                                   op=Alu.mult)
        p.emit("dve", f_cm, reads=[B_xbr, B_cw], writes=[B_xtf[0]])

        def f_ca(e):
            return e.tensor_tensor(out=acc3, in0=acc3, in1=tmp3, op=Alu.add)
        p.emit("dve", f_ca, reads=[B_xtf[0], B_xbs[0]], writes=[B_xbs[0]])

    def f_cba(e):
        return e.tensor_tensor(out=acc3, in0=acc3, in1=cb3(), op=Alu.add)
    p.emit("dve", f_cba, reads=[B_cb, B_xbs[0]], writes=[B_xbs[0]])
    p.emit("act", lambda e: e.activation(
        out=mf[:, MF_XBS:MF_XBS + DIN], in_=mf[:, MF_XBS:MF_XBS + DIN],
        func=Act.Silu), reads=[B_xbs[0]], writes=[B_xbs[0]])
    xt = 16
    def f_c0(e, xt=xt):
        return e.tensor_scalar(
            out=xbs(xt), in0=xbc_r[:, xt * SP4 + 1 + ck * CH:
                                   xt * SP4 + 1 + ck * CH + CH],
            scalar1=cw[:, (l * NXT + xt) * 4:(l * NXT + xt) * 4 + 1],
            scalar2=None, op0=Alu.mult)
    p.emit("dve", f_c0, reads=[B_xbr, B_cw], writes=[B_xbs[0]])
    for kk in range(1, 4):
        def f_cr(e, xt=xt, kk=kk):
            return e.scalar_tensor_tensor(
                out=xbs(xt),
                in0=xbc_r[:, xt * SP4 + 1 + kk + ck * CH:
                          xt * SP4 + 1 + kk + ck * CH + CH],
                scalar=cw[:, (l * NXT + xt) * 4 + kk:
                          (l * NXT + xt) * 4 + kk + 1],
                in1=xbs(xt), op0=Alu.mult, op1=Alu.add)
        p.emit("dve", f_cr, reads=[B_xbr, B_cw, B_xbs[0]], writes=[B_xbs[0]])
    def f_si(e, xt=xt):
        return e.activation(out=xbs(xt), in_=xbs(xt), func=Act.Silu,
                            bias=cbcol[:, l * NXT + xt:l * NXT + xt + 1])
    p.emit("act", f_si, reads=[B_xbs[0], B_cb], writes=[B_xbs[0]])


def emit_chunk(E, sw, b, l, ck, hook=None):
    p = E["p"]
    g = lambda k: E[k]
    sweep, NCH = g("sweep"), g("NCH")
    S, SP4 = sweep, sweep + 4
    xbc_r, B_xbr = g("xbc_r"), g("B_xbr")
    mb, mf = g("mb"), g("mf")
    stk, B_stk = g("stk"), g("B_stk")
    dr_a, B_dra = g("dr_a"), g("B_dra")
    ps, PB = g("ps"), g("PB")
    ident, B_id = g("ident"), g("B_id")
    mask01, B_mk = g("mask01"), g("B_mk")
    aeb, B_aeb = g("aeb"), g("B_aeb")
    wTt, B_wT = g("wTt"), g("B_wT")
    abg, B_abg = g("abg"), g("B_abg")
    s_ch, B_sch = g("s_ch"), g("B_sch")
    cbcol, B_cb = g("cbcol"), g("B_cb")
    cw, B_cw = g("cw"), g("B_cw")
    B_xbs, B_xtf, B_bw = g("B_xbs"), g("B_xtf"), g("B_bw")
    B_xhh, B_xhl, B_xth, B_xtl = g("B_xhh"), g("B_xhl"), g("B_xth"), g("B_xtl")
    B_bpk, B_cpa, B_cpb = g("B_bpk"), g("B_cpa"), g("B_cpb")
    B_st0, B_bcT, B_adt, B_wrk = g("B_st0"), g("B_bcT"), g("B_adt"), g("B_wrk")
    MB_XHH, MB_XHL = g("MB_XHH"), g("MB_XHL")
    MB_XTH, MB_XTL = g("MB_XTH"), g("MB_XTL")
    MB_BPK, MB_CPA, MB_CPB = g("MB_BPK"), g("MB_CPA"), g("MB_CPB")
    MB_SCR = 12672
    MF_XBS, MF_XTF, MF_BW = g("MF_XBS"), g("MF_XTF"), g("MF_BW")
    MF_STF, MF_ST0, MF_BCT, MF_ADT, MF_WRK = (
        g("MF_STF"), g("MF_ST0"), g("MF_BCT"), g("MF_ADT"), g("MF_WRK"))
    MF_EXP = 6080

    xbs = lambda t: mf[:, MF_XBS + t * 128:MF_XBS + (t + 1) * 128]
    t16 = lambda rs, re_: mf[rs:re_, MF_XBS + 16 * 128:MF_XBS + 17 * 128]
    p.emit("act", lambda e: e.copy(out=mb[0:64, MB_BPK:MB_BPK + 128],
                                   in_=t16(0, 64)),
           reads=[B_xbs[0]], writes=[B_bpk])
    p.emit("dve", lambda e: e.scalar_tensor_tensor(
        out=mb[0:64, MB_SCR:MB_SCR + 128], in0=t16(0, 64), scalar=1.0,
        in1=mb[0:64, MB_BPK:MB_BPK + 128], op0=Alu.mult, op1=Alu.subtract),
        reads=[B_xbs[0], B_bpk], writes=[B_wrk])
    p.emit("act", lambda e: e.dma_start(
        out=mb[64:128, MB_BPK:MB_BPK + 128],
        in_=mb[0:64, MB_SCR:MB_SCR + 128]),
        reads=[B_wrk], writes=[B_bpk], dma=1)
    p.emit("act", lambda e: e.copy(out=mb[64:128, MB_CPB:MB_CPB + 128],
                                   in_=t16(64, 128)),
           reads=[B_xbs[0]], writes=[B_cpb])
    p.emit("dve", lambda e: e.scalar_tensor_tensor(
        out=mb[64:128, MB_CPA:MB_CPA + 128], in0=t16(64, 128), scalar=1.0,
        in1=mb[64:128, MB_CPB:MB_CPB + 128], op0=Alu.mult, op1=Alu.subtract),
        reads=[B_xbs[0], B_cpb], writes=[B_cpa])
    p.emit("act", lambda e: e.dma_start(
        out=mb[0:64, MB_CPA:MB_CPA + 128],
        in_=mb[64:128, MB_CPB:MB_CPB + 128]),
        reads=[B_cpb], writes=[B_cpa], dma=1)
    p.emit("act", lambda e: e.dma_start(
        out=mb[0:64, MB_CPB:MB_CPB + 128],
        in_=mb[64:128, MB_CPA:MB_CPA + 128]),
        reads=[B_cpa], writes=[B_cpb], dma=1)

    for tq in range(4):
        def f_tx(e, tq=tq):
            r = None
            for i in range(4):
                t_ = tq * 4 + i
                r = e.transpose(
                    ps[6 + tq % 2][:, i * 128:(i + 1) * 128],
                    mf[:, MF_XBS + t_ * 128:MF_XBS + (t_ + 1) * 128],
                    ident[:])
            return r
        p.emit("pe", f_tx, reads=[B_xbs[0], B_id], writes=[PB[6 + tq % 2]])
        p.emit("dve", lambda e, tq=tq: e.tensor_copy(
            mf[:, MF_XTF + tq * 512:MF_XTF + (tq + 1) * 512],
            ps[6 + tq % 2][:, 0:512]),
            reads=[PB[6 + tq % 2]], writes=[B_xtf[0]])
    p.emit("act", lambda e: e.copy(out=mb[:, MB_XTH:MB_XTH + DIN],
                                   in_=mf[:, MF_XTF:MF_XTF + DIN]),
           reads=[B_xtf[0]], writes=[B_xth[0]])
    p.emit("dve", lambda e: e.scalar_tensor_tensor(
        out=mb[:, MB_XTL:MB_XTL + DIN], in0=mf[:, MF_XTF:MF_XTF + DIN],
        scalar=1.0, in1=mb[:, MB_XTH:MB_XTH + DIN],
        op0=Alu.mult, op1=Alu.subtract),
        reads=[B_xtf[0], B_xth[0]], writes=[B_xtl[0]])

    p.emit("pe", lambda e: e.transpose(
        ps[6][:, 128:192], t16(0, 64), ident[0:64, 0:64]),
        reads=[B_xbs[0], B_id], writes=[PB[6]])
    p.emit("dve", lambda e: e.tensor_copy(
        mf[:, MF_BCT:MF_BCT + 64], ps[6][:, 128:192]),
        reads=[PB[6]], writes=[B_bcT])
    p.emit("pe", lambda e, ck=ck: e.transpose(
        ps[6][:, 192:288], stk[:, ck * CH:(ck + 1) * CH],
        ident[0:96, 0:96]),
        reads=[B_stk, B_id], writes=[PB[6]])
    p.emit("dve", lambda e: e.tensor_copy(
        mf[:, MF_ADT:MF_ADT + 96], ps[6][:, 192:288]),
        reads=[PB[6]], writes=[B_adt[0]])

    p.emit("sp", lambda e, ck=ck: e.dma_start(
        out=aeb[:, 0:32],
        in_=dr_a[:, ck * CH + CH - 1].unsqueeze(0).broadcast_to([128, 32]),
        allow_slow_non_contiguous=True),
        reads=[B_dra], writes=[B_aeb], dma=1)
    p.emit("dve", lambda e: e.tensor_tensor(
        out=aeb[:, 32:64], in0=aeb[:, 0:32],
        in1=mf[:, MF_ADT:MF_ADT + 32], op=Alu.subtract),
        reads=[B_aeb, B_adt[0]], writes=[B_aeb])
    p.emit("dve", lambda e: e.tensor_tensor(
        out=aeb[:, 32:64], in0=aeb[:, 32:64],
        in1=mf[:, MF_ADT + 64:MF_ADT + 96], op=Alu.add),
        reads=[B_aeb, B_adt[0]], writes=[B_aeb])
    p.emit("act", lambda e: e.activation(out=wTt[:], in_=aeb[:, 32:64],
                                         func=Act.Exp),
           reads=[B_aeb], writes=[B_wT])
    p.emit("act", lambda e: e.activation(
        out=mf[:, MF_ADT + 96:MF_ADT + 128], in_=mf[:, MF_ADT:MF_ADT + 32],
        func=Act.Exp), reads=[B_adt[0]], writes=[B_adt[0]])
    p.emit("act", lambda e: e.activation(
        out=aeb[:, 0:32], in_=aeb[:, 0:32], func=Act.Exp),
        reads=[B_aeb], writes=[B_aeb])

    def f_st0(e):
        r = e.matmul(ps[7][:, 0:128], mb[:, MB_BPK:MB_BPK + 128],
                     mb[:, MB_CPA:MB_CPA + 128], start=True, stop=False,
                     skip_group_check=True)
        r = e.matmul(ps[7][:, 0:128], mb[:, MB_BPK:MB_BPK + 128],
                     mb[:, MB_CPB:MB_CPB + 128], start=False, stop=True,
                     skip_group_check=True)
        return r
    p.emit("pe", f_st0, reads=[B_bpk, B_cpa, B_cpb], writes=[PB[7]])
    p.emit("dve", lambda e: e.tensor_tensor(
        out=mf[:, MF_ST0:MF_ST0 + 128], in0=ps[7][:, 0:128], in1=mask01[:],
        op=Alu.mult), reads=[PB[7], B_mk], writes=[B_st0])

    for g4 in range(8):
        g8 = g4 // 2
        p.emit("sp", lambda e, g4=g4, ck=ck: e.dma_start(
            out=abg[:].rearrange("p (i n) -> p i n", n=128),
            in_=dr_a[4 * g4:4 * g4 + 4, ck * CH:(ck + 1) * CH]
            .unsqueeze(0).broadcast_to([128, 4, CH])),
            reads=[B_dra], writes=[B_abg[0]], dma=1)

        def f_ts4(e, g4=g4):
            r = None
            for i4 in range(4):
                h = 4 * g4 + i4
                r = e.tensor_scalar(
                    out=mf[:, MF_WRK + i4 * 128:MF_WRK + (i4 + 1) * 128],
                    in0=abg[:, i4 * 128:(i4 + 1) * 128],
                    scalar1=mf[:, MF_ADT + h:MF_ADT + h + 1], scalar2=0.0,
                    op0=Alu.subtract, op1=Alu.min)
            return r
        p.emit("dve", f_ts4, reads=[B_abg[0], B_adt[0]], writes=[B_wrk])
        p.emit("act", lambda e: e.activation(
            out=mf[:, MF_EXP:MF_EXP + 512], in_=mf[:, MF_WRK:MF_WRK + 512],
            func=Act.Exp), reads=[B_wrk], writes=[g("B_exp")])
        if g4 % 2 == 0:
            p.emit("dve", lambda e, g8=g8: e.tensor_tensor(
                out=mf[:, MF_BW:MF_BW + 512].rearrange(
                    "p (i n) -> p i n", n=64),
                in0=mf[:, MF_XTF + g8 * 512:MF_XTF + (g8 + 1) * 512]
                .rearrange("p (i n) -> p i n", n=64),
                in1=wTt[:, g8 * 8:(g8 + 1) * 8].unsqueeze(2)
                .broadcast_to([128, 8, 64]),
                op=Alu.mult),
                reads=[B_xtf[0], B_wT], writes=[B_bw])
            p.emit("pe", lambda e: e.matmul(
                ps[5][64:128, 0:512], mf[:, MF_BCT:MF_BCT + 64],
                mf[:, MF_BW:MF_BW + 512], start=True, stop=True,
                skip_group_check=True),
                reads=[B_bcT, B_bw], writes=[PB[5]])
        emit_head_group(E, sw, b, l, ck, g4)
        if g4 % 2 == 1:
            emit_group_tail(E, sw, b, l, ck, g8)

    emit_outproj_chunk(E, sw, b, l, ck, hook)


def emit_head_group(E, sw, b, l, ck, g4):
    p = E["p"]
    g = lambda k: E[k]
    mb, mf = g("mb"), g("mf")
    ps, PB = g("ps"), g("PB")
    state, B_st = g("state"), g("B_st")
    s_ch, B_sch = g("s_ch"), g("B_sch")
    B_sth, B_stl, B_stf = g("B_sth"), g("B_stl"), g("B_stf")
    B_st0, B_adt, B_wrk = g("B_st0"), g("B_adt"), g("B_wrk")
    B_exp, B_bw = g("B_exp"), g("B_bw")
    B_xth, B_xtl, B_xtf, B_xbs = g("B_xth"), g("B_xtl"), g("B_xtf"), g("B_xbs")
    MB_XTH, MB_XTL = g("MB_XTH"), g("MB_XTL")
    MB_STH, MB_STL = 12800, 13312
    MF_STF, MF_ST0, MF_ADT, MF_BW = (
        g("MF_STF"), g("MF_ST0"), g("MF_ADT"), g("MF_BW"))
    MF_XTF, MF_XBS = g("MF_XTF"), g("MF_XBS")
    MF_EXP = 6080
    hs = [4 * g4 + i for i in range(4)]

    def f_st4(e):
        r = None
        for i4, h in enumerate(hs):
            r = e.scalar_tensor_tensor(
                out=mf[:, MF_STF + i4 * 128:MF_STF + (i4 + 1) * 128],
                in0=mf[:, MF_EXP + i4 * 128:MF_EXP + (i4 + 1) * 128],
                scalar=mf[:, MF_ADT + 32 + h:MF_ADT + 33 + h],
                in1=mf[:, MF_ST0:MF_ST0 + 128], op0=Alu.mult, op1=Alu.mult)
        return r
    p.emit("dve", f_st4, reads=[B_exp, B_adt[0], B_st0],
           writes=[B_stf[0]])

    def f_hi4(e):
        return e.copy(out=mb[:, MB_STH:MB_STH + 512],
                      in_=mf[:, MF_STF:MF_STF + 512])
    p.emit("act", f_hi4, reads=[B_stf[0]], writes=[B_sth[0]])

    def f_lo4(e):
        return e.scalar_tensor_tensor(
            out=mb[:, MB_STL:MB_STL + 512], in0=mf[:, MF_STF:MF_STF + 512],
            scalar=1.0, in1=mb[:, MB_STH:MB_STH + 512],
            op0=Alu.mult, op1=Alu.subtract)
    p.emit("dve", f_lo4, reads=[B_stf[0], B_sth[0]], writes=[B_stl[0]])

    def f_y(e, g4=g4):
        r = None
        for i4, h in enumerate(hs):
            cs = (h % 8) * 64
            sth = mb[:, MB_STH + i4 * 128:MB_STH + (i4 + 1) * 128]
            stl = mb[:, MB_STL + i4 * 128:MB_STL + (i4 + 1) * 128]
            lh = mb[:, MB_XTH + h * 64:MB_XTH + (h + 1) * 64]
            ll = mb[:, MB_XTL + h * 64:MB_XTL + (h + 1) * 64]
            r = e.matmul(ps[2][:, cs:cs + 64], sth, lh,
                         start=True, stop=False, skip_group_check=True)
            r = e.matmul(ps[2][:, cs:cs + 64], stl, lh,
                         start=False, stop=False, skip_group_check=True)
            r = e.matmul(ps[2][:, cs:cs + 64], sth, ll,
                         start=False, stop=True, skip_group_check=True)
        return r
    p.emit("pe", f_y, reads=[B_xth[0], B_xtl[0], B_sth[0], B_stl[0]],
           writes=[PB[2]])

    t16c = mf[64:128, MF_XBS + 16 * 128:MF_XBS + 17 * 128]

    def f_y2(e, l=l, g4=g4):
        cs = (g4 % 2) * 256
        return e.matmul(
            ps[3][:, cs:cs + 256], t16c,
            state[64:128, l * DIN + g4 * 256:l * DIN + (g4 + 1) * 256],
            start=True, stop=True, skip_group_check=True)
    p.emit("pe", f_y2, reads=[B_st[l], B_xbs[0]], writes=[PB[3]])


def emit_group_tail(E, sw, b, l, ck, g8):
    p = E["p"]
    g = lambda k: E[k]
    mb, mf = g("mb"), g("mf")
    ps, PB = g("ps"), g("PB")
    state, B_st = g("state"), g("B_st")
    aeb, B_aeb = g("aeb"), g("B_aeb")
    s_ch, B_sch = g("s_ch"), g("B_sch")
    yg, B_yg = g("yg"), g("B_yg")
    zt, B_zt = g("zt"), g("B_zt")
    wring, B_wr = g("wring"), g("B_wr")
    cin, B_cin = g("cin"), g("B_cin")
    uf_hi, uf_lo, B_ufh, B_ufl = g("uf_hi"), g("uf_lo"), g("B_ufh"), g("B_ufl")
    bvcol, B_bv = g("bvcol"), g("B_bv")
    drep2, B_dr2 = g("drep2"), g("B_dr2")
    B_xtf, B_adt = g("B_xtf"), g("B_adt")
    MF_XTF, MF_ADT, MF_BW = g("MF_XTF"), g("MF_ADT"), g("MF_BW")
    sweep = g("sweep")
    S = sweep

    p.emit("dve", lambda e, l=l, g8=g8: e.tensor_tensor(
        out=s_ch[64:128, :].rearrange("p (i n) -> p i n", n=64),
        in0=state[64:128, l * DIN + g8 * 512:l * DIN + (g8 + 1) * 512]
        .rearrange("p (i n) -> p i n", n=64),
        in1=aeb[64:128, g8 * 8:(g8 + 1) * 8].unsqueeze(2)
        .broadcast_to([64, 8, 64]),
        op=Alu.mult),
        reads=[B_st[l], B_aeb], writes=[B_sch])
    p.emit("dve", lambda e, l=l, g8=g8: e.tensor_tensor(
        out=state[64:128, l * DIN + g8 * 512:l * DIN + (g8 + 1) * 512],
        in0=s_ch[64:128, :], in1=ps[5][64:128, :], op=Alu.add),
        reads=[B_sch, PB[5]], writes=[B_st[l]])

    zb = g8 % 2
    for k2 in range(KD // 2):
        slot = E.setdefault("wslot", 0)
        E["wslot"] = 1 - slot
        p.emit("sp", lambda e, l=l, g8=g8, k2=k2, slot=slot: e.dma_start(
            out=wring[:, slot * 2048:slot * 2048 + 2048],
            in_=g("cin_z")[l, g8, k2]),
            reads=[B_cin], writes=[B_wr[slot]], dma=1)

        def f_zmm(e, k2=k2, slot=slot, zb=zb, ck=ck):
            r = None
            for kl in range(2):
                k = 2 * k2 + kl
                base = slot * 2048 + kl * 1024
                wh = wring[:, base:base + 512]
                wl = wring[:, base + 512:base + 1024]
                xh_ = uf_hi[:, k * S + ck * CH:k * S + (ck + 1) * CH]
                xl_ = uf_lo[:, k * S + ck * CH:k * S + (ck + 1) * CH]
                r = e.matmul(ps[zb][:, 0:512], xh_, wh,
                             start=(k == 0), stop=False,
                             skip_group_check=True)
                r = e.matmul(ps[zb][:, 0:512], xl_, wh, start=False,
                             stop=False, skip_group_check=True)
                r = e.matmul(ps[zb][:, 0:512], xh_, wl, start=False,
                             stop=(k == KD - 1), skip_group_check=True)
            return r
        p.emit("pe", f_zmm, reads=[B_wr[slot], B_ufh, B_ufl],
               writes=[PB[zb]])
    dec_t, B_dec = g("dec"), g("B_dec")
    p.emit("dve", lambda e, g8=g8, zb=zb: e.tensor_tensor(
        out=zt[:, 0:512], in0=ps[zb][:, 0:512],
        in1=dec_t[:, g8 * 512:(g8 + 1) * 512], op=Alu.add),
        reads=[PB[zb], B_dec], writes=[B_zt])
    p.emit("act", lambda e: e.activation(
        out=zt[:, 0:512], in_=zt[:, 0:512], func=Act.Silu),
        reads=[B_zt], writes=[B_zt])

    for kt in range(4):
        ht = g8 * 4 + kt
        ygs = yg[:, ht * 128:(ht + 1) * 128]
        ygs3 = ygs.rearrange("p (a b) -> p a b", b=64)
        eab = (mf[:, MF_ADT + 96 + 2 * ht:MF_ADT + 96 + 2 * ht + 2]
               .unsqueeze(2).broadcast_to([128, 2, 64]))
        dab = (drep2[:, l * NH + 2 * ht:l * NH + 2 * ht + 2]
               .unsqueeze(2).broadcast_to([128, 2, 64]))
        p.emit("dve", lambda e, kt=kt, ygs3=ygs3, eab=eab: e.tensor_tensor(
            out=ygs3, in0=ps[3][:, kt * 128:(kt + 1) * 128]
            .rearrange("p (a b) -> p a b", b=64), in1=eab, op=Alu.mult),
            reads=[PB[3], B_adt[0]], writes=[B_yg])
        p.emit("dve", lambda e, kt=kt, ygs=ygs: e.tensor_tensor(
            out=ygs, in0=ygs, in1=ps[2][:, kt * 128:(kt + 1) * 128],
            op=Alu.add),
            reads=[PB[2], B_yg], writes=[B_yg])
        p.emit("dve", lambda e, ht=ht, dab=dab: e.tensor_tensor(
            out=s_ch[:, 0:128].rearrange("p (a b) -> p a b", b=64),
            in0=mf[:, MF_XTF + ht * 128:MF_XTF + (ht + 1) * 128]
            .rearrange("p (a b) -> p a b", b=64),
            in1=dab, op=Alu.mult),
            reads=[B_xtf[0], B_dr2], writes=[B_sch])
        p.emit("pool", lambda e, ygs=ygs: e.tensor_tensor(
            out=ygs, in0=ygs, in1=s_ch[:, 0:128], op=Alu.add),
            reads=[B_sch, B_yg], writes=[B_yg])
        p.emit("dve", lambda e, kt=kt, ygs=ygs: e.tensor_tensor(
            out=ygs, in0=ygs, in1=zt[:, kt * 128:(kt + 1) * 128],
            op=Alu.mult),
            reads=[B_zt, B_yg], writes=[B_yg])


def emit_outproj_chunk(E, sw, b, l, ck, hook=None):
    p = E["p"]
    g = lambda k: E[k]
    mb, mf = g("mb"), g("mf")
    ps, PB = g("ps"), g("PB")
    wring, B_wr = g("wring"), g("B_wr")
    cout_, B_cout = g("cout_"), g("B_cout")
    s_ch, B_sch = g("s_ch"), g("B_sch")
    h_t, B_ht = g("h_t"), g("B_ht")
    ident, B_id = g("ident"), g("B_id")
    yg, B_yg = g("yg"), g("B_yg")
    stat, B_stat = g("stat"), g("B_stat")
    cst, B_cst = g("cst"), g("B_cst")
    B_ygh, B_ygl = g("B_ygh"), g("B_ygl")
    B_ygfh, B_ygfl = g("B_xth"), g("B_xtl")
    MB_YGH, MB_YGL = g("MB_YGH"), g("MB_YGL")
    MB_XTH, MB_XTL = g("MB_XTH"), g("MB_XTL")

    B_xbs = g("B_xbs")
    MF_XBS = g("MF_XBS")
    p.emit("act", lambda e: e.activation(
        out=g("mf")[:, MF_XBS:MF_XBS + DIN], in_=yg[:, 0:DIN],
        func=Act.Square, accum_out=stat[:, 520:521]),
        reads=[B_yg], writes=[B_xbs[0], B_stat])
    p.emit("act", lambda e: e.activation(
        out=stat[:, 521:522], in_=stat[:, 520:521], func=Act.Sqrt,
        scale=float(1.0 / DIN), bias=cst[:, 3:4]),
        reads=[B_stat, B_cst], writes=[B_stat])
    p.emit("dve", lambda e: e.reciprocal(
        out=stat[:, 522:523], in_=stat[:, 521:522]),
        reads=[B_stat], writes=[B_stat])
    p.emit("dve", lambda e: e.tensor_scalar(
        out=yg[:, 0:DIN], in0=yg[:, 0:DIN], scalar1=stat[:, 522:523],
        scalar2=None, op0=Alu.mult),
        reads=[B_stat, B_yg], writes=[B_yg])
    ps6_ = g("ps")
    for tq in range(4):
        def f_ty(e, tq=tq):
            r = None
            for i in range(4):
                t_ = tq * 4 + i
                r = e.transpose(ps[6 + tq % 2][:, i * 128:(i + 1) * 128],
                                yg[:, t_ * 128:(t_ + 1) * 128], ident[:])
            return r
        p.emit("pe", f_ty, reads=[B_yg, B_id], writes=[PB[6 + tq % 2]])
        p.emit("dve", lambda e, tq=tq: e.tensor_copy(
            g("mf")[:, g("MF_XTF") + tq * 512:g("MF_XTF") + (tq + 1) * 512],
            ps[6 + tq % 2][:, 0:512]),
            reads=[PB[6 + tq % 2]], writes=[g("B_xtf")[0]])
    p.emit("act", lambda e: e.copy(
        out=mb[:, MB_XTH:MB_XTH + DIN],
        in_=g("mf")[:, g("MF_XTF"):g("MF_XTF") + DIN]),
        reads=[g("B_xtf")[0]], writes=[B_ygfh[0]])
    p.emit("dve", lambda e: e.scalar_tensor_tensor(
        out=mb[:, MB_XTL:MB_XTL + DIN],
        in0=g("mf")[:, g("MF_XTF"):g("MF_XTF") + DIN],
        scalar=1.0, in1=mb[:, MB_XTH:MB_XTH + DIN],
        op0=Alu.mult, op1=Alu.subtract),
        reads=[g("B_xtf")[0], B_ygfh[0]], writes=[B_ygfl[0]])

    if hook is not None:
        hook()

    for slab in range(2):
        pb = slab
        for k2 in range(KDIN // 2):
            slot = E.setdefault("wslot", 0)
            E["wslot"] = 1 - slot
            p.emit("sp", lambda e, l=l, slab=slab, k2=k2, slot=slot:
                   e.dma_start(
                       out=wring[:, slot * 2048:slot * 2048 + 2048],
                       in_=g("cout_z")[l, slab, k2]),
                   reads=[B_cout], writes=[B_wr[slot]], dma=1)

            def f_mm(e, k2=k2, slot=slot, pb=pb):
                r = None
                for kl in range(2):
                    k = 2 * k2 + kl
                    base = slot * 2048 + kl * 1024
                    wh = wring[:, base:base + 512]
                    wl = wring[:, base + 512:base + 1024]
                    yh = mb[:, MB_XTH + k * 128:MB_XTH + (k + 1) * 128]
                    yl = mb[:, MB_XTL + k * 128:MB_XTL + (k + 1) * 128]
                    r = e.matmul(ps[pb][:, 0:512], yh, wh,
                                 start=(k == 0), stop=False,
                                 skip_group_check=True)
                    r = e.matmul(ps[pb][:, 0:512], yl, wh, start=False,
                                 stop=False, skip_group_check=True)
                    r = e.matmul(ps[pb][:, 0:512], yh, wl, start=False,
                                 stop=(k == KDIN - 1),
                                 skip_group_check=True)
                return r
            p.emit("pe", f_mm, reads=[B_wr[slot], B_ygfh[0], B_ygfl[0]],
                   writes=[PB[pb]])

        def f_add(e, slab=slab, pb=pb, ck=ck):
            return e.scalar_tensor_tensor(
                out=h_t[:, ck * D + slab * 512:ck * D + (slab + 1) * 512],
                in0=ps[pb][:, 0:512], scalar=1.0,
                in1=h_t[:, ck * D + slab * 512:ck * D + (slab + 1) * 512],
                op0=Alu.mult, op1=Alu.add)
        p.emit("dve", f_add, reads=[PB[pb], B_ht], writes=[B_ht])


def build_full(n_sweeps=8, sweep=512):
    E = build_program(n_sweeps, sweep)
    emit_all(E)
    E["nc"].finalize()
    return E["nc"]


def kernel(**inputs):
    xs = np.ascontiguousarray(np.asarray(inputs["L0_attn_x"], dtype=np.float32))
    in_t = np.ascontiguousarray(
        np.transpose(np.asarray(inputs["L0_attn_in_proj_w"]), (0, 2, 1)))
    out_t = np.ascontiguousarray(
        np.transpose(np.asarray(inputs["L0_attn_out_proj_w"]), (0, 2, 1)))
    common = {
        "w_in_t": in_t,
        "w_out_t": out_t,
        "conv_w": np.ascontiguousarray(inputs["L0_attn_conv_w"]),
        "conv_b": np.ascontiguousarray(inputs["L0_attn_conv_b"]),
        "dt_bias": np.ascontiguousarray(inputs["L0_attn_dt_bias"]),
        "a_log": np.ascontiguousarray(inputs["L0_attn_A_log"]),
        "d_vec": np.ascontiguousarray(inputs["L0_attn_D"]),
        "rms_w": np.ascontiguousarray(inputs["L0_attn_rms_w"]),
        "ln_w": np.ascontiguousarray(inputs["L0_attn_ln_w"]),
        "ln_b": np.ascontiguousarray(inputs["L0_attn_ln_b"]),
        "w_fc": np.ascontiguousarray(inputs["L0_ffn_w_fc"]),
        "b_fc": np.ascontiguousarray(inputs["L0_ffn_b_fc"]),
        "w_proj": np.ascontiguousarray(inputs["L0_ffn_w_proj"]),
        "b_proj": np.ascontiguousarray(inputs["L0_ffn_b_proj"]),
    }
    in_maps = []
    for c in range(N_CORES):
        m = dict(common)
        m["x"] = np.ascontiguousarray(xs[2 * c:2 * c + 2].reshape(2 * L_FULL, D))
        in_maps.append(m)
    nc = build_full()
    res = run_bass_kernel_spmd(nc, in_maps, core_ids=list(range(N_CORES)))
    out = np.concatenate(
        [np.asarray(r["y"]).reshape(2, L_FULL, D) for r in res.results], axis=0)
    return out
